# version 10
# baseline (speedup 1.0000x reference)
import numpy as np
import ml_dtypes
import concourse.bass as bass
import concourse.mybir as mybir
from concourse.bass_utils import run_bass_kernel_spmd
from concourse.bass import IndirectOffsetOnAxis
from contextlib import ExitStack

F32 = mybir.dt.float32
BF16 = mybir.dt.bfloat16
U32 = mybir.dt.uint32
AF = mybir.ActivationFunctionType
ALU = mybir.AluOpType
AX = mybir.AxisListType
BF = ml_dtypes.bfloat16

NCORES = 8
D = 2048
T = 8192
OWN = 1024
NT = 8
KT = 16
HF = 2816
HT = 22
HD = 128
S = 4096
E = 8
CAP = 1024
QB = 32
RG = [list(range(NCORES))]
RSQ = float(1.0 / np.sqrt(np.float32(HD)))
ATT_PIPE = True


class Sem:
    def __init__(self, nc, es, name):
        self.h = es.enter_context(nc.semaphore(name))
        self.n = 0

    def bump(self, step=1):
        self.n += step
        return self.n


class Plan:
    ENGS = ("sync", "tensor", "vector", "scalar", "gpsimd")

    def __init__(self):
        self.items = {e: [] for e in self.ENGS}

    def on(self, eng):
        def deco(fn):
            self.items[eng].append(fn)
        return deco


def build_nc(dbg=False):
    nc = bass.Bass("TRN2", target_bir_lowering=False, debug=False)

    X = nc.dram_tensor("x_own", [OWN, D], F32, kind="ExternalInput").ap()
    WQKV = nc.dram_tensor("wqkv_sl", [128, KT * 768], F32, kind="ExternalInput").ap()
    WOUT = nc.dram_tensor("w_out", [4, 128, KT * 512], F32, kind="ExternalInput").ap()
    WG = nc.dram_tensor("wg", [128, KT * E], F32, kind="ExternalInput").ap()
    W1 = nc.dram_tensor("w1_e", [HT, 128, KT * 128], F32, kind="ExternalInput").ap()
    W3 = nc.dram_tensor("w3_e", [HT, 128, KT * 128], F32, kind="ExternalInput").ap()
    W2 = nc.dram_tensor("w2_e", [4, 128, HT * 512], F32, kind="ExternalInput").ap()
    SW1 = nc.dram_tensor("sw1", [HT, 128, KT * 128], F32, kind="ExternalInput").ap()
    SW3 = nc.dram_tensor("sw3", [HT, 128, KT * 128], F32, kind="ExternalInput").ap()
    SW2 = nc.dram_tensor("sw2", [4, 128, HT * 512], F32, kind="ExternalInput").ap()
    IDNB = nc.dram_tensor("idn_bf", [128, 128], BF16, kind="ExternalInput").ap()
    IDNF = nc.dram_tensor("idn_f32", [128, 128], F32, kind="ExternalInput").ap()
    TRI = nc.dram_tensor("tri", [128, 128], BF16, kind="ExternalInput").ap()
    ESEL = nc.dram_tensor("esel_rep", [128, 512], F32, kind="ExternalInput").ap()
    RSEL = nc.dram_tensor("rsel_rep", [128, 512], F32, kind="ExternalInput").ap()
    TBASE = nc.dram_tensor("tbase", [128, 1], F32, kind="ExternalInput").ap()
    SLTB = nc.dram_tensor("sltb", [64, 64], F32, kind="ExternalInput").ap()
    Y = nc.dram_tensor("y", [OWN, D], F32, kind="ExternalOutput").ap()

    DBG = {}
    if dbg:
        for nm, shp, dt in [("d_lown", [128, 64], F32), ("d_posrm", [64, 1024], F32),
                            ("d_tok", [128, 8], U32), ("d_comb", [128, 8], U32),
                            ("d_gk", [128, 8], F32), ("d_eidx", [128, 8], F32),
                            ("d_posv", [128, 8], F32), ("d_ftok", [128, 64], F32),
                            ("d_xnT", [128, KT * OWN], BF16),
                            ("d_acc", [128, NT * D], F32),
                            ("d_eexp", [128, 64], F32),
                            ("d_hT", [128, HT * OWN], BF16),
                            ("d_qT", [128, T], BF16), ("d_kT", [128, T], BF16),
                            ("d_vtok", [128, T], BF16), ("d_av", [128, S], BF16),
                            ("d_xdT", [128, KT * CAP], BF16),
                            ("d_ag2", [1024, D], BF16),
                            ("d_xd0", [128, D], BF16),
                            ("d_oec", [128, 8 * 128], BF16)]:
            DBG[nm] = nc.dram_tensor(nm, shp, dt, kind="ExternalOutput").ap()

    ag1_in = nc.dram_tensor("ag1_in", [D, OWN], BF16).ap()
    ag1_out = nc.dram_tensor("ag1_out", [NCORES * D, OWN], BF16, addr_space="Shared").ap()
    ag2_in = nc.dram_tensor("ag2_in", [OWN, D], BF16).ap()
    ag2_out = nc.dram_tensor("ag2_out", [T, D], BF16, addr_space="Shared").ap()
    ag3_in = nc.dram_tensor("ag3_in", [E, OWN], F32).ap()
    ag3_out = nc.dram_tensor("ag3_out", [NCORES * E, OWN], F32, addr_space="Shared").ap()
    ag4_in = nc.dram_tensor("ag4_in", [CAP, D], BF16).ap()
    ag4_out = nc.dram_tensor("ag4_out", [E * CAP, D], BF16, addr_space="Shared").ap()
    ag5_in = nc.dram_tensor("ag5_in", [256, T], BF16).ap()
    ag5_out = nc.dram_tensor("ag5_out", [NCORES * 256, T], BF16, addr_space="Shared").ap()
    qs1 = nc.dram_tensor("qs1", [128, T], BF16).ap()
    ks1 = nc.dram_tensor("ks1", [128, T], BF16).ap()
    vs1 = nc.dram_tensor("vs1", [128, T], BF16).ap()

    P = Plan()
    es = ExitStack()
    nc._kernel_es = es

    dS = Sem(nc, es, "dS")
    dG = Sem(nc, es, "dG")
    dA = Sem(nc, es, "dA")
    cC = Sem(nc, es, "cC")
    mP = Sem(nc, es, "mP")
    mV = Sem(nc, es, "mV")
    mA = Sem(nc, es, "mA")
    mG = Sem(nc, es, "mG")
    sPC = Sem(nc, es, "sPC")
    sW = [Sem(nc, es, f"sW{i}") for i in range(2)]
    sW2 = [Sem(nc, es, f"sW2{i}") for i in range(2)]
    sRB = [Sem(nc, es, f"sRB{i}") for i in range(3)]
    sRE = Sem(nc, es, "sRE")
    sAG1 = Sem(nc, es, "sAG1")
    sAG2 = Sem(nc, es, "sAG2")
    sAG3 = Sem(nc, es, "sAG3")
    sRM = Sem(nc, es, "sRM")
    sQKV = Sem(nc, es, "sQKV")
    sH1 = [Sem(nc, es, f"sH1{i}") for i in range(2)]
    sAV = Sem(nc, es, "sAV")
    sXD = [Sem(nc, es, f"sXD{i}") for i in range(2)]
    sW13 = [Sem(nc, es, f"sW13{i}") for i in range(2)]
    sW2b = [Sem(nc, es, f"sW2b{i}") for i in range(2)]
    sOE = [Sem(nc, es, f"sOE{i}") for i in range(2)]
    sATT = Sem(nc, es, "sATT")
    sWO = [Sem(nc, es, f"sWO{i}") for i in range(2)]
    sCB = [Sem(nc, es, f"sCB{i}") for i in range(2)]

    def sbx(stack, name, shape, dt, side=None):
        return stack.enter_context(
            nc.sbuf_tensor(name, list(shape), dt, side=side))

    acc = sbx(es, "acc", [128, NT * D], F32)
    idn_bf = sbx(es, "idn_bf_s", [128, 128], BF16)
    idn_f = sbx(es, "idn_f_s", [128, 128], F32)
    tri = sbx(es, "tri_s", [128, 128], BF16)
    esel = sbx(es, "esel_s", [128, 512], F32)
    rsel = sbx(es, "rsel_s", [128, 512], F32)
    tbase = sbx(es, "tbase_s", [128, 1], F32)
    sltb = sbx(es, "sltb_s", [64, 64], F32)
    wg_f = sbx(es, "wg_f", [128, KT * E], F32)
    wg_hi = sbx(es, "wg_hi", [128, KT * E], BF16)
    wg_lo = sbx(es, "wg_lo", [128, KT * E], BF16)
    iotaE = sbx(es, "iotaE", [128, 64], F32)
    iotaP = sbx(es, "iotaP", [128, 1], F32)
    ones_c = sbx(es, "ones_c", [128, 1], BF16)
    l_own = sbx(es, "l_own", [128, 64], F32)
    lmax = sbx(es, "lmax_s", [128, 8], F32)
    eexp = sbx(es, "eexp", [128, 64], F32)
    sume = sbx(es, "sume", [128, 8], F32)
    gate = sbx(es, "gate_s", [128, 8], F32)
    oh = sbx(es, "oh_s", [128, 64], F32)
    pr1 = sbx(es, "pr1", [128, 64], F32)
    pr2 = sbx(es, "pr2", [128, 64], F32)
    ohx = sbx(es, "ohx_s", [128, 64], F32)
    eidx = sbx(es, "eidx_s", [128, 8], F32)
    ohxT = sbx(es, "ohxT", [8, OWN], F32)
    ohx_rm = sbx(es, "ohx_rm", [64, 1024], F32)
    ones64 = sbx(es, "ones64", [64, 1024], F32)
    pos_rm = sbx(es, "pos_rm", [64, 1024], F32)
    tot64 = sbx(es, "tot64", [64, 1], F32)
    off64 = sbx(es, "off64", [64, 1], F32)
    pos_tok = sbx(es, "pos_tok", [128, 512], F32)
    mscr = sbx(es, "mscr", [128, 512], F32)
    f_tok = sbx(es, "f_tok", [128, 64], F32)
    pos_own = sbx(es, "pos_own", [128, 64], F32)
    posv = sbx(es, "posv_s", [128, 8], F32)
    keep = sbx(es, "keep_s", [128, 8], F32)
    gk = sbx(es, "gk_s", [128, 8], F32)
    comb_f = sbx(es, "comb_f", [128, 8], F32)
    comb_u = sbx(es, "comb_u", [128, 8], U32)
    cnt_sb = sbx(es, "cnt_sb", [1, 1024], F32)
    tok_f = sbx(es, "tok_f", [128, 8], F32)
    tok_u = sbx(es, "tok_u", [128, 8], U32)
    att_if = sbx(es, "att_if", [128, 16], F32)
    att_ix = sbx(es, "att_ix", [128, 16], U32)
    nm_run = sbx(es, "nm_run", [128, 1], F32)
    nm_tmp = sbx(es, "nm_tmp", [128, 1], F32)
    nmc = sbx(es, "nmc", [128, 8], F32)
    rs_s = sbx(es, "rs_s", [128, 8], F32)
    rsum = sbx(es, "rsum", [128, 1], F32)
    rrec = sbx(es, "rrec", [128, 1], F32)
    negmean = sbx(es, "negmean", [128, 1], F32)
    sumsq = sbx(es, "sumsq", [128, 1], F32)
    varb = sbx(es, "varb", [128, 1], F32)
    lnv = sbx(es, "lnv", [128, 1], F32)
    rstd = sbx(es, "rstd", [128, 1], F32)
    m2s = sbx(es, "m2s", [128, 1], F32)
    eps_c = sbx(es, "eps_c", [128, 1], F32)
    zero_c = sbx(es, "zero_c", [128, 1], F32)

    pf = [es.enter_context(nc.psum_tensor(f"pf{i}", [128, 512], F32))
          for i in range(6)]
    pb = [es.enter_context(nc.psum_tensor(f"pb{i}", [128, 1024], BF16))
          for i in range(2)]

    @P.on("sync")
    def _(e):
        e.dma_start(out=acc[:].rearrange("p (i d) -> p i d", d=D),
                    in_=X.rearrange("(i p) d -> p i d", p=128)).then_inc(dS.h, 16)
    ld_x = dS.bump(16)

    @P.on("scalar")
    def _(e):
        for dst, src in ((idn_bf, IDNB), (idn_f, IDNF), (tri, TRI), (esel, ESEL),
                         (rsel, RSEL), (tbase, TBASE), (sltb, SLTB)):
            e.dma_start(out=dst[:], in_=src).then_inc(dA.h, 16)
        e.dma_start(out=wg_f[:], in_=WG).then_inc(dA.h, 16)
    ld_const = dA.bump(16 * 8)

    @P.on("gpsimd")
    def _(e):
        e.iota(iotaE[:], [[0, 8], [1, 8]], base=0, channel_multiplier=0,
               allow_small_or_imprecise_dtypes=True)
        e.iota(iotaP[:], [[0, 1]], base=0, channel_multiplier=8192,
               allow_small_or_imprecise_dtypes=True).then_inc(mG.h, 1)
    g_iota = mG.bump()

    @P.on("vector")
    def _(e, w=ld_const):
        e.wait_ge(dA.h, w)
        e.memset(ones_c[:], 1.0)
        e.memset(ones64[:], 1.0)
        e.memset(eps_c[:], 1e-5)
        e.memset(zero_c[:], 0.0)
        e.tensor_copy(wg_hi[:], wg_f[:])
        e.drain()
        e.tensor_tensor(wg_lo[:], wg_f[:], wg_hi[:], ALU.subtract)
        e.drain().then_inc(mV.h, 1)
    v_wg = mV.bump()

    esA1 = ExitStack()
    nc._esA1 = esA1
    xnT = sbx(esA1, "xnT", [128, KT * OWN], BF16)

    esA2 = ExitStack()
    nc._esA2 = esA2
    xnf = sbx(esA2, "xnf", [128, D], F32)
    xnhi = [sbx(esA2, f"xnhi{i}", [128, D], BF16) for i in range(2)]
    xnlo = [sbx(esA2, f"xnlo{i}", [128, D], BF16) for i in range(2)]
    loT = [sbx(esA2, f"loT{i}", [128, D], BF16) for i in range(2)]

    v_tile = [0] * NT
    pT_hi = [0] * NT
    vE_hi = [0] * NT
    pT_lo = [0] * NT
    vE_lo = [0] * NT
    p_logit = [0] * NT
    ag2d = [0] * NT

    for t in range(NT):
        xt = acc[:, t * D:(t + 1) * D]

        @P.on("vector")
        def _(e, t=t, xt=xt):
            if t == 0:
                e.wait_ge(dS.h, ld_x)
            e.reduce_sum(negmean[:], xt, axis=AX.X, negate=True)
            e.drain()
            e.tensor_scalar(negmean[:], negmean[:], 1.0 / D, None, ALU.mult)
            e.scalar_tensor_tensor(xnf[:], xt, 1.0, xt, ALU.mult, ALU.mult,
                                   accum_out=sumsq[:])
            e.drain()
            e.tensor_tensor(m2s[:], negmean[:], negmean[:], ALU.mult)
            e.tensor_scalar(varb[:], sumsq[:], 1.0 / D, None, ALU.mult)
            e.drain()
            e.tensor_tensor(varb[:], varb[:], m2s[:], ALU.subtract)
            e.drain().then_inc(mV.h, 1)
        v_stats = mV.bump()

        @P.on("scalar")
        def _(e, w=v_stats):
            e.wait_ge(mV.h, w)
            e.activation(lnv[:], varb[:], AF.Ln, bias=eps_c[:], scale=1.0)
            e.drain()
            e.activation(rstd[:], lnv[:], AF.Exp, bias=0.0,
                         scale=-0.5)
            e.drain().then_inc(mA.h, 1)
        a_rstd = mA.bump()

        @P.on("vector")
        def _(e, t=t, xt=xt, wa=a_rstd,
              wp=(pT_lo[t - 2] if t >= 2 else 0),
              wd=(ag2d[t - 2] if t >= 2 else 0)):
            e.wait_ge(mA.h, wa)
            if wp:
                e.wait_ge(mP.h, wp)
            if wd:
                e.wait_ge(sAG2.h, wd)
            e.tensor_scalar(xnf[:], xt, negmean[:], rstd[:], ALU.add, ALU.mult)
            e.tensor_copy(xnhi[t % 2][:], xnf[:])
            e.tensor_tensor(xnlo[t % 2][:], xnf[:], xnhi[t % 2][:],
                            ALU.subtract)
            e.drain().then_inc(mV.h, 1)
        v_tile[t] = mV.bump()

        @P.on("gpsimd")
        def _(e, t=t, w=v_tile[t]):
            e.wait_ge(mV.h, w)
            e.dma_start(out=ag2_in[t * 128:(t + 1) * 128, :],
                        in_=xnhi[t % 2][:]).then_inc(sAG2.h, 16)
        ag2d[t] = sAG2.bump(16)

        @P.on("tensor")
        def _(e, t=t, w=v_tile[t], wev=(vE_lo[t - 1] if t >= 1 else 0)):
            e.wait_ge(mV.h, w)
            if t == 0:
                e.wait_ge(dA.h, ld_const)
            if wev:
                e.wait_ge(mV.h, wev)
            ins = None
            for g in range(2):
                for j in range(8):
                    k = g * 8 + j
                    ins = e.transpose(pb[g][:, j * 128:(j + 1) * 128],
                                     xnhi[t % 2][:, k * 128:(k + 1) * 128],
                                     idn_bf[:])
            ins.then_inc(mP.h, 1)
        pT_hi[t] = mP.bump()

        @P.on("vector")
        def _(e, t=t, w=pT_hi[t]):
            e.wait_ge(mP.h, w)
            xv = xnT[:].rearrange("p (k t2) -> p k t2", t2=OWN)
            e.tensor_copy(xv[:, 0:8, t * 128:(t + 1) * 128],
                          pb[0][:].rearrange("p (j t2) -> p j t2", t2=128))
            e.tensor_copy(xv[:, 8:16, t * 128:(t + 1) * 128],
                          pb[1][:].rearrange("p (j t2) -> p j t2", t2=128))
            e.drain().then_inc(mV.h, 1)
        vE_hi[t] = mV.bump()

        @P.on("tensor")
        def _(e, t=t, w=vE_hi[t]):
            e.wait_ge(mV.h, w)
            ins = None
            for g in range(2):
                for j in range(8):
                    k = g * 8 + j
                    ins = e.transpose(pb[g][:, j * 128:(j + 1) * 128],
                                     xnlo[t % 2][:, k * 128:(k + 1) * 128],
                                     idn_bf[:])
            ins.then_inc(mP.h, 1)
        pT_lo[t] = mP.bump()

        @P.on("vector")
        def _(e, t=t, w=pT_lo[t]):
            e.wait_ge(mP.h, w)
            lv = loT[t % 2][:].rearrange("p (k t2) -> p k t2", t2=128)
            e.tensor_copy(lv[:, 0:8, :],
                          pb[0][:].rearrange("p (j t2) -> p j t2", t2=128))
            e.tensor_copy(lv[:, 8:16, :],
                          pb[1][:].rearrange("p (j t2) -> p j t2", t2=128))
            e.drain().then_inc(mV.h, 1)
        vE_lo[t] = mV.bump()

        @P.on("tensor")
        def _(e, t=t, w=vE_lo[t], wwg=v_wg):
            e.wait_ge(mV.h, w)
            if t == 0:
                e.wait_ge(mV.h, wwg)
            out = pf[0][:, t * 8:(t + 1) * 8]
            n = 0
            ins = None
            for k in range(KT):
                for lhs, rhs in (
                        (xnT[:, k * OWN + t * 128: k * OWN + (t + 1) * 128],
                         wg_hi[:, k * 8:(k + 1) * 8]),
                        (xnT[:, k * OWN + t * 128: k * OWN + (t + 1) * 128],
                         wg_lo[:, k * 8:(k + 1) * 8]),
                        (loT[t % 2][:, k * 128:(k + 1) * 128],
                         wg_hi[:, k * 8:(k + 1) * 8])):
                    ins = e.matmul(out, lhs, rhs, start=(n == 0), stop=(n == 47))
                    n += 1
            ins.then_inc(mP.h, 1)
        p_logit[t] = mP.bump()

    if dbg:
        @P.on("gpsimd")
        def _(e, w=v_tile[NT - 1], w2=vE_hi[NT - 1], tgt=dG.n + 32):
            e.wait_ge(mV.h, w)
            e.wait_ge(mV.h, w2)
            e.dma_start(out=DBG["d_acc"], in_=acc[:]).then_inc(dG.h, 16)
            e.dma_start(out=DBG["d_xnT"], in_=xnT[:]).then_inc(dG.h, 16)
            e.wait_ge(dG.h, tgt)
        dG.bump(32)

    @P.on("gpsimd")
    def _(e, w=vE_hi[NT - 1], wd=ag2d[NT - 1], tgt=sAG1.n + 16):
        e.wait_ge(mV.h, w)
        e.dma_start(out=ag1_in.rearrange("(k p) t2 -> p k t2", p=128),
                    in_=xnT[:].rearrange("p (k t2) -> p k t2",
                                         t2=OWN)).then_inc(sAG1.h, 16)
        e.wait_ge(sAG1.h, tgt)
        e.wait_ge(sAG2.h, wd)
        e.collective_compute("AllGather", ALU.bypass, RG, [ag1_in],
                             [ag1_out]).then_inc(cC.h)
        e.collective_compute("AllGather", ALU.bypass, RG, [ag2_in],
                             [ag2_out]).then_inc(cC.h)
    sAG1.bump(16)
    cc_ag1 = cC.bump()
    cc_ag2 = cC.bump()

    @P.on("vector")
    def _(e, w=p_logit[NT - 1], wi=g_iota):
        e.wait_ge(mP.h, w)
        e.wait_ge(mG.h, wi)
        e.tensor_copy(l_own[:], pf[0][:, 0:64])
        e.drain()
        ov = l_own[:].rearrange("p (t e2) -> p t e2", e2=8)
        e.reduce_max(lmax[:], ov, axis=AX.X)
        e.drain()
        for t in range(NT):
            e.tensor_scalar(eexp[:, t * 8:(t + 1) * 8], l_own[:, t * 8:(t + 1) * 8],
                            lmax[:, t:t + 1], None, ALU.subtract)
        for t in range(NT):
            e.tensor_scalar(oh[:, t * 8:(t + 1) * 8], l_own[:, t * 8:(t + 1) * 8],
                            lmax[:, t:t + 1], None, ALU.is_equal)
        ohv = oh[:].rearrange("p (t e2) -> p t e2", e2=8)
        p1v = pr1[:].rearrange("p (t e2) -> p t e2", e2=8)
        p2v = pr2[:].rearrange("p (t e2) -> p t e2", e2=8)
        e.drain()
        e.tensor_copy(pr1[:], oh[:])
        e.drain()
        e.tensor_add(p1v[:, :, 1:8], ohv[:, :, 1:8], ohv[:, :, 0:7])
        e.drain()
        e.tensor_copy(pr2[:], pr1[:])
        e.drain()
        e.tensor_add(p2v[:, :, 2:8], p1v[:, :, 2:8], p1v[:, :, 0:6])
        e.drain()
        e.tensor_copy(pr1[:], pr2[:])
        e.drain()
        e.tensor_add(p1v[:, :, 4:8], p2v[:, :, 4:8], p2v[:, :, 0:4])
        e.drain()
        e.tensor_scalar(ohx[:], pr1[:], 1.0, None, ALU.is_equal)
        e.drain()
        e.tensor_mul(ohx[:], ohx[:], oh[:])
        e.drain()
        e.tensor_mul(mscr[:, 0:64], ohx[:], iotaE[:])
        e.drain()
        e.reduce_sum(eidx[:], mscr[:, 0:64].rearrange("p (t e2) -> p t e2", e2=8),
                     axis=AX.X)
        for k in range(KT):
            e.tensor_scalar(att_if[:, k:k + 1], iotaP[:], tbase[:],
                            float(k * 128 * 8192), ALU.add, ALU.add)
        e.drain()
        e.tensor_copy(att_ix[:], att_if[:])
        e.drain().then_inc(mV.h, 1)
    v_rout1 = mV.bump()

    @P.on("scalar")
    def _(e, w=v_rout1):
        e.wait_ge(mV.h, w)
        ins = None
        for t in range(NT):
            ins = e.activation(eexp[:, t * 8:(t + 1) * 8],
                               eexp[:, t * 8:(t + 1) * 8], AF.Exp, bias=0.0,
                               scale=1.0, accum_out=sume[:, t:t + 1])
        ins.then_inc(mA.h, 1)
    a_rexp = mA.bump()

    @P.on("vector")
    def _(e, w=a_rexp):
        e.wait_ge(mA.h, w)
        e.reciprocal(gate[:], sume[:])
        e.drain().then_inc(mV.h, 1)
    v_gate = mV.bump()

    @P.on("tensor")
    def _(e, w=v_rout1):
        e.wait_ge(mV.h, w)
        ins = None
        for t in range(NT):
            bank, j = (pf[1], t) if t < 4 else (pf[2], t - 4)
            ins = e.transpose(bank[0:8, j * 128:(j + 1) * 128],
                              ohx[:, t * 8:(t + 1) * 8], idn_f[:])
        ins.then_inc(mP.h, 1)
    p_ohxT = mP.bump()

    @P.on("vector")
    def _(e, w=p_ohxT):
        e.wait_ge(mP.h, w)
        e.tensor_copy(ohxT[0:8, 0:512], pf[1][0:8, 0:512])
        e.tensor_copy(ohxT[0:8, 512:1024], pf[2][0:8, 0:512])
        e.drain().then_inc(mV.h, 1)
    v_ohxT = mV.bump()

    @P.on("gpsimd")
    def _(e, w=v_ohxT, tgt=sAG3.n + 16):
        e.wait_ge(mV.h, w)
        e.dma_start(out=ag3_in, in_=ohxT[:]).then_inc(sAG3.h, 16)
        e.wait_ge(sAG3.h, tgt)
        e.collective_compute("AllGather", ALU.bypass, RG, [ag3_in],
                             [ag3_out]).then_inc(cC.h)
    sAG3.bump(16)
    cc_ag3 = cC.bump()

    @P.on("sync")
    def _(e, w=cc_ag3):
        e.wait_ge(cC.h, w)
        e.dma_start(out=ohx_rm[:], in_=ag3_out).then_inc(sRM.h, 16)
    g_ohxrm = sRM.bump(16)

    esA2.close()
    esA3 = ExitStack()
    nc._esA3 = esA3
    h_T = sbx(esA3, "h_T", [128, HT * OWN], BF16, side="right")
    iota_j = sbx(esA3, "iota_j", [128, 1024], F32, side="right")
    cmpb = [sbx(esA3, f"cmpb{i}", [128, 1024], BF16, side="right")
            for i in range(2)]
    esA3a = ExitStack()
    nc._esA3a = esA3a
    wsl = [sbx(esA3a, f"wsl{i}", [128, D], BF16) for i in range(4)]
    hsil = [sbx(esA3a, f"hsil{i}", [128, 512], BF16) for i in range(2)]

    @P.on("gpsimd")
    def _(e):
        e.iota(iota_j[:], [[1, 1024]], base=0, channel_multiplier=0,
               allow_small_or_imprecise_dtypes=True).then_inc(mG.h, 1)
    g_iotaj = mG.bump()

    cmp_done = [0] * 64
    cnt_done = [0] * 64

    def emit_cmp(gt):
        @P.on("vector")
        def _(e, gt=gt, wj=g_iotaj):
            if gt == 0:
                e.wait_ge(mG.h, wj)
            if gt >= 2:
                e.wait_ge(mP.h, cnt_done[gt - 2])
            e.tensor_scalar(cmpb[gt % 2][:], iota_j[:], f_tok[:, gt:gt + 1],
                            None, ALU.is_ge)
            e.drain().then_inc(mV.h, 1)
        cmp_done[gt] = mV.bump()

    def emit_cnt(gt, extra_wait=0):
        @P.on("tensor")
        def _(e, gt=gt, wx=extra_wait):
            e.wait_ge(mV.h, cmp_done[gt])
            if wx:
                e.wait_ge(mV.h, wx)
            e.matmul(pf[0][0:1, 0:512], ones_c[:], cmpb[gt % 2][:, 0:512],
                     start=(gt == 0), stop=(gt == 63))
            e.matmul(pf[1][0:1, 0:512], ones_c[:], cmpb[gt % 2][:, 512:1024],
                     start=(gt == 0), stop=(gt == 63)).then_inc(mP.h, 1)
        cnt_done[gt] = mP.bump()

    p_hn = {}
    v_hev = {}
    sw_ld = {}
    for mH in range(HT):
        @P.on("gpsimd")
        def _(e, mH=mH, wpl=p_logit[NT - 1], wag=ag2d[NT - 1]):
            if mH == 0:
                e.wait_ge(mP.h, wpl)
                e.wait_ge(sAG2.h, wag)
            if mH >= 2:
                e.wait_ge(mP.h, p_hn[(mH - 2, 1)])
            e.dma_start(out=wsl[mH % 2][:],
                        in_=SW1[mH]).then_inc(sW[mH % 2].h, 16)
            e.dma_start(out=wsl[2 + mH % 2][:],
                        in_=SW3[mH]).then_inc(sW[mH % 2].h, 16)
        sw_ld[mH] = sW[mH % 2].bump(32)

        for n in range(2):
            idx = 2 * mH + n

            @P.on("tensor")
            def _(e, mH=mH, n=n, idx=idx, w=sw_ld[mH], wxn=vE_lo[NT - 1]):
                if n == 0:
                    e.wait_ge(sW[mH % 2].h, w)
                if idx == 0:
                    e.wait_ge(mV.h, wxn)
                par = idx % 2
                if idx >= 2:
                    e.wait_ge(mV.h, v_hev[idx - 2])
                b1, b3 = (pf[0], pf[1]) if par == 0 else (pf[4], pf[5])
                for k in range(KT):
                    e.matmul(b1[:], wsl[mH % 2][:, k * 128:(k + 1) * 128],
                             xnT[:, k * OWN + n * 512: k * OWN + n * 512 + 512],
                             start=(k == 0), stop=(k == KT - 1))
                ins = None
                for k in range(KT):
                    ins = e.matmul(b3[:], wsl[2 + mH % 2][:, k * 128:(k + 1) * 128],
                                   xnT[:, k * OWN + n * 512: k * OWN + n * 512 + 512],
                                   start=(k == 0), stop=(k == KT - 1))
                ins.then_inc(mP.h, 1)
            p_hn[(mH, n)] = mP.bump()

            @P.on("scalar")
            def _(e, idx=idx, w=p_hn[(mH, n)]):
                e.wait_ge(mP.h, w)
                e.activation(hsil[idx % 2][:], pf[0 if idx % 2 == 0 else 4][:],
                             AF.Silu, bias=0.0, scale=1.0).then_inc(mA.h, 1)
            a_sil = mA.bump()

            @P.on("vector")
            def _(e, mH=mH, n=n, idx=idx, wa=a_sil):
                e.wait_ge(mA.h, wa)
                e.tensor_tensor(h_T[:, mH * OWN + n * 512: mH * OWN + n * 512 + 512],
                                hsil[idx % 2][:], pf[1 if idx % 2 == 0 else 5][:],
                                ALU.mult)
                e.drain().then_inc(mV.h, 1)
            v_hev[idx] = mV.bump()
    v_hev_last = v_hev[2 * HT - 1]

    @P.on("vector")
    def _(e, w=g_ohxrm):
        e.wait_ge(sRM.h, w)
        e.tensor_tensor_scan(pos_rm[:], ones64[:], ohx_rm[:], 0.0,
                             ALU.mult, ALU.add)
        e.drain()
        e.tensor_copy(tot64[:], pos_rm[:, 1023:1024])
        e.drain().then_inc(mV.h, 1)
    v_scan = mV.bump()

    @P.on("tensor")
    def _(e, w=v_scan):
        e.wait_ge(mV.h, w)
        e.matmul(pf[3][0:64, 0:1], sltb[:], tot64[:], start=True,
                 stop=True).then_inc(mP.h, 1)
    p_off = mP.bump()

    @P.on("vector")
    def _(e, w=p_off):
        e.wait_ge(mP.h, w)
        e.tensor_copy(off64[:], pf[3][0:64, 0:1])
        e.drain()
        e.tensor_scalar(pos_rm[:], pos_rm[:], off64[:], None, ALU.add)
        e.drain().then_inc(mV.h, 1)
    v_posrm = mV.bump()

    @P.on("tensor")
    def _(e, w=v_posrm):
        e.wait_ge(mV.h, w)
        ins = None
        for tl in range(8):
            ins = e.transpose(pf[3][:, tl * 64:(tl + 1) * 64],
                              pos_rm[0:64, tl * 128:(tl + 1) * 128],
                              idn_f[0:64, 0:64])
        ins.then_inc(mP.h, 1)
    p_postok = mP.bump()

    @P.on("vector")
    def _(e, w=p_postok, wg=v_gate):
        e.wait_ge(mP.h, w)
        e.wait_ge(mV.h, wg)
        e.tensor_copy(pos_tok[:], pf[3][:])
        e.drain()
        e.tensor_mul(mscr[:], pos_tok[:], esel[:])
        e.drain()
        e.reduce_sum(f_tok[:],
                     mscr[:].rearrange("p (tl r e2) -> p tl r e2", r=8, e2=8),
                     axis=AX.X)
        e.drain()
        e.tensor_mul(mscr[:], pos_tok[:], rsel[:])
        e.drain()
        e.reduce_sum(pos_own[:],
                     mscr[:].rearrange("p (tl r e2) -> p tl e2 r", r=8, e2=8),
                     axis=AX.X)
        e.drain()
        e.tensor_mul(mscr[:, 0:64], pos_own[:], ohx[:])
        e.drain()
        e.reduce_sum(posv[:], mscr[:, 0:64].rearrange("p (t e2) -> p t e2", e2=8),
                     axis=AX.X)
        e.drain()
        e.tensor_scalar(posv[:], posv[:], -1.0, None, ALU.add)
        e.drain()
        e.tensor_scalar(keep[:], posv[:], float(CAP), None, ALU.is_lt)
        e.drain()
        e.tensor_mul(gk[:], gate[:], keep[:])
        e.scalar_tensor_tensor(comb_f[:], eidx[:], float(CAP), posv[:],
                               ALU.mult, ALU.add)
        e.drain()
        e.tensor_scalar(comb_f[:], comb_f[:], -1.0e6, None, ALU.add)
        e.drain()
        e.tensor_mul(comb_f[:], comb_f[:], keep[:])
        e.drain()
        e.tensor_scalar(comb_f[:], comb_f[:], 1.0e6, None, ALU.add)
        e.drain()
        e.tensor_copy(comb_u[:], comb_f[:])
        e.drain().then_inc(mV.h, 1)
    v_route = mV.bump()

    esA3a.close()
    esA1.close()
    esA3b = ExitStack()
    nc._esA3b = esA3b
    w2sl = [sbx(esA3b, f"w2sl{i}", [128, HT * 512], BF16) for i in range(2)]

    p_w2last = [0] * 4
    v_w2ev = {}
    s2_ld = {}
    gtq = [0]
    for n in range(4):
        @P.on("gpsimd")
        def _(e, n=n):
            if n == 0:
                e.wait_ge(mP.h, p_hn[(HT - 1, 1)])
                e.wait_ge(mV.h, v_hev_last)
            if n >= 2:
                e.wait_ge(mP.h, p_w2last[n - 2])
            e.dma_start(out=w2sl[n % 2][:],
                        in_=SW2[n]).then_inc(sW2[n % 2].h, 16)
        s2_ld[n] = sW2[n % 2].bump(16)

        for j in range(NT):
            gi = n * NT + j

            @P.on("tensor")
            def _(e, n=n, j=j, gi=gi, w=s2_ld[n], whev=v_hev_last):
                if gi == 0:
                    e.wait_ge(mV.h, whev)
                    e.wait_ge(mV.h, v_route)
                if j == 0:
                    e.wait_ge(sW2[n % 2].h, w)
                par = gi % 2
                if gi >= 2:
                    e.wait_ge(mV.h, v_w2ev[gi - 2])
                bank = pf[2 + par]
                ins = None
                for h in range(HT):
                    ins = e.matmul(bank[:],
                                   h_T[:, h * OWN + j * 128: h * OWN + (j + 1) * 128],
                                   w2sl[n % 2][:, h * 512:(h + 1) * 512],
                                   start=(h == 0), stop=(h == HT - 1))
                ins.then_inc(mP.h, 1)
            p_w2 = mP.bump()
            if j == NT - 1:
                p_w2last[n] = p_w2

            @P.on("vector")
            def _(e, n=n, j=j, gi=gi, w=p_w2):
                e.wait_ge(mP.h, w)
                sl = acc[:, j * D + n * 512: j * D + (n + 1) * 512]
                e.scalar_tensor_tensor(sl, pf[2 + gi % 2][:], 1.0, sl,
                                       ALU.mult, ALU.add)
                e.drain().then_inc(mV.h, 1)
            v_w2ev[gi] = mV.bump()

            for _k in range(2):
                if gtq[0] < 64:
                    emit_cmp(gtq[0])
                    emit_cnt(gtq[0], extra_wait=(v_hev_last if gtq[0] == 0 else 0))
                    gtq[0] += 1
    v_w2ev_last = v_w2ev[4 * NT - 1]

    @P.on("vector")
    def _(e, w=cnt_done[63]):
        e.wait_ge(mP.h, w)
        e.tensor_copy(cnt_sb[0:1, 0:512], pf[0][0:1, 0:512])
        e.tensor_copy(cnt_sb[0:1, 512:1024], pf[1][0:1, 0:512])
        e.drain().then_inc(mV.h, 1)
    v_cnt = mV.bump()

    @P.on("tensor")
    def _(e, w=v_cnt):
        e.wait_ge(mV.h, w)
        ins = None
        for j in range(8):
            ins = e.transpose(pf[2][:, j:j + 1],
                              cnt_sb[0:1, j * 128:(j + 1) * 128],
                              idn_f[0:1, 0:1])
        ins.then_inc(mP.h, 1)
    p_tokT = mP.bump()

    @P.on("vector")
    def _(e, w=p_tokT):
        e.wait_ge(mP.h, w)
        e.tensor_copy(tok_f[:], pf[2][:, 0:8])
        e.drain()
        e.tensor_copy(tok_u[:], tok_f[:])
        e.drain().then_inc(mV.h, 1)
    v_tok = mV.bump()

    esA3b.close()
    esA3.close()

    esB1 = ExitStack()
    nc._esB1 = esB1
    qT = sbx(esB1, "qT", [128, T], BF16)
    kT = sbx(esB1, "kT", [128, T], BF16)
    vT = sbx(esB1, "vT", [128, T], BF16)

    esB2 = ExitStack()
    nc._esB2 = esB2
    wqkv_bf = sbx(esB2, "wqkv_bf", [128, KT * 768], BF16)
    rb = [sbx(esB2, f"rb{i}", [128, 8 * 512], BF16) for i in range(3)]
    h1stg = [sbx(esB2, f"h1stg{i}", [128, 3 * 512], BF16) for i in range(2)]

    @P.on("gpsimd")
    def _(e, w=v_tok):
        e.wait_ge(mV.h, w)
        e.dma_start(out=wqkv_bf[:], in_=WQKV).then_inc(sQKV.h, 16)
    g_wqkv = sQKV.bump(16)

    p_qkv = {}
    v_qev = {}
    h1_st = {}
    rb_ld = {}
    for ch in range(16):
        r, hh = ch // 2, ch % 2
        for half in range(2):
            idx = 2 * ch + half

            @P.on("sync")
            def _(e, ch=ch, half=half, r=r, hh=hh, idx=idx, wv=v_tok):
                if idx == 0:
                    e.wait_ge(cC.h, cc_ag1)
                    e.wait_ge(mV.h, wv)
                if idx >= 3:
                    e.wait_ge(mP.h, p_qkv[(idx - 3) // 2])
                src = ag1_out[r * D + half * 1024: r * D + half * 1024 + 1024,
                              hh * 512: hh * 512 + 512]
                e.dma_start(out=rb[idx % 3][:].rearrange("p (k t2) -> p k t2",
                                                         t2=512),
                            in_=src.rearrange("(k p) t2 -> p k t2",
                                              p=128)).then_inc(sRB[idx % 3].h, 16)
            rb_ld[idx] = sRB[idx % 3].bump(16)

        @P.on("tensor")
        def _(e, ch=ch, wb=rb_ld[2 * ch + 1], wq=g_wqkv, wtk=v_tok,
              slot=(2 * ch + 1) % 3):
            if ch == 0:
                e.wait_ge(sQKV.h, wq)
                e.wait_ge(mV.h, wtk)
            e.wait_ge(sRB[(2 * ch) % 3].h, rb_ld[2 * ch])
            e.wait_ge(sRB[slot].h, rb_ld[2 * ch + 1])
            if ch >= 1:
                e.wait_ge(mV.h, v_qev[ch - 1])
            ins = None
            for m in range(6):
                for k in range(KT):
                    ins = e.matmul(
                        pf[m][:],
                        wqkv_bf[:, k * 768 + m * 128: k * 768 + (m + 1) * 128],
                        rb[(2 * ch + k // 8) % 3][:, (k % 8) * 512:(k % 8 + 1) * 512],
                        start=(k == 0), stop=(k == KT - 1))
            ins.then_inc(mP.h, 1)
        p_qkv[ch] = mP.bump()

        @P.on("vector")
        def _(e, ch=ch, w=p_qkv[ch]):
            e.wait_ge(mP.h, w)
            if ch >= 2:
                e.wait_ge(sH1[ch % 2].h, h1_st[ch - 2])
            cs = slice(ch * 512, (ch + 1) * 512)
            e.tensor_scalar(qT[:, cs], pf[0][:], RSQ, None, ALU.mult)
            e.tensor_copy(kT[:, cs], pf[2][:])
            e.tensor_copy(vT[:, cs], pf[4][:])
            e.tensor_scalar(h1stg[ch % 2][:, 0:512], pf[1][:], RSQ, None, ALU.mult)
            e.tensor_copy(h1stg[ch % 2][:, 512:1024], pf[3][:])
            e.tensor_copy(h1stg[ch % 2][:, 1024:1536], pf[5][:])
            e.drain().then_inc(mV.h, 1)
        v_qev[ch] = mV.bump()

        @P.on("gpsimd")
        def _(e, ch=ch, w=v_qev[ch]):
            e.wait_ge(mV.h, w)
            cs = slice(ch * 512, (ch + 1) * 512)
            e.dma_start(out=qs1[:, cs],
                        in_=h1stg[ch % 2][:, 0:512]).then_inc(sH1[ch % 2].h, 16)
            e.dma_start(out=ks1[:, cs],
                        in_=h1stg[ch % 2][:, 512:1024]).then_inc(sH1[ch % 2].h, 16)
            e.dma_start(out=vs1[:, cs],
                        in_=h1stg[ch % 2][:, 1024:1536]).then_inc(sH1[ch % 2].h, 16)
        h1_st[ch] = sH1[ch % 2].bump(48)

    esB2.close()

    esB3 = ExitStack()
    nc._esB3 = esB3
    aeb = [sbx(esB3, f"aeb{i}", [128, S], BF16, side="right") for i in range(2)]
    asb = sbx(esB3, "asb", [128, S], BF16, side="right")
    avb = sbx(esB3, "avb", [128, S], BF16, side="right")
    atnT = [sbx(esB3, f"atnT{i}", [128, 4096], BF16, side="right")
            for i in range(2)]
    rs2 = [sbx(esB3, f"rs2{i}", [128, 8], F32) for i in range(2)]

    sc_rr = [0]
    sc_free = [(0, 0), (0, 0), (0, 0)]
    pbt_free = [(None, 0), (None, 0)]
    av_free = [0, 0]
    atn_rd = [0, 0]
    asb_rd = [0]
    g_av = {}
    h1_re = [0, 0, 0]

    def emit_attention(hid, wstart_dS, wstart_mV):
        for g in range(8):
            @P.on("tensor")
            def _(e, g=g, hid=hid, wfree=pbt_free[g % 2],
                  wsd=wstart_dS, wsv=wstart_mV,
                  wh14=h1_st.get(14, 0), wh15=h1_st.get(15, 0)):
                if g == 0:
                    if wsd:
                        e.wait_ge(sRE.h, wsd)
                    if wsv:
                        e.wait_ge(mV.h, wsv)
                    if wh14:
                        e.wait_ge(sH1[0].h, wh14)
                        e.wait_ge(sH1[1].h, wh15)
                if wfree[1]:
                    e.wait_ge(wfree[0], wfree[1])
                ins = None
                for j in range(8):
                    jj = g * 8 + j
                    ins = e.transpose(pb[g % 2][:, j * 128:(j + 1) * 128],
                                      vT[:, jj * 128:(jj + 1) * 128], idn_bf[:])
                ins.then_inc(mP.h, 1)
            p_vt = mP.bump()

            @P.on("scalar")
            def _(e, g=g, w=p_vt):
                e.wait_ge(mP.h, w)
                e.copy(vT[:, g * 1024:(g + 1) * 1024], pb[g % 2][:])
                e.drain().then_inc(mA.h, 1)
            pbt_free[g % 2] = (mA.h, mA.bump())
        v_vtok = mV.n

        for b in range(2):
            tok0 = b * S
            blkid = 2 * hid + b
            st_aexp = [0, 0]
            st_vscl = [0, 0]
            st_pavg = [0, 0]
            st_nch = [0, 0]
            st_L = [0, 0]

            def emit_scores(i):
                L = 128 * (i + 1)
                nch = (L + 511) // 512
                st_nch[i % 2] = nch
                st_L[i % 2] = L
                for c in range(nch):
                    Lc = min(512, L - 512 * c)
                    bk = sc_rr[0] % 3
                    sc_rr[0] += 1

                    @P.on("tensor")
                    def _(e, i=i, c=c, Lc=Lc, bk=bk, tok0=tok0,
                          wfree=sc_free[bk], wv=v_vtok):
                        if i == 0 and c == 0:
                            e.wait_ge(mV.h, wv)
                        if wfree[0]:
                            e.wait_ge(mA.h, wfree[0])
                        ins = e.matmul(pf[bk][:, 0:Lc],
                                       qT[:, tok0 + i * 128: tok0 + (i + 1) * 128],
                                       kT[:, tok0 + 512 * c: tok0 + 512 * c + Lc],
                                       start=True, stop=(c != i // 4))
                        if c == i // 4:
                            off = i * 128 - 512 * c
                            ins = e.matmul(pf[bk][:, off:off + 128], tri[:],
                                           idn_bf[:], start=False, stop=True)
                        ins.then_inc(mP.h, 1)
                    p_sc = mP.bump()

                    @P.on("scalar")
                    def _(e, i=i, c=c, Lc=Lc, bk=bk, w=p_sc,
                          wae=(st_vscl[i % 2] if c == 0 else 0)):
                        e.wait_ge(mP.h, w)
                        if wae:
                            e.wait_ge(mV.h, wae)
                        e.activation(aeb[i % 2][:, 512 * c: 512 * c + Lc],
                                     pf[bk][:, 0:Lc],
                                     AF.Exp, bias=zero_c[:], scale=1.0,
                                     accum_out=rs2[i % 2][:, c:c + 1],
                                     ).then_inc(mA.h, 1)
                    a_cp = mA.bump()
                    sc_free[bk] = (a_cp, 0)
                    if c == nch - 1:
                        st_aexp[i % 2] = a_cp

            def emit_tail(i):
                L = st_L[i % 2]
                nch = st_nch[i % 2]

                asb_rd_prev = asb_rd[0]

                @P.on("vector")
                def _(e, i=i, L=L, nch=nch, w=st_aexp[i % 2], wtr=asb_rd_prev):
                    e.wait_ge(mA.h, w)
                    if wtr:
                        e.wait_ge(mP.h, wtr)
                    e.reduce_sum(rsum[:], rs2[i % 2][:, 0:nch], axis=AX.X)
                    e.drain()
                    e.reciprocal(rrec[:], rsum[:])
                    e.drain()
                    e.tensor_scalar(asb[:, 0:L], aeb[i % 2][:, 0:L], rrec[:],
                                    None, ALU.mult)
                    e.drain().then_inc(mV.h, 1)
                st_vscl[i % 2] = mV.bump()

                ngrp = (i + 8) // 8
                atn_rd_prev = atn_rd[i % 2]
                for g in range(ngrp):
                    j0, j1 = 8 * g, min(8 * g + 8, i + 1)

                    @P.on("tensor")
                    def _(e, g=g, i=i, j0=j0, j1=j1, w=st_vscl[i % 2],
                          wfree=pbt_free[g % 2]):
                        if g == 0:
                            e.wait_ge(mV.h, w)
                        if wfree[1]:
                            e.wait_ge(wfree[0], wfree[1])
                        ins = None
                        for j in range(j0, j1):
                            ins = e.transpose(
                                pb[g % 2][:, (j - j0) * 128:(j - j0 + 1) * 128],
                                asb[:, j * 128:(j + 1) * 128], idn_bf[:])
                        ins.then_inc(mP.h, 1)
                    p_tr = mP.bump()
                    asb_rd[0] = p_tr

                    @P.on("scalar")
                    def _(e, g=g, i=i, j0=j0, j1=j1, w=p_tr,
                          wrd=(atn_rd_prev if g == 0 else 0),
                          last=(g == ngrp - 1)):
                        e.wait_ge(mP.h, w)
                        if wrd:
                            e.wait_ge(mP.h, wrd)
                        ins = e.copy(
                            atnT[i % 2][:, j0 * 128: j1 * 128],
                            pb[g % 2][:, 0:(j1 - j0) * 128])
                        if last:
                            e.drain().then_inc(mA.h, 1)
                        else:
                            ins.then_inc(mA.h, 1)
                    v_tr = mA.bump()
                    pbt_free[g % 2] = (mA.h, v_tr)

                @P.on("tensor")
                def _(e, i=i, b=b, w=v_tr, wavf=av_free[i % 2]):
                    e.wait_ge(mA.h, w)
                    if wavf:
                        e.wait_ge(mV.h, wavf)
                    ins = None
                    for j in range(i + 1):
                        ins = e.matmul(pf[3 + i % 2][:, 0:128],
                                       vT[:, (b * 32 + j) * 128:
                                          (b * 32 + j + 1) * 128],
                                       atnT[i % 2][:, j * 128:(j + 1) * 128],
                                       start=(j == 0), stop=(j == i))
                    ins.then_inc(mP.h, 1)
                st_pavg[i % 2] = mP.bump()
                atn_rd[i % 2] = st_pavg[i % 2]

                @P.on("vector")
                def _(e, i=i, w=st_pavg[i % 2],
                      wdma=(g_av.get(blkid - 1, 0) if i == 0 else 0)):
                    e.wait_ge(mP.h, w)
                    if wdma:
                        e.wait_ge(sAV.h, wdma)
                    ins = e.tensor_copy(avb[:, i * 128:(i + 1) * 128],
                                        pf[3 + i % 2][:, 0:128])
                    if i == QB - 1:
                        e.drain().then_inc(mV.h, 1)
                    else:
                        ins.then_inc(mV.h, 1)
                av_free[i % 2] = mV.bump()

            if ATT_PIPE:
                for i in range(QB):
                    emit_scores(i)
                    if i >= 1:
                        emit_tail(i - 1)
                emit_tail(QB - 1)
            else:
                for i in range(QB):
                    emit_scores(i)
                    emit_tail(i)
            v_bh = mV.n

            @P.on("gpsimd")
            def _(e, hid=hid, b=b, w=v_bh):
                e.wait_ge(mV.h, w)
                e.dma_start(out=ag5_in[hid * 128:(hid + 1) * 128,
                                       b * S:(b + 1) * S],
                            in_=avb[:]).then_inc(sAV.h, 16)
            g_av[blkid] = sAV.bump(16)
        return mP.n

    p_head0 = emit_attention(0, 0, v_qev[15])

    for iq, (dst, src) in enumerate(((qT, qs1), (kT, ks1), (vT, vs1))):
        @P.on("sync")
        def _(e, dst=dst, src=src, iq=iq, w=p_head0, wh14=h1_st[14],
              wh15=h1_st[15]):
            if iq == 0:
                e.wait_ge(mP.h, w)
                e.wait_ge(sH1[0].h, wh14)
                e.wait_ge(sH1[1].h, wh15)
            e.dma_start(out=dst[:], in_=src).then_inc(sRE.h, 16)
        h1_re[iq] = sRE.bump(16)

    p_head1 = emit_attention(1, h1_re[2], 0)
    v_att_end = mV.n

    @P.on("gpsimd")
    def _(e, w=g_av[3]):
        e.wait_ge(sAV.h, w)
        e.collective_compute("AllGather", ALU.bypass, RG, [ag5_in],
                             [ag5_out]).then_inc(cC.h)
    cc_ag5 = cC.bump()

    if dbg:
        @P.on("gpsimd")
        def _(e, w=v_qev[15]):
            e.wait_ge(mV.h, w)
            e.dma_start(out=DBG["d_qT"], in_=qT[:]).then_inc(dG.h, 16)
            e.dma_start(out=DBG["d_kT"], in_=kT[:]).then_inc(dG.h, 16)
        dG.bump(32)

        @P.on("gpsimd")
        def _(e, w=g_av[0]):
            e.wait_ge(dG.h, w)
            e.dma_start(out=DBG["d_vtok"], in_=vT[:]).then_inc(dG.h, 16)
            e.dma_start(out=DBG["d_av"], in_=avb[:]).then_inc(dG.h, 16)
        dG.bump(32)

    esB3.close()
    esB1.close()

    esC1 = ExitStack()
    nc._esC1 = esC1
    h_T2 = sbx(esC1, "h_T2", [128, HT * CAP], BF16, side="right")
    esC1x = ExitStack()
    nc._esC1x = esC1x
    xd_T = sbx(esC1x, "xd_T", [128, KT * CAP], BF16)

    esC2 = ExitStack()
    nc._esC2 = esC2
    xdt = [sbx(esC2, f"xdt{i}", [128, D], BF16) for i in range(2)]

    g_xd = {}
    p_xdt = {}
    v_xdev = {}
    for j in range(NT):
        @P.on("gpsimd")
        def _(e, j=j, wtok=v_tok, wag2=cc_ag2, whd=p_head1):
            if j == 0:
                e.wait_ge(mV.h, wtok)
                e.wait_ge(cC.h, wag2)
                e.wait_ge(mP.h, whd)
            if j >= 2:
                e.wait_ge(mP.h, p_xdt[j - 2])
            e.memset(xdt[j % 2][:], 0.0)
            e.indirect_dma_start(
                xdt[j % 2][:], None, ag2_out,
                IndirectOffsetOnAxis(ap=tok_u[:, j:j + 1], axis=0),
                bounds_check=T - 1, oob_is_err=False).then_inc(sXD[j % 2].h, 16)
        g_xd[j] = sXD[j % 2].bump(16)

        if dbg and j == 0:
            @P.on("gpsimd")
            def _(e, w=g_xd[0], tgt=dG.n + 32):
                e.wait_ge(dG.h, w)
                e.dma_start(out=DBG["d_xd0"], in_=xdt[0][:]).then_inc(dG.h, 16)
                e.dma_start(out=DBG["d_ag2"], in_=ag2_out[0:1024, :]).then_inc(dG.h, 16)
                e.wait_ge(dG.h, tgt)
            dG.bump(32)

        for g in range(2):
            @P.on("tensor")
            def _(e, j=j, g=g, w=g_xd[j], wfree=pbt_free[(2 * j + g) % 2]):
                if g == 0:
                    e.wait_ge(sXD[j % 2].h, w)
                if wfree[1]:
                    e.wait_ge(wfree[0], wfree[1])
                ins = None
                for kk in range(8):
                    k = g * 8 + kk
                    ins = e.transpose(
                        pb[(2 * j + g) % 2][:, kk * 128:(kk + 1) * 128],
                        xdt[j % 2][:, k * 128:(k + 1) * 128], idn_bf[:])
                ins.then_inc(mP.h, 1)
            pw = mP.bump()

            @P.on("vector")
            def _(e, j=j, g=g, w=pw):
                e.wait_ge(mP.h, w)
                xv = xd_T[:].rearrange("p (k t2) -> p k t2", t2=CAP)
                e.tensor_copy(xv[:, g * 8:(g + 1) * 8, j * 128:(j + 1) * 128],
                              pb[(2 * j + g) % 2][:]
                              .rearrange("p (kk t2) -> p kk t2", t2=128))
                e.drain().then_inc(mV.h, 1)
            pbt_free[(2 * j + g) % 2] = (mV.h, mV.bump())
        p_xdt[j] = mP.n
    v_xdT = mV.n

    esC2.close()

    esC3 = ExitStack()
    nc._esC3 = esC3
    wslC = [sbx(esC3, f"wslC{i}", [128, D], BF16) for i in range(4)]
    hsilC = [sbx(esC3, f"hsilC{i}", [128, 512], BF16) for i in range(2)]

    p_hn2 = {}
    v_hev2 = {}
    for mH in range(HT):
        @P.on("gpsimd")
        def _(e, mH=mH, whd=p_head1):
            if mH == 0:
                e.wait_ge(mP.h, whd)
            if mH <= 1:
                e.wait_ge(mP.h, p_xdt[NT - 1])
            if mH >= 2:
                e.wait_ge(mP.h, p_hn2[(mH - 2, 1)])
            e.dma_start(out=wslC[mH % 2][:],
                        in_=W1[mH]).then_inc(sW13[mH % 2].h, 16)
            e.dma_start(out=wslC[2 + mH % 2][:],
                        in_=W3[mH]).then_inc(sW13[mH % 2].h, 16)
        w13_ld = sW13[mH % 2].bump(32)

        for n in range(2):
            idx = 2 * mH + n

            @P.on("tensor")
            def _(e, mH=mH, n=n, idx=idx, w=w13_ld, wxd=v_xdT, wat=v_att_end):
                if n == 0:
                    e.wait_ge(sW13[mH % 2].h, w)
                if idx == 0:
                    e.wait_ge(mV.h, wxd)
                if idx <= 1:
                    e.wait_ge(mV.h, wat)
                par = idx % 2
                if idx >= 2:
                    e.wait_ge(mV.h, v_hev2[idx - 2])
                b1, b3 = (pf[0], pf[1]) if par == 0 else (pf[4], pf[5])
                for k in range(KT):
                    e.matmul(b1[:], wslC[mH % 2][:, k * 128:(k + 1) * 128],
                             xd_T[:, k * CAP + n * 512: k * CAP + n * 512 + 512],
                             start=(k == 0), stop=(k == KT - 1))
                ins = None
                for k in range(KT):
                    ins = e.matmul(b3[:], wslC[2 + mH % 2][:, k * 128:(k + 1) * 128],
                                   xd_T[:, k * CAP + n * 512: k * CAP + n * 512 + 512],
                                   start=(k == 0), stop=(k == KT - 1))
                ins.then_inc(mP.h, 1)
            p_hn2[(mH, n)] = mP.bump()

            @P.on("scalar")
            def _(e, idx=idx, w=p_hn2[(mH, n)]):
                e.wait_ge(mP.h, w)
                e.activation(hsilC[idx % 2][:], pf[0 if idx % 2 == 0 else 4][:],
                             AF.Silu, bias=0.0, scale=1.0).then_inc(mA.h, 1)
            a_sil2 = mA.bump()

            @P.on("vector")
            def _(e, mH=mH, n=n, idx=idx, wa=a_sil2):
                e.wait_ge(mA.h, wa)
                e.tensor_tensor(
                    h_T2[:, mH * CAP + n * 512: mH * CAP + n * 512 + 512],
                    hsilC[idx % 2][:], pf[1 if idx % 2 == 0 else 5][:],
                    ALU.mult)
                e.drain().then_inc(mV.h, 1)
            v_hev2[idx] = mV.bump()
    v_hev2_last = v_hev2[2 * HT - 1]
    a_sil2_last = mA.n

    if dbg:
        @P.on("gpsimd")
        def _(e, w=v_xdT, w2=v_hev2_last, tgt=dG.n + 16):
            e.wait_ge(mV.h, w)
            e.dma_start(out=DBG["d_xdT"], in_=xd_T[:]).then_inc(dG.h, 16)
            e.wait_ge(dG.h, tgt)
            e.wait_ge(mV.h, w2)
            e.dma_start(out=DBG["d_hT"], in_=h_T2[:]).then_inc(dG.h, 16)
        dG.bump(32)
    esC3.close()
    esC1x.close()

    esC4 = ExitStack()
    nc._esC4 = esC4
    w2slC = [sbx(esC4, f"w2slC{i}", [128, HT * 512], BF16) for i in range(2)]
    oe_st = [sbx(esC4, f"oest{i}", [128, 512], BF16) for i in range(2)]

    p_w2last2 = [0] * 4
    v_oeev = {}
    g_oest = {}
    for n in range(4):
        @P.on("gpsimd")
        def _(e, n=n, whe=v_hev2_last):
            if n == 0:
                e.wait_ge(mP.h, p_hn2[(HT - 1, 1)])
                e.wait_ge(mV.h, whe)
            if n >= 2:
                e.wait_ge(mP.h, p_w2last2[n - 2])
            e.dma_start(out=w2slC[n % 2][:],
                        in_=W2[n]).then_inc(sW2b[n % 2].h, 16)
        w2_ld = sW2b[n % 2].bump(16)

        for j in range(NT):
            gi = n * NT + j

            @P.on("tensor")
            def _(e, n=n, j=j, gi=gi, w=w2_ld, whev=v_hev2_last):
                if gi == 0:
                    e.wait_ge(mV.h, whev)
                if j == 0:
                    e.wait_ge(sW2b[n % 2].h, w)
                par = gi % 2
                if gi >= 2:
                    e.wait_ge(mV.h, v_oeev[gi - 2])
                bank = pf[2 + par]
                ins = None
                for h in range(HT):
                    ins = e.matmul(bank[:],
                                   h_T2[:, h * CAP + j * 128: h * CAP + (j + 1) * 128],
                                   w2slC[n % 2][:, h * 512:(h + 1) * 512],
                                   start=(h == 0), stop=(h == HT - 1))
                ins.then_inc(mP.h, 1)
            p_oe = mP.bump()
            if j == NT - 1:
                p_w2last2[n] = p_oe

            @P.on("vector")
            def _(e, n=n, j=j, gi=gi, w=p_oe):
                e.wait_ge(mP.h, w)
                if gi >= 2:
                    e.wait_ge(sOE[gi % 2].h, g_oest[gi - 2])
                e.tensor_copy(oe_st[gi % 2][:], pf[2 + gi % 2][:])
                e.drain().then_inc(mV.h, 1)
            v_oe = mV.bump()
            v_oeev[gi] = v_oe

            @P.on("gpsimd")
            def _(e, n=n, j=j, gi=gi, w=v_oe):
                e.wait_ge(mV.h, w)
                e.dma_start(out=ag4_in[j * 128:(j + 1) * 128,
                                       n * 512:(n + 1) * 512],
                            in_=oe_st[gi % 2][:]).then_inc(sOE[gi % 2].h, 16)
            g_oest[gi] = sOE[gi % 2].bump(16)

    @P.on("gpsimd")
    def _(e, w14=g_oest[4 * NT - 2], w15=g_oest[4 * NT - 1]):
        e.wait_ge(sOE[0].h, w14)
        e.wait_ge(sOE[1].h, w15)
        e.collective_compute("AllGather", ALU.bypass, RG, [ag4_in],
                             [ag4_out]).then_inc(cC.h)
    cc_ag4 = cC.bump()


    esC4.close()
    esC1.close()

    esD = ExitStack()
    nc._esD = esD
    att_own = sbx(esD, "att_own", [128, KT * OWN], BF16)
    woutb = [sbx(esD, f"woutb{i}", [128, KT * 512], BF16) for i in range(2)]
    gtl = [sbx(esD, f"gtl{i}", [128, D], BF16) for i in range(2)]

    g_att = {}
    for k in range(KT):
        @P.on("gpsimd")
        def _(e, k=k, w=cc_ag5, wr=v_route, wc=p_w2last2[3]):
            if k == 0:
                e.wait_ge(cC.h, w)
                e.wait_ge(mV.h, wr)
                e.wait_ge(mP.h, wc)
            e.indirect_dma_start(
                att_own[:, k * OWN:(k + 1) * OWN], None, ag5_out,
                IndirectOffsetOnAxis(ap=att_ix[:, k:k + 1], axis=1),
                bounds_check=NCORES * 256 * T - 1,
                oob_is_err=False).then_inc(sATT.h, 16)
    g_att = sATT.bump(16 * KT)

    p_wolast = [0] * 4
    v_woev = {}
    for n in range(4):
        @P.on("gpsimd")
        def _(e, n=n):
            if n >= 2:
                e.wait_ge(mP.h, p_wolast[n - 2])
            e.dma_start(out=woutb[n % 2][:],
                        in_=WOUT[n]).then_inc(sWO[n % 2].h, 16)
        wo_ld = sWO[n % 2].bump(16)

        for j in range(NT):
            gi = n * NT + j

            @P.on("tensor")
            def _(e, n=n, j=j, gi=gi, w=wo_ld, wgat=g_att,
                  wsl2=a_sil2_last):
                if gi == 0:
                    e.wait_ge(sATT.h, wgat)
                    e.wait_ge(mA.h, wsl2)
                if j == 0:
                    e.wait_ge(sWO[n % 2].h, w)
                par = gi % 2
                if gi >= 2:
                    e.wait_ge(mV.h, v_woev[gi - 2])
                bank = pf[par]
                ins = None
                for k in range(KT):
                    ins = e.matmul(bank[:],
                                   att_own[:, k * OWN + j * 128:
                                           k * OWN + (j + 1) * 128],
                                   woutb[n % 2][:, k * 512:(k + 1) * 512],
                                   start=(k == 0), stop=(k == KT - 1))
                ins.then_inc(mP.h, 1)
            p_wo = mP.bump()
            if j == NT - 1:
                p_wolast[n] = p_wo

            @P.on("vector")
            def _(e, n=n, j=j, gi=gi, w=p_wo):
                e.wait_ge(mP.h, w)
                sl = acc[:, j * D + n * 512: j * D + (n + 1) * 512]
                e.scalar_tensor_tensor(sl, pf[gi % 2][:], 1.0, sl,
                                       ALU.mult, ALU.add)
                e.drain().then_inc(mV.h, 1)
            v_woev[gi] = mV.bump()
    v_wo_last = v_woev[4 * NT - 1]

    v_comb = {}
    for j in range(NT):
        @P.on("gpsimd")
        def _(e, j=j, w=cc_ag4):
            if j == 0:
                e.wait_ge(cC.h, w)
            if j < 2:
                e.memset(gtl[j % 2][:], 0.0)
            if j >= 2:
                e.wait_ge(mV.h, v_comb[j - 2])
            e.indirect_dma_start(
                gtl[j % 2][:], None, ag4_out,
                IndirectOffsetOnAxis(ap=comb_u[:, j:j + 1], axis=0),
                bounds_check=E * CAP - 1, oob_is_err=False).then_inc(sCB[j % 2].h, 16)
        g_cb = sCB[j % 2].bump(16)

        @P.on("vector")
        def _(e, j=j, w=g_cb):
            e.wait_ge(sCB[j % 2].h, w)
            sl = acc[:, j * D:(j + 1) * D]
            e.scalar_tensor_tensor(sl, gtl[j % 2][:], gk[:, j:j + 1], sl,
                                   ALU.mult, ALU.add)
            e.drain().then_inc(mV.h, 1)
        v_comb[j] = mV.bump()

    @P.on("gpsimd")
    def _(e, wv=v_comb[NT - 1], ww=v_wo_last, tgt=dG.n + 16):
        e.wait_ge(mV.h, wv)
        e.wait_ge(mV.h, ww)
        e.dma_start(out=Y.rearrange("(i p) d -> p i d", p=128),
                    in_=acc[:].rearrange("p (i d) -> p i d", d=D)).then_inc(dG.h, 16)
        e.wait_ge(dG.h, tgt)
    dG.bump(16)

    if dbg:
        @P.on("gpsimd")
        def _(e, tgt=dG.n + 16 * 9):
            for nm, src in (("d_lown", l_own), ("d_posrm", pos_rm),
                            ("d_tok", tok_u), ("d_comb", comb_u), ("d_gk", gk),
                            ("d_eidx", eidx), ("d_posv", posv),
                            ("d_eexp", eexp), ("d_ftok", f_tok)):
                e.dma_start(out=DBG[nm], in_=src[:]).then_inc(dG.h, 16)
            e.wait_ge(dG.h, tgt)
        dG.bump(16 * 9)

    esD.close()

    with nc.Block() as block:
        @block.sync
        def _(eng):
            for fn in P.items["sync"]:
                fn(eng)

        @block.tensor
        def _(eng):
            for fn in P.items["tensor"]:
                fn(eng)

        @block.vector
        def _(eng):
            for fn in P.items["vector"]:
                fn(eng)

        @block.scalar
        def _(eng):
            for fn in P.items["scalar"]:
                fn(eng)

        @block.gpsimd
        def _(eng):
            for fn in P.items["gpsimd"]:
                fn(eng)

    return nc


def _host_inputs(L0_attn_x, L0_attn_w_qkv, L0_attn_w_out, L0_ffn_Wg, L0_ffn_W1,
                 L0_ffn_W3, L0_ffn_W2, L0_ffn_sw1, L0_ffn_sw3, L0_ffn_sw2):
    f32 = np.float32
    x = np.ascontiguousarray(np.asarray(L0_attn_x, f32).reshape(T, D))
    wqkv = np.asarray(L0_attn_w_qkv, f32)
    idn = np.eye(128, dtype=f32)
    trim = np.triu(np.full((128, 128), -1e9, dtype=f32), 1)
    sltb = np.zeros((64, 64), dtype=f32)
    for rp in range(8):
        for ep in range(8):
            for r2 in range(rp):
                sltb[r2 * 8 + ep, rp * 8 + ep] = 1.0

    def kslice(w, c0, w_=128):
        return np.ascontiguousarray(
            w[:, c0:c0 + w_].reshape(KT, 128, w_).transpose(1, 0, 2)
            .reshape(128, KT * w_))

    def lhs_tiles(w):
        return np.ascontiguousarray(
            w.reshape(KT, 128, HT, 128).transpose(2, 1, 0, 3)
            .reshape(HT, 128, KT * 128))

    def rhs_chunks(w, kt):
        return np.ascontiguousarray(
            w.reshape(kt, 128, 4, 512).transpose(2, 1, 0, 3)
            .reshape(4, 128, kt * 512))

    wg_pre = np.ascontiguousarray(
        np.asarray(L0_ffn_Wg, f32).reshape(KT, 128, E).transpose(1, 0, 2)
        .reshape(128, KT * E))
    sw1p = lhs_tiles(np.asarray(L0_ffn_sw1, f32))
    sw3p = lhs_tiles(np.asarray(L0_ffn_sw3, f32))
    sw2p = rhs_chunks(np.asarray(L0_ffn_sw2, f32), HT)
    woutp = rhs_chunks(np.asarray(L0_attn_w_out, f32), KT)
    maps = []
    for c in range(NCORES):
        esel = np.zeros((128, 512), dtype=f32)
        rsel = np.zeros((128, 512), dtype=f32)
        for tl in range(8):
            for r in range(8):
                esel[:, tl * 64 + r * 8 + c] = 1.0
            for ee in range(8):
                rsel[:, tl * 64 + c * 8 + ee] = 1.0
        wq = np.concatenate([kslice(wqkv, 256 * c, 256),
                             kslice(wqkv, D + 256 * c, 256),
                             kslice(wqkv, 2 * D + 256 * c, 256)], axis=1)
        qb = wq[:, 0:KT * 256].reshape(128, KT, 256)
        kb = wq[:, KT * 256:2 * KT * 256].reshape(128, KT, 256)
        vb = wq[:, 2 * KT * 256:].reshape(128, KT, 256)
        wqp = np.ascontiguousarray(
            np.concatenate([qb, kb, vb], axis=2).reshape(128, KT * 768))
        maps.append({
            "x_own": np.ascontiguousarray(x[c * OWN:(c + 1) * OWN]),
            "wqkv_sl": wqp,
            "w_out": woutp,
            "wg": wg_pre,
            "w1_e": lhs_tiles(np.asarray(L0_ffn_W1[c], f32)),
            "w3_e": lhs_tiles(np.asarray(L0_ffn_W3[c], f32)),
            "w2_e": rhs_chunks(np.asarray(L0_ffn_W2[c], f32), HT),
            "sw1": sw1p,
            "sw3": sw3p,
            "sw2": sw2p,
            "idn_bf": np.ascontiguousarray(idn.astype(BF)),
            "idn_f32": idn,
            "tri": np.ascontiguousarray(trim.T.astype(BF)),
            "esel_rep": esel,
            "rsel_rep": rsel,
            "tbase": np.full((128, 1), 1024.0 * c, dtype=f32),
            "sltb": sltb,
        })
    return maps


def kernel(**inputs):
    maps = _host_inputs(**inputs)
    nc = build_nc()
    res = run_bass_kernel_spmd(nc, maps, core_ids=list(range(NCORES)))
    out = np.concatenate([np.asarray(r["y"]) for r in res.results], axis=0)
    return out.reshape(2, S, D).astype(np.float32)


# version 11
# speedup vs baseline: 1.0701x; 1.0701x over previous
import numpy as np
import ml_dtypes
import concourse.bass as bass
import concourse.mybir as mybir
from concourse.bass_utils import run_bass_kernel_spmd
from concourse.bass import IndirectOffsetOnAxis
from contextlib import ExitStack

F32 = mybir.dt.float32
BF16 = mybir.dt.bfloat16
U32 = mybir.dt.uint32
AF = mybir.ActivationFunctionType
ALU = mybir.AluOpType
AX = mybir.AxisListType
BF = ml_dtypes.bfloat16

NCORES = 8
D = 2048
T = 8192
OWN = 1024
NT = 8
KT = 16
HF = 2816
HT = 22
HD = 128
S = 4096
E = 8
CAP = 1024
QB = 32
RG = [list(range(NCORES))]
RSQ = float(1.0 / np.sqrt(np.float32(HD)))
ATT_PIPE = True


class Sem:
    def __init__(self, nc, es, name):
        self.h = es.enter_context(nc.semaphore(name))
        self.n = 0

    def bump(self, step=1):
        self.n += step
        return self.n


class Plan:
    ENGS = ("sync", "tensor", "vector", "scalar", "gpsimd")

    def __init__(self):
        self.items = {e: [] for e in self.ENGS}

    def on(self, eng):
        def deco(fn):
            self.items[eng].append(fn)
        return deco


def build_nc(dbg=False):
    nc = bass.Bass("TRN2", target_bir_lowering=False, debug=False)

    X = nc.dram_tensor("x_own", [OWN, D], F32, kind="ExternalInput").ap()
    WQKV = nc.dram_tensor("wqkv_sl", [128, KT * 768], F32, kind="ExternalInput").ap()
    WOUT = nc.dram_tensor("w_out", [4, 128, KT * 512], F32, kind="ExternalInput").ap()
    WG = nc.dram_tensor("wg", [128, KT * E], F32, kind="ExternalInput").ap()
    W1 = nc.dram_tensor("w1_e", [HT, 128, KT * 128], F32, kind="ExternalInput").ap()
    W3 = nc.dram_tensor("w3_e", [HT, 128, KT * 128], F32, kind="ExternalInput").ap()
    W2 = nc.dram_tensor("w2_e", [4, 128, HT * 512], F32, kind="ExternalInput").ap()
    SW1 = nc.dram_tensor("sw1", [HT, 128, KT * 128], F32, kind="ExternalInput").ap()
    SW3 = nc.dram_tensor("sw3", [HT, 128, KT * 128], F32, kind="ExternalInput").ap()
    SW2 = nc.dram_tensor("sw2", [4, 128, HT * 512], F32, kind="ExternalInput").ap()
    IDNB = nc.dram_tensor("idn_bf", [128, 128], BF16, kind="ExternalInput").ap()
    IDNF = nc.dram_tensor("idn_f32", [128, 128], F32, kind="ExternalInput").ap()
    TRI = nc.dram_tensor("tri", [128, 128], BF16, kind="ExternalInput").ap()
    TRIU = nc.dram_tensor("triu", [128, 128], BF16, kind="ExternalInput").ap()
    ESEL = nc.dram_tensor("esel_rep", [128, 512], F32, kind="ExternalInput").ap()
    RSEL = nc.dram_tensor("rsel_rep", [128, 512], F32, kind="ExternalInput").ap()
    TBASE = nc.dram_tensor("tbase", [128, 1], F32, kind="ExternalInput").ap()
    SLTB = nc.dram_tensor("sltb", [64, 64], F32, kind="ExternalInput").ap()
    Y = nc.dram_tensor("y", [OWN, D], F32, kind="ExternalOutput").ap()

    DBG = {}
    if dbg:
        for nm, shp, dt in [("d_lown", [128, 64], F32), ("d_posrm", [64, 1024], F32),
                            ("d_tok", [128, 8], U32), ("d_comb", [128, 8], U32),
                            ("d_gk", [128, 8], F32), ("d_eidx", [128, 8], F32),
                            ("d_posv", [128, 8], F32), ("d_ftok", [128, 64], F32),
                            ("d_xnT", [128, KT * OWN], BF16),
                            ("d_acc", [128, NT * D], F32),
                            ("d_eexp", [128, 64], F32),
                            ("d_hT", [128, HT * OWN], BF16),
                            ("d_qT", [128, T], BF16), ("d_kT", [128, T], BF16),
                            ("d_vtok", [128, T], BF16), ("d_av", [128, S], BF16),
                            ("d_xdT", [128, KT * CAP], BF16),
                            ("d_ag2", [1024, D], BF16),
                            ("d_xd0", [128, D], BF16),
                            ("d_oec", [128, 8 * 128], BF16)]:
            DBG[nm] = nc.dram_tensor(nm, shp, dt, kind="ExternalOutput").ap()

    ag1_in = nc.dram_tensor("ag1_in", [D, OWN], BF16).ap()
    ag1_out = nc.dram_tensor("ag1_out", [NCORES * D, OWN], BF16, addr_space="Shared").ap()
    ag2_in = nc.dram_tensor("ag2_in", [OWN, D], BF16).ap()
    ag2_out = nc.dram_tensor("ag2_out", [T, D], BF16, addr_space="Shared").ap()
    ag3_in = nc.dram_tensor("ag3_in", [E, OWN], F32).ap()
    ag3_out = nc.dram_tensor("ag3_out", [NCORES * E, OWN], F32, addr_space="Shared").ap()
    ag4_in = nc.dram_tensor("ag4_in", [CAP, D], BF16).ap()
    ag4_out = nc.dram_tensor("ag4_out", [E * CAP, D], BF16, addr_space="Shared").ap()
    ag5_in = nc.dram_tensor("ag5_in", [256, T], BF16).ap()
    ag5_out = nc.dram_tensor("ag5_out", [NCORES * 256, T], BF16, addr_space="Shared").ap()
    qs1 = nc.dram_tensor("qs1", [128, T], BF16).ap()
    ks1 = nc.dram_tensor("ks1", [128, T], BF16).ap()
    vs1 = nc.dram_tensor("vs1", [128, T], BF16).ap()

    P = Plan()
    es = ExitStack()
    nc._kernel_es = es

    dS = Sem(nc, es, "dS")
    dG = Sem(nc, es, "dG")
    dA = Sem(nc, es, "dA")
    cC = Sem(nc, es, "cC")
    mP = Sem(nc, es, "mP")
    mV = Sem(nc, es, "mV")
    mA = Sem(nc, es, "mA")
    mG = Sem(nc, es, "mG")
    sPC = Sem(nc, es, "sPC")
    sW = [Sem(nc, es, f"sW{i}") for i in range(2)]
    sW2 = [Sem(nc, es, f"sW2{i}") for i in range(2)]
    sRB = [Sem(nc, es, f"sRB{i}") for i in range(3)]
    sRE = Sem(nc, es, "sRE")
    sAG1 = Sem(nc, es, "sAG1")
    sAG2 = Sem(nc, es, "sAG2")
    sAG3 = Sem(nc, es, "sAG3")
    sRM = Sem(nc, es, "sRM")
    sQKV = Sem(nc, es, "sQKV")
    sH1 = [Sem(nc, es, f"sH1{i}") for i in range(2)]
    sAV = Sem(nc, es, "sAV")
    sXD = [Sem(nc, es, f"sXD{i}") for i in range(2)]
    sW13 = [Sem(nc, es, f"sW13{i}") for i in range(2)]
    sW2b = [Sem(nc, es, f"sW2b{i}") for i in range(2)]
    sOE = [Sem(nc, es, f"sOE{i}") for i in range(2)]
    sATT = Sem(nc, es, "sATT")
    sWO = [Sem(nc, es, f"sWO{i}") for i in range(2)]
    sCB = [Sem(nc, es, f"sCB{i}") for i in range(2)]

    def sbx(stack, name, shape, dt, side=None):
        return stack.enter_context(
            nc.sbuf_tensor(name, list(shape), dt, side=side))

    acc = sbx(es, "acc", [128, NT * D], F32)
    idn_bf = sbx(es, "idn_bf_s", [128, 128], BF16)
    idn_f = sbx(es, "idn_f_s", [128, 128], F32)
    tri = sbx(es, "tri_s", [128, 128], BF16)
    triu = sbx(es, "triu_s", [128, 128], BF16)
    negf = sbx(es, "negf_s", [128, 128], BF16)
    ones_r = sbx(es, "ones_r", [1, 128], F32)
    ones_bf = sbx(es, "ones_bf", [128, 1], BF16)
    esel = sbx(es, "esel_s", [128, 512], F32)
    rsel = sbx(es, "rsel_s", [128, 512], F32)
    tbase = sbx(es, "tbase_s", [128, 1], F32)
    sltb = sbx(es, "sltb_s", [64, 64], F32)
    wg_f = sbx(es, "wg_f", [128, KT * E], F32)
    wg_hi = sbx(es, "wg_hi", [128, KT * E], BF16)
    wg_lo = sbx(es, "wg_lo", [128, KT * E], BF16)
    iotaE = sbx(es, "iotaE", [128, 64], F32)
    iotaP = sbx(es, "iotaP", [128, 1], F32)
    ones_c = sbx(es, "ones_c", [128, 1], BF16)
    l_own = sbx(es, "l_own", [128, 64], F32)
    lmax = sbx(es, "lmax_s", [128, 8], F32)
    eexp = sbx(es, "eexp", [128, 64], F32)
    sume = sbx(es, "sume", [128, 8], F32)
    gate = sbx(es, "gate_s", [128, 8], F32)
    oh = sbx(es, "oh_s", [128, 64], F32)
    pr1 = sbx(es, "pr1", [128, 64], F32)
    pr2 = sbx(es, "pr2", [128, 64], F32)
    ohx = sbx(es, "ohx_s", [128, 64], F32)
    eidx = sbx(es, "eidx_s", [128, 8], F32)
    ohxT = sbx(es, "ohxT", [8, OWN], F32)
    ohx_rm = sbx(es, "ohx_rm", [64, 1024], F32)
    ones64 = sbx(es, "ones64", [64, 1024], F32)
    pos_rm = sbx(es, "pos_rm", [64, 1024], F32)
    tot64 = sbx(es, "tot64", [64, 1], F32)
    off64 = sbx(es, "off64", [64, 1], F32)
    pos_tok = sbx(es, "pos_tok", [128, 512], F32)
    mscr = sbx(es, "mscr", [128, 512], F32)
    f_tok = sbx(es, "f_tok", [128, 64], F32)
    pos_own = sbx(es, "pos_own", [128, 64], F32)
    posv = sbx(es, "posv_s", [128, 8], F32)
    keep = sbx(es, "keep_s", [128, 8], F32)
    gk = sbx(es, "gk_s", [128, 8], F32)
    comb_f = sbx(es, "comb_f", [128, 8], F32)
    comb_u = sbx(es, "comb_u", [128, 8], U32)
    cnt_sb = sbx(es, "cnt_sb", [1, 1024], F32)
    tok_f = sbx(es, "tok_f", [128, 8], F32)
    tok_u = sbx(es, "tok_u", [128, 8], U32)
    att_if = sbx(es, "att_if", [128, 16], F32)
    att_ix = sbx(es, "att_ix", [128, 16], U32)
    nm_run = sbx(es, "nm_run", [128, 1], F32)
    nm_tmp = sbx(es, "nm_tmp", [128, 1], F32)
    nmc = sbx(es, "nmc", [128, 8], F32)
    rs_s = sbx(es, "rs_s", [128, 8], F32)
    rsum = sbx(es, "rsum", [128, 1], F32)
    rrec = sbx(es, "rrec", [128, 1], F32)
    negmean = sbx(es, "negmean", [128, 1], F32)
    sumsq = sbx(es, "sumsq", [128, 1], F32)
    varb = sbx(es, "varb", [128, 1], F32)
    lnv = sbx(es, "lnv", [128, 1], F32)
    rstd = sbx(es, "rstd", [128, 1], F32)
    m2s = sbx(es, "m2s", [128, 1], F32)
    eps_c = sbx(es, "eps_c", [128, 1], F32)
    zero_c = sbx(es, "zero_c", [128, 1], F32)

    pf = [es.enter_context(nc.psum_tensor(f"pf{i}", [128, 512], F32))
          for i in range(6)]
    pb = [es.enter_context(nc.psum_tensor(f"pb{i}", [128, 1024], BF16))
          for i in range(2)]

    @P.on("sync")
    def _(e):
        e.dma_start(out=acc[:].rearrange("p (i d) -> p i d", d=D),
                    in_=X.rearrange("(i p) d -> p i d", p=128)).then_inc(dS.h, 16)
    ld_x = dS.bump(16)

    @P.on("scalar")
    def _(e):
        for dst, src in ((idn_bf, IDNB), (idn_f, IDNF), (tri, TRI),
                         (triu, TRIU), (esel, ESEL),
                         (rsel, RSEL), (tbase, TBASE), (sltb, SLTB)):
            e.dma_start(out=dst[:], in_=src).then_inc(dA.h, 16)
        e.dma_start(out=wg_f[:], in_=WG).then_inc(dA.h, 16)
    ld_const = dA.bump(16 * 8)

    @P.on("gpsimd")
    def _(e):
        e.iota(iotaE[:], [[0, 8], [1, 8]], base=0, channel_multiplier=0,
               allow_small_or_imprecise_dtypes=True)
        e.iota(iotaP[:], [[0, 1]], base=0, channel_multiplier=8192,
               allow_small_or_imprecise_dtypes=True).then_inc(mG.h, 1)
    g_iota = mG.bump()

    @P.on("vector")
    def _(e, w=ld_const):
        e.wait_ge(dA.h, w)
        e.memset(ones_c[:], 1.0)
        e.memset(ones64[:], 1.0)
        e.memset(eps_c[:], 1e-5)
        e.memset(zero_c[:], 0.0)
        e.memset(ones_r[:], 1.0)
        e.memset(ones_bf[:], 1.0)
        e.memset(negf[:], -1e9)
        e.tensor_copy(wg_hi[:], wg_f[:])
        e.drain()
        e.tensor_tensor(wg_lo[:], wg_f[:], wg_hi[:], ALU.subtract)
        e.drain().then_inc(mV.h, 1)
    v_wg = mV.bump()

    esA1 = ExitStack()
    nc._esA1 = esA1
    xnT = sbx(esA1, "xnT", [128, KT * OWN], BF16)

    esA2 = ExitStack()
    nc._esA2 = esA2
    xnf = sbx(esA2, "xnf", [128, D], F32)
    xnhi = [sbx(esA2, f"xnhi{i}", [128, D], BF16) for i in range(2)]
    xnlo = [sbx(esA2, f"xnlo{i}", [128, D], BF16) for i in range(2)]
    loT = [sbx(esA2, f"loT{i}", [128, D], BF16) for i in range(2)]

    v_tile = [0] * NT
    pT_hi = [0] * NT
    vE_hi = [0] * NT
    pT_lo = [0] * NT
    vE_lo = [0] * NT
    p_logit = [0] * NT
    ag2d = [0] * NT

    for t in range(NT):
        xt = acc[:, t * D:(t + 1) * D]

        @P.on("vector")
        def _(e, t=t, xt=xt):
            if t == 0:
                e.wait_ge(dS.h, ld_x)
            e.reduce_sum(negmean[:], xt, axis=AX.X, negate=True)
            e.drain()
            e.tensor_scalar(negmean[:], negmean[:], 1.0 / D, None, ALU.mult)
            e.scalar_tensor_tensor(xnf[:], xt, 1.0, xt, ALU.mult, ALU.mult,
                                   accum_out=sumsq[:])
            e.drain()
            e.tensor_tensor(m2s[:], negmean[:], negmean[:], ALU.mult)
            e.tensor_scalar(varb[:], sumsq[:], 1.0 / D, None, ALU.mult)
            e.drain()
            e.tensor_tensor(varb[:], varb[:], m2s[:], ALU.subtract)
            e.drain().then_inc(mV.h, 1)
        v_stats = mV.bump()

        @P.on("scalar")
        def _(e, w=v_stats):
            e.wait_ge(mV.h, w)
            e.activation(lnv[:], varb[:], AF.Ln, bias=eps_c[:], scale=1.0)
            e.drain()
            e.activation(rstd[:], lnv[:], AF.Exp, bias=0.0,
                         scale=-0.5)
            e.drain().then_inc(mA.h, 1)
        a_rstd = mA.bump()

        @P.on("vector")
        def _(e, t=t, xt=xt, wa=a_rstd,
              wp=(pT_lo[t - 2] if t >= 2 else 0),
              wd=(ag2d[t - 2] if t >= 2 else 0)):
            e.wait_ge(mA.h, wa)
            if wp:
                e.wait_ge(mP.h, wp)
            if wd:
                e.wait_ge(sAG2.h, wd)
            e.tensor_scalar(xnf[:], xt, negmean[:], rstd[:], ALU.add, ALU.mult)
            e.tensor_copy(xnhi[t % 2][:], xnf[:])
            e.tensor_tensor(xnlo[t % 2][:], xnf[:], xnhi[t % 2][:],
                            ALU.subtract)
            e.drain().then_inc(mV.h, 1)
        v_tile[t] = mV.bump()

        @P.on("gpsimd")
        def _(e, t=t, w=v_tile[t]):
            e.wait_ge(mV.h, w)
            e.dma_start(out=ag2_in[t * 128:(t + 1) * 128, :],
                        in_=xnhi[t % 2][:]).then_inc(sAG2.h, 16)
        ag2d[t] = sAG2.bump(16)

        @P.on("tensor")
        def _(e, t=t, w=v_tile[t], wev=(vE_lo[t - 1] if t >= 1 else 0)):
            e.wait_ge(mV.h, w)
            if t == 0:
                e.wait_ge(dA.h, ld_const)
            if wev:
                e.wait_ge(mV.h, wev)
            ins = None
            for g in range(2):
                for j in range(8):
                    k = g * 8 + j
                    ins = e.transpose(pb[g][:, j * 128:(j + 1) * 128],
                                     xnhi[t % 2][:, k * 128:(k + 1) * 128],
                                     idn_bf[:])
            ins.then_inc(mP.h, 1)
        pT_hi[t] = mP.bump()

        @P.on("vector")
        def _(e, t=t, w=pT_hi[t]):
            e.wait_ge(mP.h, w)
            xv = xnT[:].rearrange("p (k t2) -> p k t2", t2=OWN)
            e.tensor_copy(xv[:, 0:8, t * 128:(t + 1) * 128],
                          pb[0][:].rearrange("p (j t2) -> p j t2", t2=128))
            e.tensor_copy(xv[:, 8:16, t * 128:(t + 1) * 128],
                          pb[1][:].rearrange("p (j t2) -> p j t2", t2=128))
            e.drain().then_inc(mV.h, 1)
        vE_hi[t] = mV.bump()

        @P.on("tensor")
        def _(e, t=t, w=vE_hi[t]):
            e.wait_ge(mV.h, w)
            ins = None
            for g in range(2):
                for j in range(8):
                    k = g * 8 + j
                    ins = e.transpose(pb[g][:, j * 128:(j + 1) * 128],
                                     xnlo[t % 2][:, k * 128:(k + 1) * 128],
                                     idn_bf[:])
            ins.then_inc(mP.h, 1)
        pT_lo[t] = mP.bump()

        @P.on("vector")
        def _(e, t=t, w=pT_lo[t]):
            e.wait_ge(mP.h, w)
            lv = loT[t % 2][:].rearrange("p (k t2) -> p k t2", t2=128)
            e.tensor_copy(lv[:, 0:8, :],
                          pb[0][:].rearrange("p (j t2) -> p j t2", t2=128))
            e.tensor_copy(lv[:, 8:16, :],
                          pb[1][:].rearrange("p (j t2) -> p j t2", t2=128))
            e.drain().then_inc(mV.h, 1)
        vE_lo[t] = mV.bump()

        @P.on("tensor")
        def _(e, t=t, w=vE_lo[t], wwg=v_wg):
            e.wait_ge(mV.h, w)
            if t == 0:
                e.wait_ge(mV.h, wwg)
            out = pf[0][:, t * 8:(t + 1) * 8]
            n = 0
            ins = None
            for k in range(KT):
                for lhs, rhs in (
                        (xnT[:, k * OWN + t * 128: k * OWN + (t + 1) * 128],
                         wg_hi[:, k * 8:(k + 1) * 8]),
                        (xnT[:, k * OWN + t * 128: k * OWN + (t + 1) * 128],
                         wg_lo[:, k * 8:(k + 1) * 8]),
                        (loT[t % 2][:, k * 128:(k + 1) * 128],
                         wg_hi[:, k * 8:(k + 1) * 8])):
                    ins = e.matmul(out, lhs, rhs, start=(n == 0), stop=(n == 47))
                    n += 1
            ins.then_inc(mP.h, 1)
        p_logit[t] = mP.bump()

    if dbg:
        @P.on("gpsimd")
        def _(e, w=v_tile[NT - 1], w2=vE_hi[NT - 1], tgt=dG.n + 32):
            e.wait_ge(mV.h, w)
            e.wait_ge(mV.h, w2)
            e.dma_start(out=DBG["d_acc"], in_=acc[:]).then_inc(dG.h, 16)
            e.dma_start(out=DBG["d_xnT"], in_=xnT[:]).then_inc(dG.h, 16)
            e.wait_ge(dG.h, tgt)
        dG.bump(32)

    @P.on("gpsimd")
    def _(e, w=vE_hi[NT - 1], wd=ag2d[NT - 1], tgt=sAG1.n + 16):
        e.wait_ge(mV.h, w)
        e.dma_start(out=ag1_in.rearrange("(k p) t2 -> p k t2", p=128),
                    in_=xnT[:].rearrange("p (k t2) -> p k t2",
                                         t2=OWN)).then_inc(sAG1.h, 16)
        e.wait_ge(sAG1.h, tgt)
        e.wait_ge(sAG2.h, wd)
        e.collective_compute("AllGather", ALU.bypass, RG, [ag1_in],
                             [ag1_out]).then_inc(cC.h)
        e.collective_compute("AllGather", ALU.bypass, RG, [ag2_in],
                             [ag2_out]).then_inc(cC.h)
    sAG1.bump(16)
    cc_ag1 = cC.bump()
    cc_ag2 = cC.bump()

    @P.on("vector")
    def _(e, w=p_logit[NT - 1], wi=g_iota):
        e.wait_ge(mP.h, w)
        e.wait_ge(mG.h, wi)
        e.tensor_copy(l_own[:], pf[0][:, 0:64])
        e.drain()
        ov = l_own[:].rearrange("p (t e2) -> p t e2", e2=8)
        e.reduce_max(lmax[:], ov, axis=AX.X)
        e.drain()
        for t in range(NT):
            e.tensor_scalar(eexp[:, t * 8:(t + 1) * 8], l_own[:, t * 8:(t + 1) * 8],
                            lmax[:, t:t + 1], None, ALU.subtract)
        for t in range(NT):
            e.tensor_scalar(oh[:, t * 8:(t + 1) * 8], l_own[:, t * 8:(t + 1) * 8],
                            lmax[:, t:t + 1], None, ALU.is_equal)
        ohv = oh[:].rearrange("p (t e2) -> p t e2", e2=8)
        p1v = pr1[:].rearrange("p (t e2) -> p t e2", e2=8)
        p2v = pr2[:].rearrange("p (t e2) -> p t e2", e2=8)
        e.drain()
        e.tensor_copy(pr1[:], oh[:])
        e.drain()
        e.tensor_add(p1v[:, :, 1:8], ohv[:, :, 1:8], ohv[:, :, 0:7])
        e.drain()
        e.tensor_copy(pr2[:], pr1[:])
        e.drain()
        e.tensor_add(p2v[:, :, 2:8], p1v[:, :, 2:8], p1v[:, :, 0:6])
        e.drain()
        e.tensor_copy(pr1[:], pr2[:])
        e.drain()
        e.tensor_add(p1v[:, :, 4:8], p2v[:, :, 4:8], p2v[:, :, 0:4])
        e.drain()
        e.tensor_scalar(ohx[:], pr1[:], 1.0, None, ALU.is_equal)
        e.drain()
        e.tensor_mul(ohx[:], ohx[:], oh[:])
        e.drain()
        e.tensor_mul(mscr[:, 0:64], ohx[:], iotaE[:])
        e.drain()
        e.reduce_sum(eidx[:], mscr[:, 0:64].rearrange("p (t e2) -> p t e2", e2=8),
                     axis=AX.X)
        for k in range(KT):
            e.tensor_scalar(att_if[:, k:k + 1], iotaP[:], tbase[:],
                            float(k * 128 * 8192), ALU.add, ALU.add)
        e.drain()
        e.tensor_copy(att_ix[:], att_if[:])
        e.drain().then_inc(mV.h, 1)
    v_rout1 = mV.bump()

    @P.on("scalar")
    def _(e, w=v_rout1):
        e.wait_ge(mV.h, w)
        ins = None
        for t in range(NT):
            ins = e.activation(eexp[:, t * 8:(t + 1) * 8],
                               eexp[:, t * 8:(t + 1) * 8], AF.Exp, bias=0.0,
                               scale=1.0, accum_out=sume[:, t:t + 1])
        ins.then_inc(mA.h, 1)
    a_rexp = mA.bump()

    @P.on("vector")
    def _(e, w=a_rexp):
        e.wait_ge(mA.h, w)
        e.reciprocal(gate[:], sume[:])
        e.drain().then_inc(mV.h, 1)
    v_gate = mV.bump()

    @P.on("tensor")
    def _(e, w=v_rout1):
        e.wait_ge(mV.h, w)
        ins = None
        for t in range(NT):
            bank, j = (pf[1], t) if t < 4 else (pf[2], t - 4)
            ins = e.transpose(bank[0:8, j * 128:(j + 1) * 128],
                              ohx[:, t * 8:(t + 1) * 8], idn_f[:])
        ins.then_inc(mP.h, 1)
    p_ohxT = mP.bump()

    @P.on("vector")
    def _(e, w=p_ohxT):
        e.wait_ge(mP.h, w)
        e.tensor_copy(ohxT[0:8, 0:512], pf[1][0:8, 0:512])
        e.tensor_copy(ohxT[0:8, 512:1024], pf[2][0:8, 0:512])
        e.drain().then_inc(mV.h, 1)
    v_ohxT = mV.bump()

    @P.on("gpsimd")
    def _(e, w=v_ohxT, tgt=sAG3.n + 16):
        e.wait_ge(mV.h, w)
        e.dma_start(out=ag3_in, in_=ohxT[:]).then_inc(sAG3.h, 16)
        e.wait_ge(sAG3.h, tgt)
        e.collective_compute("AllGather", ALU.bypass, RG, [ag3_in],
                             [ag3_out]).then_inc(cC.h)
    sAG3.bump(16)
    cc_ag3 = cC.bump()

    @P.on("sync")
    def _(e, w=cc_ag3):
        e.wait_ge(cC.h, w)
        e.dma_start(out=ohx_rm[:], in_=ag3_out).then_inc(sRM.h, 16)
    g_ohxrm = sRM.bump(16)

    esA2.close()
    esA3 = ExitStack()
    nc._esA3 = esA3
    h_T = sbx(esA3, "h_T", [128, HT * OWN], BF16, side="right")
    iota_j = sbx(esA3, "iota_j", [128, 1024], F32, side="right")
    cmpb = [sbx(esA3, f"cmpb{i}", [128, 1024], BF16, side="right")
            for i in range(2)]
    esA3a = ExitStack()
    nc._esA3a = esA3a
    wsl = [sbx(esA3a, f"wsl{i}", [128, D], BF16) for i in range(4)]
    hsil = [sbx(esA3a, f"hsil{i}", [128, 512], BF16) for i in range(2)]

    @P.on("gpsimd")
    def _(e):
        e.iota(iota_j[:], [[1, 1024]], base=0, channel_multiplier=0,
               allow_small_or_imprecise_dtypes=True).then_inc(mG.h, 1)
    g_iotaj = mG.bump()

    cmp_done = [0] * 64
    cnt_done = [0] * 64

    def emit_cmp(gt):
        @P.on("vector")
        def _(e, gt=gt, wj=g_iotaj):
            if gt == 0:
                e.wait_ge(mG.h, wj)
            if gt >= 2:
                e.wait_ge(mP.h, cnt_done[gt - 2])
            e.tensor_scalar(cmpb[gt % 2][:], iota_j[:], f_tok[:, gt:gt + 1],
                            None, ALU.is_ge)
            e.drain().then_inc(mV.h, 1)
        cmp_done[gt] = mV.bump()

    def emit_cnt(gt, extra_wait=0):
        @P.on("tensor")
        def _(e, gt=gt, wx=extra_wait):
            e.wait_ge(mV.h, cmp_done[gt])
            if wx:
                e.wait_ge(mV.h, wx)
            e.matmul(pf[0][0:1, 0:512], ones_c[:], cmpb[gt % 2][:, 0:512],
                     start=(gt == 0), stop=(gt == 63))
            e.matmul(pf[1][0:1, 0:512], ones_c[:], cmpb[gt % 2][:, 512:1024],
                     start=(gt == 0), stop=(gt == 63)).then_inc(mP.h, 1)
        cnt_done[gt] = mP.bump()

    p_hn = {}
    v_hev = {}
    sw_ld = {}
    for mH in range(HT):
        @P.on("gpsimd")
        def _(e, mH=mH, wpl=p_logit[NT - 1], wag=ag2d[NT - 1]):
            if mH == 0:
                e.wait_ge(mP.h, wpl)
                e.wait_ge(sAG2.h, wag)
            if mH >= 2:
                e.wait_ge(mP.h, p_hn[(mH - 2, 1)])
            e.dma_start(out=wsl[mH % 2][:],
                        in_=SW1[mH]).then_inc(sW[mH % 2].h, 16)
            e.dma_start(out=wsl[2 + mH % 2][:],
                        in_=SW3[mH]).then_inc(sW[mH % 2].h, 16)
        sw_ld[mH] = sW[mH % 2].bump(32)

        for n in range(2):
            idx = 2 * mH + n

            @P.on("tensor")
            def _(e, mH=mH, n=n, idx=idx, w=sw_ld[mH], wxn=vE_lo[NT - 1]):
                if n == 0:
                    e.wait_ge(sW[mH % 2].h, w)
                if idx == 0:
                    e.wait_ge(mV.h, wxn)
                par = idx % 2
                if idx >= 2:
                    e.wait_ge(mV.h, v_hev[idx - 2])
                b1, b3 = (pf[0], pf[1]) if par == 0 else (pf[4], pf[5])
                for k in range(KT):
                    e.matmul(b1[:], wsl[mH % 2][:, k * 128:(k + 1) * 128],
                             xnT[:, k * OWN + n * 512: k * OWN + n * 512 + 512],
                             start=(k == 0), stop=(k == KT - 1))
                ins = None
                for k in range(KT):
                    ins = e.matmul(b3[:], wsl[2 + mH % 2][:, k * 128:(k + 1) * 128],
                                   xnT[:, k * OWN + n * 512: k * OWN + n * 512 + 512],
                                   start=(k == 0), stop=(k == KT - 1))
                ins.then_inc(mP.h, 1)
            p_hn[(mH, n)] = mP.bump()

            @P.on("scalar")
            def _(e, idx=idx, w=p_hn[(mH, n)]):
                e.wait_ge(mP.h, w)
                e.activation(hsil[idx % 2][:], pf[0 if idx % 2 == 0 else 4][:],
                             AF.Silu, bias=0.0, scale=1.0).then_inc(mA.h, 1)
            a_sil = mA.bump()

            @P.on("vector")
            def _(e, mH=mH, n=n, idx=idx, wa=a_sil):
                e.wait_ge(mA.h, wa)
                e.tensor_tensor(h_T[:, mH * OWN + n * 512: mH * OWN + n * 512 + 512],
                                hsil[idx % 2][:], pf[1 if idx % 2 == 0 else 5][:],
                                ALU.mult)
                e.drain().then_inc(mV.h, 1)
            v_hev[idx] = mV.bump()
    v_hev_last = v_hev[2 * HT - 1]

    @P.on("vector")
    def _(e, w=g_ohxrm):
        e.wait_ge(sRM.h, w)
        e.tensor_tensor_scan(pos_rm[:], ones64[:], ohx_rm[:], 0.0,
                             ALU.mult, ALU.add)
        e.drain()
        e.tensor_copy(tot64[:], pos_rm[:, 1023:1024])
        e.drain().then_inc(mV.h, 1)
    v_scan = mV.bump()

    @P.on("tensor")
    def _(e, w=v_scan):
        e.wait_ge(mV.h, w)
        e.matmul(pf[3][0:64, 0:1], sltb[:], tot64[:], start=True,
                 stop=True).then_inc(mP.h, 1)
    p_off = mP.bump()

    @P.on("vector")
    def _(e, w=p_off):
        e.wait_ge(mP.h, w)
        e.tensor_copy(off64[:], pf[3][0:64, 0:1])
        e.drain()
        e.tensor_scalar(pos_rm[:], pos_rm[:], off64[:], None, ALU.add)
        e.drain().then_inc(mV.h, 1)
    v_posrm = mV.bump()

    @P.on("tensor")
    def _(e, w=v_posrm):
        e.wait_ge(mV.h, w)
        ins = None
        for tl in range(8):
            ins = e.transpose(pf[3][:, tl * 64:(tl + 1) * 64],
                              pos_rm[0:64, tl * 128:(tl + 1) * 128],
                              idn_f[0:64, 0:64])
        ins.then_inc(mP.h, 1)
    p_postok = mP.bump()

    @P.on("vector")
    def _(e, w=p_postok, wg=v_gate):
        e.wait_ge(mP.h, w)
        e.wait_ge(mV.h, wg)
        e.tensor_copy(pos_tok[:], pf[3][:])
        e.drain()
        e.tensor_mul(mscr[:], pos_tok[:], esel[:])
        e.drain()
        e.reduce_sum(f_tok[:],
                     mscr[:].rearrange("p (tl r e2) -> p tl r e2", r=8, e2=8),
                     axis=AX.X)
        e.drain()
        e.tensor_mul(mscr[:], pos_tok[:], rsel[:])
        e.drain()
        e.reduce_sum(pos_own[:],
                     mscr[:].rearrange("p (tl r e2) -> p tl e2 r", r=8, e2=8),
                     axis=AX.X)
        e.drain()
        e.tensor_mul(mscr[:, 0:64], pos_own[:], ohx[:])
        e.drain()
        e.reduce_sum(posv[:], mscr[:, 0:64].rearrange("p (t e2) -> p t e2", e2=8),
                     axis=AX.X)
        e.drain()
        e.tensor_scalar(posv[:], posv[:], -1.0, None, ALU.add)
        e.drain()
        e.tensor_scalar(keep[:], posv[:], float(CAP), None, ALU.is_lt)
        e.drain()
        e.tensor_mul(gk[:], gate[:], keep[:])
        e.scalar_tensor_tensor(comb_f[:], eidx[:], float(CAP), posv[:],
                               ALU.mult, ALU.add)
        e.drain()
        e.tensor_scalar(comb_f[:], comb_f[:], -1.0e6, None, ALU.add)
        e.drain()
        e.tensor_mul(comb_f[:], comb_f[:], keep[:])
        e.drain()
        e.tensor_scalar(comb_f[:], comb_f[:], 1.0e6, None, ALU.add)
        e.drain()
        e.tensor_copy(comb_u[:], comb_f[:])
        e.drain().then_inc(mV.h, 1)
    v_route = mV.bump()

    esA3a.close()
    esA1.close()
    esA3b = ExitStack()
    nc._esA3b = esA3b
    w2sl = [sbx(esA3b, f"w2sl{i}", [128, HT * 512], BF16) for i in range(2)]

    p_w2last = [0] * 4
    v_w2ev = {}
    s2_ld = {}
    gtq = [0]
    for n in range(4):
        @P.on("gpsimd")
        def _(e, n=n):
            if n == 0:
                e.wait_ge(mP.h, p_hn[(HT - 1, 1)])
                e.wait_ge(mV.h, v_hev_last)
            if n >= 2:
                e.wait_ge(mP.h, p_w2last[n - 2])
            e.dma_start(out=w2sl[n % 2][:],
                        in_=SW2[n]).then_inc(sW2[n % 2].h, 16)
        s2_ld[n] = sW2[n % 2].bump(16)

        for j in range(NT):
            gi = n * NT + j

            @P.on("tensor")
            def _(e, n=n, j=j, gi=gi, w=s2_ld[n], whev=v_hev_last):
                if gi == 0:
                    e.wait_ge(mV.h, whev)
                    e.wait_ge(mV.h, v_route)
                if j == 0:
                    e.wait_ge(sW2[n % 2].h, w)
                par = gi % 2
                if gi >= 2:
                    e.wait_ge(mV.h, v_w2ev[gi - 2])
                bank = pf[2 + par]
                ins = None
                for h in range(HT):
                    ins = e.matmul(bank[:],
                                   h_T[:, h * OWN + j * 128: h * OWN + (j + 1) * 128],
                                   w2sl[n % 2][:, h * 512:(h + 1) * 512],
                                   start=(h == 0), stop=(h == HT - 1))
                ins.then_inc(mP.h, 1)
            p_w2 = mP.bump()
            if j == NT - 1:
                p_w2last[n] = p_w2

            @P.on("vector")
            def _(e, n=n, j=j, gi=gi, w=p_w2):
                e.wait_ge(mP.h, w)
                sl = acc[:, j * D + n * 512: j * D + (n + 1) * 512]
                e.scalar_tensor_tensor(sl, pf[2 + gi % 2][:], 1.0, sl,
                                       ALU.mult, ALU.add)
                e.drain().then_inc(mV.h, 1)
            v_w2ev[gi] = mV.bump()

            for _k in range(2):
                if gtq[0] < 64:
                    emit_cmp(gtq[0])
                    emit_cnt(gtq[0], extra_wait=(v_hev_last if gtq[0] == 0 else 0))
                    gtq[0] += 1
    v_w2ev_last = v_w2ev[4 * NT - 1]

    @P.on("vector")
    def _(e, w=cnt_done[63]):
        e.wait_ge(mP.h, w)
        e.tensor_copy(cnt_sb[0:1, 0:512], pf[0][0:1, 0:512])
        e.tensor_copy(cnt_sb[0:1, 512:1024], pf[1][0:1, 0:512])
        e.drain().then_inc(mV.h, 1)
    v_cnt = mV.bump()

    @P.on("tensor")
    def _(e, w=v_cnt):
        e.wait_ge(mV.h, w)
        ins = None
        for j in range(8):
            ins = e.transpose(pf[2][:, j:j + 1],
                              cnt_sb[0:1, j * 128:(j + 1) * 128],
                              idn_f[0:1, 0:1])
        ins.then_inc(mP.h, 1)
    p_tokT = mP.bump()

    @P.on("vector")
    def _(e, w=p_tokT):
        e.wait_ge(mP.h, w)
        e.tensor_copy(tok_f[:], pf[2][:, 0:8])
        e.drain()
        e.tensor_copy(tok_u[:], tok_f[:])
        e.drain().then_inc(mV.h, 1)
    v_tok = mV.bump()

    esA3b.close()
    esA3.close()

    esB1 = ExitStack()
    nc._esB1 = esB1
    qT = sbx(esB1, "qT", [128, T], BF16)
    kT = sbx(esB1, "kT", [128, T], BF16)
    vT = sbx(esB1, "vT", [128, T], BF16)

    esB2 = ExitStack()
    nc._esB2 = esB2
    wqkv_bf = sbx(esB2, "wqkv_bf", [128, KT * 768], BF16)
    rb = [sbx(esB2, f"rb{i}", [128, 8 * 512], BF16) for i in range(3)]
    h1stg = [sbx(esB2, f"h1stg{i}", [128, 3 * 512], BF16) for i in range(2)]

    @P.on("gpsimd")
    def _(e, w=v_tok):
        e.wait_ge(mV.h, w)
        e.dma_start(out=wqkv_bf[:], in_=WQKV).then_inc(sQKV.h, 16)
    g_wqkv = sQKV.bump(16)

    p_qkv = {}
    v_qev = {}
    h1_st = {}
    rb_ld = {}
    for ch in range(16):
        r, hh = ch // 2, ch % 2
        for half in range(2):
            idx = 2 * ch + half

            @P.on("sync")
            def _(e, ch=ch, half=half, r=r, hh=hh, idx=idx, wv=v_tok):
                if idx == 0:
                    e.wait_ge(cC.h, cc_ag1)
                    e.wait_ge(mV.h, wv)
                if idx >= 3:
                    e.wait_ge(mP.h, p_qkv[(idx - 3) // 2])
                src = ag1_out[r * D + half * 1024: r * D + half * 1024 + 1024,
                              hh * 512: hh * 512 + 512]
                e.dma_start(out=rb[idx % 3][:].rearrange("p (k t2) -> p k t2",
                                                         t2=512),
                            in_=src.rearrange("(k p) t2 -> p k t2",
                                              p=128)).then_inc(sRB[idx % 3].h, 16)
            rb_ld[idx] = sRB[idx % 3].bump(16)

        @P.on("tensor")
        def _(e, ch=ch, wb=rb_ld[2 * ch + 1], wq=g_wqkv, wtk=v_tok,
              slot=(2 * ch + 1) % 3):
            if ch == 0:
                e.wait_ge(sQKV.h, wq)
                e.wait_ge(mV.h, wtk)
            e.wait_ge(sRB[(2 * ch) % 3].h, rb_ld[2 * ch])
            e.wait_ge(sRB[slot].h, rb_ld[2 * ch + 1])
            if ch >= 1:
                e.wait_ge(mV.h, v_qev[ch - 1])
            ins = None
            for m in range(6):
                for k in range(KT):
                    ins = e.matmul(
                        pf[m][:],
                        wqkv_bf[:, k * 768 + m * 128: k * 768 + (m + 1) * 128],
                        rb[(2 * ch + k // 8) % 3][:, (k % 8) * 512:(k % 8 + 1) * 512],
                        start=(k == 0), stop=(k == KT - 1))
            ins.then_inc(mP.h, 1)
        p_qkv[ch] = mP.bump()

        @P.on("vector")
        def _(e, ch=ch, w=p_qkv[ch]):
            e.wait_ge(mP.h, w)
            if ch >= 2:
                e.wait_ge(sH1[ch % 2].h, h1_st[ch - 2])
            cs = slice(ch * 512, (ch + 1) * 512)
            e.tensor_scalar(qT[:, cs], pf[0][:], RSQ, None, ALU.mult)
            e.tensor_copy(kT[:, cs], pf[2][:])
            e.tensor_copy(vT[:, cs], pf[4][:])
            e.tensor_scalar(h1stg[ch % 2][:, 0:512], pf[1][:], RSQ, None, ALU.mult)
            e.tensor_copy(h1stg[ch % 2][:, 512:1024], pf[3][:])
            e.tensor_copy(h1stg[ch % 2][:, 1024:1536], pf[5][:])
            e.drain().then_inc(mV.h, 1)
        v_qev[ch] = mV.bump()

        @P.on("gpsimd")
        def _(e, ch=ch, w=v_qev[ch]):
            e.wait_ge(mV.h, w)
            cs = slice(ch * 512, (ch + 1) * 512)
            e.dma_start(out=qs1[:, cs],
                        in_=h1stg[ch % 2][:, 0:512]).then_inc(sH1[ch % 2].h, 16)
            e.dma_start(out=ks1[:, cs],
                        in_=h1stg[ch % 2][:, 512:1024]).then_inc(sH1[ch % 2].h, 16)
            e.dma_start(out=vs1[:, cs],
                        in_=h1stg[ch % 2][:, 1024:1536]).then_inc(sH1[ch % 2].h, 16)
        h1_st[ch] = sH1[ch % 2].bump(48)

    esB2.close()

    esB3 = ExitStack()
    nc._esB3 = esB3
    avb = sbx(esB3, "avb", [128, S], BF16, side="right")
    aebT = [sbx(esB3, f"aebT{i}", [128, 512], BF16, side="right")
            for i in range(3)]
    rr_sb = [sbx(esB3, f"rr_sb{i}", [1, 512], F32) for i in range(2)]
    rrB_sb = sbx(esB3, "rrB_sb", [128, 512], F32, side="right")

    sc_rr = [0]
    sc_free = [(0, 0), (0, 0), (0, 0)]
    pbt_free = [(None, 0), (None, 0)]
    av_free = [0, 0]
    av_stop = [0, 0]
    v_recip = [0, 0]
    bc_done = [0, 0]
    rrb_free = [0]
    slab_free = [0, 0, 0]
    slab_rr = [0]
    g_av = {}
    h1_re = [0, 0, 0]

    def emit_attention(hid, wstart_dS, wstart_mV):
        for g in range(8):
            @P.on("tensor")
            def _(e, g=g, hid=hid, wfree=pbt_free[g % 2],
                  wsd=wstart_dS, wsv=wstart_mV,
                  wh14=h1_st.get(14, 0), wh15=h1_st.get(15, 0)):
                if g == 0:
                    if wsd:
                        e.wait_ge(sRE.h, wsd)
                    if wsv:
                        e.wait_ge(mV.h, wsv)
                    if wh14:
                        e.wait_ge(sH1[0].h, wh14)
                        e.wait_ge(sH1[1].h, wh15)
                if wfree[1]:
                    e.wait_ge(wfree[0], wfree[1])
                ins = None
                for j in range(8):
                    jj = g * 8 + j
                    ins = e.transpose(pb[g % 2][:, j * 128:(j + 1) * 128],
                                      vT[:, jj * 128:(jj + 1) * 128], idn_bf[:])
                ins.then_inc(mP.h, 1)
            p_vt = mP.bump()

            @P.on("vector")
            def _(e, g=g, w=p_vt):
                e.wait_ge(mP.h, w)
                e.tensor_copy(vT[:, g * 1024:(g + 1) * 1024], pb[g % 2][:])
                e.drain().then_inc(mV.h, 1)
            pbt_free[g % 2] = (mV.h, mV.bump())
        v_vtok = mV.n

        for b in range(2):
            tok0 = b * S
            blkid = 2 * hid + b
            pend = [None]

            def emit_scores_exp(I, j):
                bk = sc_rr[0] % 3
                sc_rr[0] += 1
                masks = []
                for ql in range(4):
                    qb = 4 * I + ql
                    if j > qb:
                        masks.append((ql, negf))
                    elif j == qb:
                        masks.append((ql, triu))

                @P.on("tensor")
                def _(e, I=I, j=j, bk=bk, tok0=tok0, masks=masks,
                      wfree=sc_free[bk], wv=v_vtok):
                    if I == 0 and j == 0:
                        e.wait_ge(mV.h, wv)
                    if wfree[0]:
                        e.wait_ge(mA.h, wfree[0])
                    ins = e.matmul(pf[bk][:, 0:512],
                                   kT[:, tok0 + j * 128: tok0 + (j + 1) * 128],
                                   qT[:, tok0 + 512 * I: tok0 + 512 * (I + 1)],
                                   start=True, stop=(not masks))
                    for mi, (ql, mt) in enumerate(masks):
                        ins = e.matmul(pf[bk][:, ql * 128:(ql + 1) * 128],
                                       mt[:], idn_bf[:], start=False,
                                       stop=(mi == len(masks) - 1))
                    ins.then_inc(mP.h, 1)
                p_sc = mP.bump()

                s = slab_rr[0] % 3
                slab_rr[0] += 1

                @P.on("scalar")
                def _(e, s=s, bk=bk, w=p_sc, wsl=slab_free[s]):
                    e.wait_ge(mP.h, w)
                    if wsl:
                        e.wait_ge(mP.h, wsl)
                    e.activation(aebT[s][:], pf[bk][:, 0:512], AF.Exp,
                                 bias=0.0, scale=1.0).then_inc(mA.h, 1)
                a_e = mA.bump()
                sc_free[bk] = (a_e, 0)
                return s, a_e

            def emit_avrs(I, j, s, a_e):
                ntile = 4 * I + 4

                @P.on("tensor")
                def _(e, I=I, j=j, b=b, s=s, w=a_e, ntile=ntile,
                      wav=av_free[I % 2], wrs=rrb_free[0]):
                    e.wait_ge(mA.h, w)
                    if j == 0:
                        if wav:
                            e.wait_ge(mV.h, wav)
                        if wrs:
                            e.wait_ge(mV.h, wrs)
                    e.matmul(pf[3 + I % 2][:, 0:512],
                             vT[:, (b * 32 + j) * 128:(b * 32 + j + 1) * 128],
                             aebT[s][:], start=(j == 0),
                             stop=(j == ntile - 1)).then_inc(mP.h, 1)
                    e.matmul(pf[5][0:1, 0:512], ones_bf[:],
                             aebT[s][:], start=(j == 0),
                             stop=(j == ntile - 1)).then_inc(mP.h, 1)
                v_av = mP.bump()
                v_rs = mP.bump()
                slab_free[s] = v_rs
                if j == ntile - 1:
                    av_stop[I % 2] = v_av
                return v_rs

            def emit_recip(I, w_rs):
                @P.on("vector")
                def _(e, I=I, w=w_rs, wbc=bc_done[I % 2]):
                    e.wait_ge(mP.h, w)
                    if wbc:
                        e.wait_ge(mP.h, wbc)
                    e.reciprocal(rr_sb[I % 2][0:1, 0:512],
                                 pf[5][0:1, 0:512])
                    e.drain().then_inc(mV.h, 1)
                v_recip[I % 2] = mV.bump()

            def emit_norm_tail(I):
                @P.on("tensor")
                def _(e, I=I, w=v_recip[I % 2]):
                    e.wait_ge(mV.h, w)
                    e.matmul(pf[5][0:128, 0:512], ones_r[:],
                             rr_sb[I % 2][0:1, 0:512],
                             start=True, stop=True).then_inc(mP.h, 1)
                bc = mP.bump()
                bc_done[I % 2] = bc

                @P.on("vector")
                def _(e, I=I, blkid=blkid, w=bc, wav=av_stop[I % 2],
                      wdma=(g_av.get(blkid - 1, 0) if I == 0 else 0)):
                    e.wait_ge(mP.h, w)
                    e.tensor_copy(rrB_sb[:], pf[5][0:128, 0:512])
                    e.drain().then_inc(mV.h, 1)
                    e.wait_ge(mP.h, wav)
                    if wdma:
                        e.wait_ge(sAV.h, wdma)
                    ins = e.tensor_tensor(avb[:, 512 * I:512 * (I + 1)],
                                          pf[3 + I % 2][:, 0:512], rrB_sb[:],
                                          ALU.mult)
                    if I == 7:
                        e.drain().then_inc(mV.h, 1)
                    else:
                        ins.then_inc(mV.h, 1)
                rrb_free[0] = mV.bump()
                av_free[I % 2] = mV.bump()

            for I in range(8):
                ntile = 4 * I + 4
                prev = None
                w_rs = 0
                for j in range(ntile):
                    se = emit_scores_exp(I, j)
                    if j == 1 and pend[0] is not None:
                        emit_norm_tail(pend[0])
                        pend[0] = None
                    if prev is not None:
                        w_rs = emit_avrs(I, j - 1, *prev)
                    prev = se
                w_rs = emit_avrs(I, ntile - 1, *prev)
                emit_recip(I, w_rs)
                pend[0] = I
            emit_norm_tail(pend[0])
            v_bh = mV.n

            @P.on("gpsimd")
            def _(e, hid=hid, b=b, w=v_bh):
                e.wait_ge(mV.h, w)
                e.dma_start(out=ag5_in[hid * 128:(hid + 1) * 128,
                                       b * S:(b + 1) * S],
                            in_=avb[:]).then_inc(sAV.h, 16)
            g_av[blkid] = sAV.bump(16)
        return mP.n

    p_head0 = emit_attention(0, 0, v_qev[15])

    for iq, (dst, src) in enumerate(((qT, qs1), (kT, ks1), (vT, vs1))):
        @P.on("sync")
        def _(e, dst=dst, src=src, iq=iq, w=p_head0, wh14=h1_st[14],
              wh15=h1_st[15]):
            if iq == 0:
                e.wait_ge(mP.h, w)
                e.wait_ge(sH1[0].h, wh14)
                e.wait_ge(sH1[1].h, wh15)
            e.dma_start(out=dst[:], in_=src).then_inc(sRE.h, 16)
        h1_re[iq] = sRE.bump(16)

    p_head1 = emit_attention(1, h1_re[2], 0)
    v_att_end = mV.n

    @P.on("gpsimd")
    def _(e, w=g_av[3]):
        e.wait_ge(sAV.h, w)
        e.collective_compute("AllGather", ALU.bypass, RG, [ag5_in],
                             [ag5_out]).then_inc(cC.h)
    cc_ag5 = cC.bump()

    if dbg:
        @P.on("gpsimd")
        def _(e, w=v_qev[15]):
            e.wait_ge(mV.h, w)
            e.dma_start(out=DBG["d_qT"], in_=qT[:]).then_inc(dG.h, 16)
            e.dma_start(out=DBG["d_kT"], in_=kT[:]).then_inc(dG.h, 16)
        dG.bump(32)

        @P.on("gpsimd")
        def _(e, w=g_av[0]):
            e.wait_ge(dG.h, w)
            e.dma_start(out=DBG["d_vtok"], in_=vT[:]).then_inc(dG.h, 16)
            e.dma_start(out=DBG["d_av"], in_=avb[:]).then_inc(dG.h, 16)
        dG.bump(32)

    esB3.close()
    esB1.close()

    esC1 = ExitStack()
    nc._esC1 = esC1
    h_T2 = sbx(esC1, "h_T2", [128, HT * CAP], BF16, side="right")
    esC1x = ExitStack()
    nc._esC1x = esC1x
    xd_T = sbx(esC1x, "xd_T", [128, KT * CAP], BF16)

    esC2 = ExitStack()
    nc._esC2 = esC2
    xdt = [sbx(esC2, f"xdt{i}", [128, D], BF16) for i in range(2)]

    g_xd = {}
    p_xdt = {}
    v_xdev = {}
    for j in range(NT):
        @P.on("gpsimd")
        def _(e, j=j, wtok=v_tok, wag2=cc_ag2, whd=p_head1):
            if j == 0:
                e.wait_ge(mV.h, wtok)
                e.wait_ge(cC.h, wag2)
                e.wait_ge(mP.h, whd)
            if j >= 2:
                e.wait_ge(mP.h, p_xdt[j - 2])
            e.memset(xdt[j % 2][:], 0.0)
            e.indirect_dma_start(
                xdt[j % 2][:], None, ag2_out,
                IndirectOffsetOnAxis(ap=tok_u[:, j:j + 1], axis=0),
                bounds_check=T - 1, oob_is_err=False).then_inc(sXD[j % 2].h, 16)
        g_xd[j] = sXD[j % 2].bump(16)

        if dbg and j == 0:
            @P.on("gpsimd")
            def _(e, w=g_xd[0], tgt=dG.n + 32):
                e.wait_ge(dG.h, w)
                e.dma_start(out=DBG["d_xd0"], in_=xdt[0][:]).then_inc(dG.h, 16)
                e.dma_start(out=DBG["d_ag2"], in_=ag2_out[0:1024, :]).then_inc(dG.h, 16)
                e.wait_ge(dG.h, tgt)
            dG.bump(32)

        for g in range(2):
            @P.on("tensor")
            def _(e, j=j, g=g, w=g_xd[j], wfree=pbt_free[(2 * j + g) % 2]):
                if g == 0:
                    e.wait_ge(sXD[j % 2].h, w)
                if wfree[1]:
                    e.wait_ge(wfree[0], wfree[1])
                ins = None
                for kk in range(8):
                    k = g * 8 + kk
                    ins = e.transpose(
                        pb[(2 * j + g) % 2][:, kk * 128:(kk + 1) * 128],
                        xdt[j % 2][:, k * 128:(k + 1) * 128], idn_bf[:])
                ins.then_inc(mP.h, 1)
            pw = mP.bump()

            @P.on("vector")
            def _(e, j=j, g=g, w=pw):
                e.wait_ge(mP.h, w)
                xv = xd_T[:].rearrange("p (k t2) -> p k t2", t2=CAP)
                e.tensor_copy(xv[:, g * 8:(g + 1) * 8, j * 128:(j + 1) * 128],
                              pb[(2 * j + g) % 2][:]
                              .rearrange("p (kk t2) -> p kk t2", t2=128))
                e.drain().then_inc(mV.h, 1)
            pbt_free[(2 * j + g) % 2] = (mV.h, mV.bump())
        p_xdt[j] = mP.n
    v_xdT = mV.n

    esC2.close()

    esC3 = ExitStack()
    nc._esC3 = esC3
    wslC = [sbx(esC3, f"wslC{i}", [128, D], BF16) for i in range(4)]
    hsilC = [sbx(esC3, f"hsilC{i}", [128, 512], BF16) for i in range(2)]

    p_hn2 = {}
    v_hev2 = {}
    for mH in range(HT):
        @P.on("gpsimd")
        def _(e, mH=mH, whd=p_head1):
            if mH == 0:
                e.wait_ge(mP.h, whd)
            if mH <= 1:
                e.wait_ge(mP.h, p_xdt[NT - 1])
            if mH >= 2:
                e.wait_ge(mP.h, p_hn2[(mH - 2, 1)])
            e.dma_start(out=wslC[mH % 2][:],
                        in_=W1[mH]).then_inc(sW13[mH % 2].h, 16)
            e.dma_start(out=wslC[2 + mH % 2][:],
                        in_=W3[mH]).then_inc(sW13[mH % 2].h, 16)
        w13_ld = sW13[mH % 2].bump(32)

        for n in range(2):
            idx = 2 * mH + n

            @P.on("tensor")
            def _(e, mH=mH, n=n, idx=idx, w=w13_ld, wxd=v_xdT, wat=v_att_end):
                if n == 0:
                    e.wait_ge(sW13[mH % 2].h, w)
                if idx == 0:
                    e.wait_ge(mV.h, wxd)
                if idx <= 1:
                    e.wait_ge(mV.h, wat)
                par = idx % 2
                if idx >= 2:
                    e.wait_ge(mV.h, v_hev2[idx - 2])
                b1, b3 = (pf[0], pf[1]) if par == 0 else (pf[4], pf[5])
                for k in range(KT):
                    e.matmul(b1[:], wslC[mH % 2][:, k * 128:(k + 1) * 128],
                             xd_T[:, k * CAP + n * 512: k * CAP + n * 512 + 512],
                             start=(k == 0), stop=(k == KT - 1))
                ins = None
                for k in range(KT):
                    ins = e.matmul(b3[:], wslC[2 + mH % 2][:, k * 128:(k + 1) * 128],
                                   xd_T[:, k * CAP + n * 512: k * CAP + n * 512 + 512],
                                   start=(k == 0), stop=(k == KT - 1))
                ins.then_inc(mP.h, 1)
            p_hn2[(mH, n)] = mP.bump()

            @P.on("scalar")
            def _(e, idx=idx, w=p_hn2[(mH, n)]):
                e.wait_ge(mP.h, w)
                e.activation(hsilC[idx % 2][:], pf[0 if idx % 2 == 0 else 4][:],
                             AF.Silu, bias=0.0, scale=1.0).then_inc(mA.h, 1)
            a_sil2 = mA.bump()

            @P.on("vector")
            def _(e, mH=mH, n=n, idx=idx, wa=a_sil2):
                e.wait_ge(mA.h, wa)
                e.tensor_tensor(
                    h_T2[:, mH * CAP + n * 512: mH * CAP + n * 512 + 512],
                    hsilC[idx % 2][:], pf[1 if idx % 2 == 0 else 5][:],
                    ALU.mult)
                e.drain().then_inc(mV.h, 1)
            v_hev2[idx] = mV.bump()
    v_hev2_last = v_hev2[2 * HT - 1]
    a_sil2_last = mA.n

    if dbg:
        @P.on("gpsimd")
        def _(e, w=v_xdT, w2=v_hev2_last, tgt=dG.n + 16):
            e.wait_ge(mV.h, w)
            e.dma_start(out=DBG["d_xdT"], in_=xd_T[:]).then_inc(dG.h, 16)
            e.wait_ge(dG.h, tgt)
            e.wait_ge(mV.h, w2)
            e.dma_start(out=DBG["d_hT"], in_=h_T2[:]).then_inc(dG.h, 16)
        dG.bump(32)
    esC3.close()
    esC1x.close()

    esC4 = ExitStack()
    nc._esC4 = esC4
    w2slC = [sbx(esC4, f"w2slC{i}", [128, HT * 512], BF16) for i in range(2)]
    oe_st = [sbx(esC4, f"oest{i}", [128, 512], BF16) for i in range(2)]

    p_w2last2 = [0] * 4
    v_oeev = {}
    g_oest = {}
    for n in range(4):
        @P.on("gpsimd")
        def _(e, n=n, whe=v_hev2_last):
            if n == 0:
                e.wait_ge(mP.h, p_hn2[(HT - 1, 1)])
                e.wait_ge(mV.h, whe)
            if n >= 2:
                e.wait_ge(mP.h, p_w2last2[n - 2])
            e.dma_start(out=w2slC[n % 2][:],
                        in_=W2[n]).then_inc(sW2b[n % 2].h, 16)
        w2_ld = sW2b[n % 2].bump(16)

        for j in range(NT):
            gi = n * NT + j

            @P.on("tensor")
            def _(e, n=n, j=j, gi=gi, w=w2_ld, whev=v_hev2_last):
                if gi == 0:
                    e.wait_ge(mV.h, whev)
                if j == 0:
                    e.wait_ge(sW2b[n % 2].h, w)
                par = gi % 2
                if gi >= 2:
                    e.wait_ge(mV.h, v_oeev[gi - 2])
                bank = pf[2 + par]
                ins = None
                for h in range(HT):
                    ins = e.matmul(bank[:],
                                   h_T2[:, h * CAP + j * 128: h * CAP + (j + 1) * 128],
                                   w2slC[n % 2][:, h * 512:(h + 1) * 512],
                                   start=(h == 0), stop=(h == HT - 1))
                ins.then_inc(mP.h, 1)
            p_oe = mP.bump()
            if j == NT - 1:
                p_w2last2[n] = p_oe

            @P.on("vector")
            def _(e, n=n, j=j, gi=gi, w=p_oe):
                e.wait_ge(mP.h, w)
                if gi >= 2:
                    e.wait_ge(sOE[gi % 2].h, g_oest[gi - 2])
                e.tensor_copy(oe_st[gi % 2][:], pf[2 + gi % 2][:])
                e.drain().then_inc(mV.h, 1)
            v_oe = mV.bump()
            v_oeev[gi] = v_oe

            @P.on("gpsimd")
            def _(e, n=n, j=j, gi=gi, w=v_oe):
                e.wait_ge(mV.h, w)
                e.dma_start(out=ag4_in[j * 128:(j + 1) * 128,
                                       n * 512:(n + 1) * 512],
                            in_=oe_st[gi % 2][:]).then_inc(sOE[gi % 2].h, 16)
            g_oest[gi] = sOE[gi % 2].bump(16)

    @P.on("gpsimd")
    def _(e, w14=g_oest[4 * NT - 2], w15=g_oest[4 * NT - 1]):
        e.wait_ge(sOE[0].h, w14)
        e.wait_ge(sOE[1].h, w15)
        e.collective_compute("AllGather", ALU.bypass, RG, [ag4_in],
                             [ag4_out]).then_inc(cC.h)
    cc_ag4 = cC.bump()


    esC4.close()
    esC1.close()

    esD = ExitStack()
    nc._esD = esD
    att_own = sbx(esD, "att_own", [128, KT * OWN], BF16)
    woutb = [sbx(esD, f"woutb{i}", [128, KT * 512], BF16) for i in range(2)]
    gtl = [sbx(esD, f"gtl{i}", [128, D], BF16) for i in range(2)]

    g_att = {}
    for k in range(KT):
        @P.on("gpsimd")
        def _(e, k=k, w=cc_ag5, wr=v_route, wc=p_w2last2[3]):
            if k == 0:
                e.wait_ge(cC.h, w)
                e.wait_ge(mV.h, wr)
                e.wait_ge(mP.h, wc)
            e.indirect_dma_start(
                att_own[:, k * OWN:(k + 1) * OWN], None, ag5_out,
                IndirectOffsetOnAxis(ap=att_ix[:, k:k + 1], axis=1),
                bounds_check=NCORES * 256 * T - 1,
                oob_is_err=False).then_inc(sATT.h, 16)
    g_att = sATT.bump(16 * KT)

    p_wolast = [0] * 4
    v_woev = {}
    for n in range(4):
        @P.on("gpsimd")
        def _(e, n=n):
            if n >= 2:
                e.wait_ge(mP.h, p_wolast[n - 2])
            e.dma_start(out=woutb[n % 2][:],
                        in_=WOUT[n]).then_inc(sWO[n % 2].h, 16)
        wo_ld = sWO[n % 2].bump(16)

        for j in range(NT):
            gi = n * NT + j

            @P.on("tensor")
            def _(e, n=n, j=j, gi=gi, w=wo_ld, wgat=g_att,
                  wsl2=a_sil2_last):
                if gi == 0:
                    e.wait_ge(sATT.h, wgat)
                    e.wait_ge(mA.h, wsl2)
                if j == 0:
                    e.wait_ge(sWO[n % 2].h, w)
                par = gi % 2
                if gi >= 2:
                    e.wait_ge(mV.h, v_woev[gi - 2])
                bank = pf[par]
                ins = None
                for k in range(KT):
                    ins = e.matmul(bank[:],
                                   att_own[:, k * OWN + j * 128:
                                           k * OWN + (j + 1) * 128],
                                   woutb[n % 2][:, k * 512:(k + 1) * 512],
                                   start=(k == 0), stop=(k == KT - 1))
                ins.then_inc(mP.h, 1)
            p_wo = mP.bump()
            if j == NT - 1:
                p_wolast[n] = p_wo

            @P.on("vector")
            def _(e, n=n, j=j, gi=gi, w=p_wo):
                e.wait_ge(mP.h, w)
                sl = acc[:, j * D + n * 512: j * D + (n + 1) * 512]
                e.scalar_tensor_tensor(sl, pf[gi % 2][:], 1.0, sl,
                                       ALU.mult, ALU.add)
                e.drain().then_inc(mV.h, 1)
            v_woev[gi] = mV.bump()
    v_wo_last = v_woev[4 * NT - 1]

    v_comb = {}
    for j in range(NT):
        @P.on("gpsimd")
        def _(e, j=j, w=cc_ag4):
            if j == 0:
                e.wait_ge(cC.h, w)
            if j < 2:
                e.memset(gtl[j % 2][:], 0.0)
            if j >= 2:
                e.wait_ge(mV.h, v_comb[j - 2])
            e.indirect_dma_start(
                gtl[j % 2][:], None, ag4_out,
                IndirectOffsetOnAxis(ap=comb_u[:, j:j + 1], axis=0),
                bounds_check=E * CAP - 1, oob_is_err=False).then_inc(sCB[j % 2].h, 16)
        g_cb = sCB[j % 2].bump(16)

        @P.on("vector")
        def _(e, j=j, w=g_cb):
            e.wait_ge(sCB[j % 2].h, w)
            sl = acc[:, j * D:(j + 1) * D]
            e.scalar_tensor_tensor(sl, gtl[j % 2][:], gk[:, j:j + 1], sl,
                                   ALU.mult, ALU.add)
            e.drain().then_inc(mV.h, 1)
        v_comb[j] = mV.bump()

    @P.on("gpsimd")
    def _(e, wv=v_comb[NT - 1], ww=v_wo_last, tgt=dG.n + 16):
        e.wait_ge(mV.h, wv)
        e.wait_ge(mV.h, ww)
        e.dma_start(out=Y.rearrange("(i p) d -> p i d", p=128),
                    in_=acc[:].rearrange("p (i d) -> p i d", d=D)).then_inc(dG.h, 16)
        e.wait_ge(dG.h, tgt)
    dG.bump(16)

    if dbg:
        @P.on("gpsimd")
        def _(e, tgt=dG.n + 16 * 9):
            for nm, src in (("d_lown", l_own), ("d_posrm", pos_rm),
                            ("d_tok", tok_u), ("d_comb", comb_u), ("d_gk", gk),
                            ("d_eidx", eidx), ("d_posv", posv),
                            ("d_eexp", eexp), ("d_ftok", f_tok)):
                e.dma_start(out=DBG[nm], in_=src[:]).then_inc(dG.h, 16)
            e.wait_ge(dG.h, tgt)
        dG.bump(16 * 9)

    esD.close()

    with nc.Block() as block:
        @block.sync
        def _(eng):
            for fn in P.items["sync"]:
                fn(eng)

        @block.tensor
        def _(eng):
            for fn in P.items["tensor"]:
                fn(eng)

        @block.vector
        def _(eng):
            for fn in P.items["vector"]:
                fn(eng)

        @block.scalar
        def _(eng):
            for fn in P.items["scalar"]:
                fn(eng)

        @block.gpsimd
        def _(eng):
            for fn in P.items["gpsimd"]:
                fn(eng)

    return nc


def _host_inputs(L0_attn_x, L0_attn_w_qkv, L0_attn_w_out, L0_ffn_Wg, L0_ffn_W1,
                 L0_ffn_W3, L0_ffn_W2, L0_ffn_sw1, L0_ffn_sw3, L0_ffn_sw2):
    f32 = np.float32
    x = np.ascontiguousarray(np.asarray(L0_attn_x, f32).reshape(T, D))
    wqkv = np.asarray(L0_attn_w_qkv, f32)
    idn = np.eye(128, dtype=f32)
    trim = np.triu(np.full((128, 128), -1e9, dtype=f32), 1)
    sltb = np.zeros((64, 64), dtype=f32)
    for rp in range(8):
        for ep in range(8):
            for r2 in range(rp):
                sltb[r2 * 8 + ep, rp * 8 + ep] = 1.0

    def kslice(w, c0, w_=128):
        return np.ascontiguousarray(
            w[:, c0:c0 + w_].reshape(KT, 128, w_).transpose(1, 0, 2)
            .reshape(128, KT * w_))

    def lhs_tiles(w):
        return np.ascontiguousarray(
            w.reshape(KT, 128, HT, 128).transpose(2, 1, 0, 3)
            .reshape(HT, 128, KT * 128))

    def rhs_chunks(w, kt):
        return np.ascontiguousarray(
            w.reshape(kt, 128, 4, 512).transpose(2, 1, 0, 3)
            .reshape(4, 128, kt * 512))

    wg_pre = np.ascontiguousarray(
        np.asarray(L0_ffn_Wg, f32).reshape(KT, 128, E).transpose(1, 0, 2)
        .reshape(128, KT * E))
    sw1p = lhs_tiles(np.asarray(L0_ffn_sw1, f32))
    sw3p = lhs_tiles(np.asarray(L0_ffn_sw3, f32))
    sw2p = rhs_chunks(np.asarray(L0_ffn_sw2, f32), HT)
    woutp = rhs_chunks(np.asarray(L0_attn_w_out, f32), KT)
    maps = []
    for c in range(NCORES):
        esel = np.zeros((128, 512), dtype=f32)
        rsel = np.zeros((128, 512), dtype=f32)
        for tl in range(8):
            for r in range(8):
                esel[:, tl * 64 + r * 8 + c] = 1.0
            for ee in range(8):
                rsel[:, tl * 64 + c * 8 + ee] = 1.0
        wq = np.concatenate([kslice(wqkv, 256 * c, 256),
                             kslice(wqkv, D + 256 * c, 256),
                             kslice(wqkv, 2 * D + 256 * c, 256)], axis=1)
        qb = wq[:, 0:KT * 256].reshape(128, KT, 256)
        kb = wq[:, KT * 256:2 * KT * 256].reshape(128, KT, 256)
        vb = wq[:, 2 * KT * 256:].reshape(128, KT, 256)
        wqp = np.ascontiguousarray(
            np.concatenate([qb, kb, vb], axis=2).reshape(128, KT * 768))
        maps.append({
            "x_own": np.ascontiguousarray(x[c * OWN:(c + 1) * OWN]),
            "wqkv_sl": wqp,
            "w_out": woutp,
            "wg": wg_pre,
            "w1_e": lhs_tiles(np.asarray(L0_ffn_W1[c], f32)),
            "w3_e": lhs_tiles(np.asarray(L0_ffn_W3[c], f32)),
            "w2_e": rhs_chunks(np.asarray(L0_ffn_W2[c], f32), HT),
            "sw1": sw1p,
            "sw3": sw3p,
            "sw2": sw2p,
            "idn_bf": np.ascontiguousarray(idn.astype(BF)),
            "idn_f32": idn,
            "tri": np.ascontiguousarray(trim.T.astype(BF)),
            "triu": np.ascontiguousarray(trim.astype(BF)),
            "esel_rep": esel,
            "rsel_rep": rsel,
            "tbase": np.full((128, 1), 1024.0 * c, dtype=f32),
            "sltb": sltb,
        })
    return maps


def kernel(**inputs):
    maps = _host_inputs(**inputs)
    nc = build_nc()
    res = run_bass_kernel_spmd(nc, maps, core_ids=list(range(NCORES)))
    out = np.concatenate([np.asarray(r["y"]) for r in res.results], axis=0)
    return out.reshape(2, S, D).astype(np.float32)


# version 13 (final)
# speedup vs baseline: 1.0995x; 1.0275x over previous
import numpy as np
import ml_dtypes
import concourse.bass as bass
import concourse.mybir as mybir
from concourse.bass_utils import run_bass_kernel_spmd
from concourse.bass import IndirectOffsetOnAxis
from contextlib import ExitStack

F32 = mybir.dt.float32
BF16 = mybir.dt.bfloat16
U32 = mybir.dt.uint32
AF = mybir.ActivationFunctionType
ALU = mybir.AluOpType
AX = mybir.AxisListType
BF = ml_dtypes.bfloat16

NCORES = 8
D = 2048
T = 8192
OWN = 1024
NT = 8
KT = 16
HF = 2816
HT = 22
HD = 128
S = 4096
E = 8
CAP = 1024
QB = 32
RG = [list(range(NCORES))]
RSQ = float(1.0 / np.sqrt(np.float32(HD)))
ATT_PIPE = True


class Sem:
    def __init__(self, nc, es, name):
        self.h = es.enter_context(nc.semaphore(name))
        self.n = 0

    def bump(self, step=1):
        self.n += step
        return self.n


class Plan:
    ENGS = ("sync", "tensor", "vector", "scalar", "gpsimd")

    def __init__(self):
        self.items = {e: [] for e in self.ENGS}

    def on(self, eng):
        def deco(fn):
            self.items[eng].append(fn)
        return deco


def build_nc(dbg=False):
    nc = bass.Bass("TRN2", target_bir_lowering=False, debug=False)

    X = nc.dram_tensor("x_own", [OWN, D], F32, kind="ExternalInput").ap()
    WQKV = nc.dram_tensor("wqkv_sl", [128, KT * 768], F32, kind="ExternalInput").ap()
    WOUT = nc.dram_tensor("w_out", [4, 128, KT * 512], F32, kind="ExternalInput").ap()
    WG = nc.dram_tensor("wg", [128, KT * E], F32, kind="ExternalInput").ap()
    W1 = nc.dram_tensor("w1_e", [HT, 128, KT * 128], F32, kind="ExternalInput").ap()
    W3 = nc.dram_tensor("w3_e", [HT, 128, KT * 128], F32, kind="ExternalInput").ap()
    W2 = nc.dram_tensor("w2_e", [4, 128, HT * 512], F32, kind="ExternalInput").ap()
    SW1 = nc.dram_tensor("sw1", [HT, 128, KT * 128], F32, kind="ExternalInput").ap()
    SW3 = nc.dram_tensor("sw3", [HT, 128, KT * 128], F32, kind="ExternalInput").ap()
    SW2 = nc.dram_tensor("sw2", [4, 128, HT * 512], F32, kind="ExternalInput").ap()
    IDNB = nc.dram_tensor("idn_bf", [128, 128], BF16, kind="ExternalInput").ap()
    IDNF = nc.dram_tensor("idn_f32", [128, 128], F32, kind="ExternalInput").ap()
    TRI = nc.dram_tensor("tri", [128, 128], BF16, kind="ExternalInput").ap()
    TRIU = nc.dram_tensor("triu", [128, 128], BF16, kind="ExternalInput").ap()
    ESEL = nc.dram_tensor("esel_rep", [128, 512], F32, kind="ExternalInput").ap()
    RSEL = nc.dram_tensor("rsel_rep", [128, 512], F32, kind="ExternalInput").ap()
    TBASE = nc.dram_tensor("tbase", [128, 1], F32, kind="ExternalInput").ap()
    SLTB = nc.dram_tensor("sltb", [64, 64], F32, kind="ExternalInput").ap()
    Y = nc.dram_tensor("y", [OWN, D], F32, kind="ExternalOutput").ap()

    DBG = {}
    if dbg:
        for nm, shp, dt in [("d_lown", [128, 64], F32), ("d_posrm", [64, 1024], F32),
                            ("d_tok", [128, 8], U32), ("d_comb", [128, 8], U32),
                            ("d_gk", [128, 8], F32), ("d_eidx", [128, 8], F32),
                            ("d_posv", [128, 8], F32), ("d_ftok", [128, 64], F32),
                            ("d_xnT", [128, KT * OWN], BF16),
                            ("d_acc", [128, NT * D], F32),
                            ("d_eexp", [128, 64], F32),
                            ("d_hT", [128, HT * OWN], BF16),
                            ("d_qT", [128, T], BF16), ("d_kT", [128, T], BF16),
                            ("d_vtok", [128, T], BF16), ("d_av", [128, S], BF16),
                            ("d_xdT", [128, KT * CAP], BF16),
                            ("d_ag2", [1024, D], BF16),
                            ("d_xd0", [128, D], BF16),
                            ("d_oec", [128, 8 * 128], BF16)]:
            DBG[nm] = nc.dram_tensor(nm, shp, dt, kind="ExternalOutput").ap()

    ag1_in = nc.dram_tensor("ag1_in", [D, OWN], BF16).ap()
    ag1_out = nc.dram_tensor("ag1_out", [NCORES * D, OWN], BF16, addr_space="Shared").ap()
    ag2_in = nc.dram_tensor("ag2_in", [OWN, D], BF16).ap()
    ag2_out = nc.dram_tensor("ag2_out", [T, D], BF16, addr_space="Shared").ap()
    ag3_in = nc.dram_tensor("ag3_in", [E, OWN], F32).ap()
    ag3_out = nc.dram_tensor("ag3_out", [NCORES * E, OWN], F32, addr_space="Shared").ap()
    ag4_in = nc.dram_tensor("ag4_in", [CAP, D], BF16).ap()
    ag4_out = nc.dram_tensor("ag4_out", [E * CAP, D], BF16, addr_space="Shared").ap()
    ag5_in = nc.dram_tensor("ag5_in", [256, T], BF16).ap()
    ag5_out = nc.dram_tensor("ag5_out", [NCORES * 256, T], BF16, addr_space="Shared").ap()
    qs1 = nc.dram_tensor("qs1", [128, T], BF16).ap()
    ks1 = nc.dram_tensor("ks1", [128, T], BF16).ap()
    vs1 = nc.dram_tensor("vs1", [128, T], BF16).ap()

    P = Plan()
    es = ExitStack()
    nc._kernel_es = es

    dS = Sem(nc, es, "dS")
    dG = Sem(nc, es, "dG")
    dA = Sem(nc, es, "dA")
    cC = Sem(nc, es, "cC")
    mP = Sem(nc, es, "mP")
    mV = Sem(nc, es, "mV")
    mA = Sem(nc, es, "mA")
    mG = Sem(nc, es, "mG")
    sPC = Sem(nc, es, "sPC")
    sW = [Sem(nc, es, f"sW{i}") for i in range(2)]
    sW2 = [Sem(nc, es, f"sW2{i}") for i in range(2)]
    sRB = [Sem(nc, es, f"sRB{i}") for i in range(3)]
    sRE = Sem(nc, es, "sRE")
    sAG1 = Sem(nc, es, "sAG1")
    sAG2 = Sem(nc, es, "sAG2")
    sAG3 = Sem(nc, es, "sAG3")
    sRM = Sem(nc, es, "sRM")
    sQKV = Sem(nc, es, "sQKV")
    sH1 = [Sem(nc, es, f"sH1{i}") for i in range(2)]
    sAV = Sem(nc, es, "sAV")
    sXD = [Sem(nc, es, f"sXD{i}") for i in range(2)]
    sW13 = [Sem(nc, es, f"sW13{i}") for i in range(2)]
    sW2b = [Sem(nc, es, f"sW2b{i}") for i in range(2)]
    sOE = [Sem(nc, es, f"sOE{i}") for i in range(2)]
    sATT = Sem(nc, es, "sATT")
    sWO = [Sem(nc, es, f"sWO{i}") for i in range(2)]
    sCB = [Sem(nc, es, f"sCB{i}") for i in range(2)]

    def sbx(stack, name, shape, dt, side=None):
        return stack.enter_context(
            nc.sbuf_tensor(name, list(shape), dt, side=side))

    acc = sbx(es, "acc", [128, NT * D], F32)
    idn_bf = sbx(es, "idn_bf_s", [128, 128], BF16)
    idn_f = sbx(es, "idn_f_s", [128, 128], F32)
    tri = sbx(es, "tri_s", [128, 128], BF16)
    triu = sbx(es, "triu_s", [128, 128], BF16)
    negf = sbx(es, "negf_s", [128, 128], BF16)
    ones_r = sbx(es, "ones_r", [1, 128], F32)
    ones_bf = sbx(es, "ones_bf", [128, 1], BF16)
    esel = sbx(es, "esel_s", [128, 512], F32)
    rsel = sbx(es, "rsel_s", [128, 512], F32)
    tbase = sbx(es, "tbase_s", [128, 1], F32)
    sltb = sbx(es, "sltb_s", [64, 64], F32)
    wg_f = sbx(es, "wg_f", [128, KT * E], F32)
    wg_hi = sbx(es, "wg_hi", [128, KT * E], BF16)
    wg_lo = sbx(es, "wg_lo", [128, KT * E], BF16)
    iotaE = sbx(es, "iotaE", [128, 64], F32)
    iotaP = sbx(es, "iotaP", [128, 1], F32)
    ones_c = sbx(es, "ones_c", [128, 1], BF16)
    l_own = sbx(es, "l_own", [128, 64], F32)
    lmax = sbx(es, "lmax_s", [128, 8], F32)
    eexp = sbx(es, "eexp", [128, 64], F32)
    sume = sbx(es, "sume", [128, 8], F32)
    gate = sbx(es, "gate_s", [128, 8], F32)
    oh = sbx(es, "oh_s", [128, 64], F32)
    pr1 = sbx(es, "pr1", [128, 64], F32)
    pr2 = sbx(es, "pr2", [128, 64], F32)
    ohx = sbx(es, "ohx_s", [128, 64], F32)
    eidx = sbx(es, "eidx_s", [128, 8], F32)
    ohxT = sbx(es, "ohxT", [8, OWN], F32)
    ohx_rm = sbx(es, "ohx_rm", [64, 1024], F32)
    ones64 = sbx(es, "ones64", [64, 1024], F32)
    pos_rm = sbx(es, "pos_rm", [64, 1024], F32)
    tot64 = sbx(es, "tot64", [64, 1], F32)
    off64 = sbx(es, "off64", [64, 1], F32)
    pos_tok = sbx(es, "pos_tok", [128, 512], F32)
    mscr = sbx(es, "mscr", [128, 512], F32)
    f_tok = sbx(es, "f_tok", [128, 64], F32)
    pos_own = sbx(es, "pos_own", [128, 64], F32)
    posv = sbx(es, "posv_s", [128, 8], F32)
    keep = sbx(es, "keep_s", [128, 8], F32)
    gk = sbx(es, "gk_s", [128, 8], F32)
    comb_f = sbx(es, "comb_f", [128, 8], F32)
    comb_u = sbx(es, "comb_u", [128, 8], U32)
    cnt_sb = sbx(es, "cnt_sb", [1, 1024], F32)
    tok_f = sbx(es, "tok_f", [128, 8], F32)
    tok_u = sbx(es, "tok_u", [128, 8], U32)
    att_if = sbx(es, "att_if", [128, 16], F32)
    att_ix = sbx(es, "att_ix", [128, 16], U32)
    nm_run = sbx(es, "nm_run", [128, 1], F32)
    nm_tmp = sbx(es, "nm_tmp", [128, 1], F32)
    nmc = sbx(es, "nmc", [128, 8], F32)
    rs_s = sbx(es, "rs_s", [128, 8], F32)
    rsum = sbx(es, "rsum", [128, 1], F32)
    rrec = sbx(es, "rrec", [128, 1], F32)
    negmean = sbx(es, "negmean", [128, 1], F32)
    sumsq = sbx(es, "sumsq", [128, 1], F32)
    varb = sbx(es, "varb", [128, 1], F32)
    lnv = sbx(es, "lnv", [128, 1], F32)
    rstd = sbx(es, "rstd", [128, 1], F32)
    m2s = sbx(es, "m2s", [128, 1], F32)
    eps_c = sbx(es, "eps_c", [128, 1], F32)
    zero_c = sbx(es, "zero_c", [128, 1], F32)

    pf = [es.enter_context(nc.psum_tensor(f"pf{i}", [128, 512], F32))
          for i in range(6)]
    pb = [es.enter_context(nc.psum_tensor(f"pb{i}", [128, 1024], BF16))
          for i in range(2)]

    @P.on("sync")
    def _(e):
        e.dma_start(out=acc[:].rearrange("p (i d) -> p i d", d=D),
                    in_=X.rearrange("(i p) d -> p i d", p=128)).then_inc(dS.h, 16)
    ld_x = dS.bump(16)

    @P.on("scalar")
    def _(e):
        for dst, src in ((idn_bf, IDNB), (idn_f, IDNF), (tri, TRI),
                         (triu, TRIU), (esel, ESEL),
                         (rsel, RSEL), (tbase, TBASE), (sltb, SLTB)):
            e.dma_start(out=dst[:], in_=src).then_inc(dA.h, 16)
        e.dma_start(out=wg_f[:], in_=WG).then_inc(dA.h, 16)
    ld_const = dA.bump(16 * 8)

    @P.on("gpsimd")
    def _(e):
        e.iota(iotaE[:], [[0, 8], [1, 8]], base=0, channel_multiplier=0,
               allow_small_or_imprecise_dtypes=True)
        e.iota(iotaP[:], [[0, 1]], base=0, channel_multiplier=8192,
               allow_small_or_imprecise_dtypes=True).then_inc(mG.h, 1)
    g_iota = mG.bump()

    @P.on("vector")
    def _(e, w=ld_const):
        e.wait_ge(dA.h, w)
        e.memset(ones_c[:], 1.0)
        e.memset(ones64[:], 1.0)
        e.memset(eps_c[:], 1e-5)
        e.memset(zero_c[:], 0.0)
        e.memset(ones_r[:], 1.0)
        e.memset(ones_bf[:], 1.0)
        e.memset(negf[:], -1e9)
        e.tensor_copy(wg_hi[:], wg_f[:])
        e.drain()
        e.tensor_tensor(wg_lo[:], wg_f[:], wg_hi[:], ALU.subtract)
        e.drain().then_inc(mV.h, 1)
    v_wg = mV.bump()

    esA1 = ExitStack()
    nc._esA1 = esA1
    xnT = sbx(esA1, "xnT", [128, KT * OWN], BF16)

    esA2 = ExitStack()
    nc._esA2 = esA2
    xnf = sbx(esA2, "xnf", [128, D], F32)
    xnhi = [sbx(esA2, f"xnhi{i}", [128, D], BF16) for i in range(2)]
    xnlo = [sbx(esA2, f"xnlo{i}", [128, D], BF16) for i in range(2)]
    loT = [sbx(esA2, f"loT{i}", [128, D], BF16) for i in range(2)]

    v_tile = [0] * NT
    pT_hi = [0] * NT
    vE_hi = [0] * NT
    pT_lo = [0] * NT
    vE_lo = [0] * NT
    p_logit = [0] * NT
    ag2d = [0] * NT

    for t in range(NT):
        xt = acc[:, t * D:(t + 1) * D]

        @P.on("vector")
        def _(e, t=t, xt=xt):
            if t == 0:
                e.wait_ge(dS.h, ld_x)
            e.reduce_sum(negmean[:], xt, axis=AX.X, negate=True)
            e.drain()
            e.tensor_scalar(negmean[:], negmean[:], 1.0 / D, None, ALU.mult)
            e.scalar_tensor_tensor(xnf[:], xt, 1.0, xt, ALU.mult, ALU.mult,
                                   accum_out=sumsq[:])
            e.drain()
            e.tensor_tensor(m2s[:], negmean[:], negmean[:], ALU.mult)
            e.tensor_scalar(varb[:], sumsq[:], 1.0 / D, None, ALU.mult)
            e.drain()
            e.tensor_tensor(varb[:], varb[:], m2s[:], ALU.subtract)
            e.drain().then_inc(mV.h, 1)
        v_stats = mV.bump()

        @P.on("scalar")
        def _(e, w=v_stats):
            e.wait_ge(mV.h, w)
            e.activation(lnv[:], varb[:], AF.Ln, bias=eps_c[:], scale=1.0)
            e.drain()
            e.activation(rstd[:], lnv[:], AF.Exp, bias=0.0,
                         scale=-0.5)
            e.drain().then_inc(mA.h, 1)
        a_rstd = mA.bump()

        @P.on("vector")
        def _(e, t=t, xt=xt, wa=a_rstd,
              wp=(pT_lo[t - 2] if t >= 2 else 0),
              wd=(ag2d[t - 2] if t >= 2 else 0)):
            e.wait_ge(mA.h, wa)
            if wp:
                e.wait_ge(mP.h, wp)
            if wd:
                e.wait_ge(sAG2.h, wd)
            e.tensor_scalar(xnf[:], xt, negmean[:], rstd[:], ALU.add, ALU.mult)
            e.tensor_copy(xnhi[t % 2][:], xnf[:])
            e.tensor_tensor(xnlo[t % 2][:], xnf[:], xnhi[t % 2][:],
                            ALU.subtract)
            e.drain().then_inc(mV.h, 1)
        v_tile[t] = mV.bump()

        @P.on("gpsimd")
        def _(e, t=t, w=v_tile[t]):
            e.wait_ge(mV.h, w)
            e.dma_start(out=ag2_in[t * 128:(t + 1) * 128, :],
                        in_=xnhi[t % 2][:]).then_inc(sAG2.h, 16)
        ag2d[t] = sAG2.bump(16)

        @P.on("tensor")
        def _(e, t=t, w=v_tile[t], wev=(vE_lo[t - 1] if t >= 1 else 0)):
            e.wait_ge(mV.h, w)
            if t == 0:
                e.wait_ge(dA.h, ld_const)
            if wev:
                e.wait_ge(mV.h, wev)
            ins = None
            for g in range(2):
                for j in range(8):
                    k = g * 8 + j
                    ins = e.transpose(pb[g][:, j * 128:(j + 1) * 128],
                                     xnhi[t % 2][:, k * 128:(k + 1) * 128],
                                     idn_bf[:])
            ins.then_inc(mP.h, 1)
        pT_hi[t] = mP.bump()

        @P.on("vector")
        def _(e, t=t, w=pT_hi[t]):
            e.wait_ge(mP.h, w)
            xv = xnT[:].rearrange("p (k t2) -> p k t2", t2=OWN)
            e.tensor_copy(xv[:, 0:8, t * 128:(t + 1) * 128],
                          pb[0][:].rearrange("p (j t2) -> p j t2", t2=128))
            e.tensor_copy(xv[:, 8:16, t * 128:(t + 1) * 128],
                          pb[1][:].rearrange("p (j t2) -> p j t2", t2=128))
            e.drain().then_inc(mV.h, 1)
        vE_hi[t] = mV.bump()

        @P.on("tensor")
        def _(e, t=t, w=vE_hi[t]):
            e.wait_ge(mV.h, w)
            ins = None
            for g in range(2):
                for j in range(8):
                    k = g * 8 + j
                    ins = e.transpose(pb[g][:, j * 128:(j + 1) * 128],
                                     xnlo[t % 2][:, k * 128:(k + 1) * 128],
                                     idn_bf[:])
            ins.then_inc(mP.h, 1)
        pT_lo[t] = mP.bump()

        @P.on("vector")
        def _(e, t=t, w=pT_lo[t]):
            e.wait_ge(mP.h, w)
            lv = loT[t % 2][:].rearrange("p (k t2) -> p k t2", t2=128)
            e.tensor_copy(lv[:, 0:8, :],
                          pb[0][:].rearrange("p (j t2) -> p j t2", t2=128))
            e.tensor_copy(lv[:, 8:16, :],
                          pb[1][:].rearrange("p (j t2) -> p j t2", t2=128))
            e.drain().then_inc(mV.h, 1)
        vE_lo[t] = mV.bump()

        @P.on("tensor")
        def _(e, t=t, w=vE_lo[t], wwg=v_wg):
            e.wait_ge(mV.h, w)
            if t == 0:
                e.wait_ge(mV.h, wwg)
            out = pf[0][:, t * 8:(t + 1) * 8]
            n = 0
            ins = None
            for k in range(KT):
                for lhs, rhs in (
                        (xnT[:, k * OWN + t * 128: k * OWN + (t + 1) * 128],
                         wg_hi[:, k * 8:(k + 1) * 8]),
                        (xnT[:, k * OWN + t * 128: k * OWN + (t + 1) * 128],
                         wg_lo[:, k * 8:(k + 1) * 8]),
                        (loT[t % 2][:, k * 128:(k + 1) * 128],
                         wg_hi[:, k * 8:(k + 1) * 8])):
                    ins = e.matmul(out, lhs, rhs, start=(n == 0), stop=(n == 47))
                    n += 1
            ins.then_inc(mP.h, 1)
        p_logit[t] = mP.bump()

    if dbg:
        @P.on("gpsimd")
        def _(e, w=v_tile[NT - 1], w2=vE_hi[NT - 1], tgt=dG.n + 32):
            e.wait_ge(mV.h, w)
            e.wait_ge(mV.h, w2)
            e.dma_start(out=DBG["d_acc"], in_=acc[:]).then_inc(dG.h, 16)
            e.dma_start(out=DBG["d_xnT"], in_=xnT[:]).then_inc(dG.h, 16)
            e.wait_ge(dG.h, tgt)
        dG.bump(32)

    @P.on("gpsimd")
    def _(e, w=vE_hi[NT - 1], wd=ag2d[NT - 1], tgt=sAG1.n + 16):
        e.wait_ge(mV.h, w)
        e.dma_start(out=ag1_in.rearrange("(k p) t2 -> p k t2", p=128),
                    in_=xnT[:].rearrange("p (k t2) -> p k t2",
                                         t2=OWN)).then_inc(sAG1.h, 16)
        e.wait_ge(sAG1.h, tgt)
        e.wait_ge(sAG2.h, wd)
        e.collective_compute("AllGather", ALU.bypass, RG, [ag1_in],
                             [ag1_out]).then_inc(cC.h)
        e.collective_compute("AllGather", ALU.bypass, RG, [ag2_in],
                             [ag2_out]).then_inc(cC.h)
    sAG1.bump(16)
    cc_ag1 = cC.bump()
    cc_ag2 = cC.bump()

    @P.on("vector")
    def _(e, w=p_logit[NT - 1], wi=g_iota):
        e.wait_ge(mP.h, w)
        e.wait_ge(mG.h, wi)
        e.tensor_copy(l_own[:], pf[0][:, 0:64])
        e.drain()
        ov = l_own[:].rearrange("p (t e2) -> p t e2", e2=8)
        e.reduce_max(lmax[:], ov, axis=AX.X)
        e.drain()
        for t in range(NT):
            e.tensor_scalar(eexp[:, t * 8:(t + 1) * 8], l_own[:, t * 8:(t + 1) * 8],
                            lmax[:, t:t + 1], None, ALU.subtract)
        for t in range(NT):
            e.tensor_scalar(oh[:, t * 8:(t + 1) * 8], l_own[:, t * 8:(t + 1) * 8],
                            lmax[:, t:t + 1], None, ALU.is_equal)
        ohv = oh[:].rearrange("p (t e2) -> p t e2", e2=8)
        p1v = pr1[:].rearrange("p (t e2) -> p t e2", e2=8)
        p2v = pr2[:].rearrange("p (t e2) -> p t e2", e2=8)
        e.drain()
        e.tensor_copy(pr1[:], oh[:])
        e.drain()
        e.tensor_add(p1v[:, :, 1:8], ohv[:, :, 1:8], ohv[:, :, 0:7])
        e.drain()
        e.tensor_copy(pr2[:], pr1[:])
        e.drain()
        e.tensor_add(p2v[:, :, 2:8], p1v[:, :, 2:8], p1v[:, :, 0:6])
        e.drain()
        e.tensor_copy(pr1[:], pr2[:])
        e.drain()
        e.tensor_add(p1v[:, :, 4:8], p2v[:, :, 4:8], p2v[:, :, 0:4])
        e.drain()
        e.tensor_scalar(ohx[:], pr1[:], 1.0, None, ALU.is_equal)
        e.drain()
        e.tensor_mul(ohx[:], ohx[:], oh[:])
        e.drain()
        e.tensor_mul(mscr[:, 0:64], ohx[:], iotaE[:])
        e.drain()
        e.reduce_sum(eidx[:], mscr[:, 0:64].rearrange("p (t e2) -> p t e2", e2=8),
                     axis=AX.X)
        for k in range(KT):
            e.tensor_scalar(att_if[:, k:k + 1], iotaP[:], tbase[:],
                            float(k * 128 * 8192), ALU.add, ALU.add)
        e.drain()
        e.tensor_copy(att_ix[:], att_if[:])
        e.drain().then_inc(mV.h, 1)
    v_rout1 = mV.bump()

    @P.on("scalar")
    def _(e, w=v_rout1):
        e.wait_ge(mV.h, w)
        ins = None
        for t in range(NT):
            ins = e.activation(eexp[:, t * 8:(t + 1) * 8],
                               eexp[:, t * 8:(t + 1) * 8], AF.Exp, bias=0.0,
                               scale=1.0, accum_out=sume[:, t:t + 1])
        ins.then_inc(mA.h, 1)
    a_rexp = mA.bump()

    @P.on("vector")
    def _(e, w=a_rexp):
        e.wait_ge(mA.h, w)
        e.reciprocal(gate[:], sume[:])
        e.drain().then_inc(mV.h, 1)
    v_gate = mV.bump()

    @P.on("tensor")
    def _(e, w=v_rout1):
        e.wait_ge(mV.h, w)
        ins = None
        for t in range(NT):
            bank, j = (pf[1], t) if t < 4 else (pf[2], t - 4)
            ins = e.transpose(bank[0:8, j * 128:(j + 1) * 128],
                              ohx[:, t * 8:(t + 1) * 8], idn_f[:])
        ins.then_inc(mP.h, 1)
    p_ohxT = mP.bump()

    @P.on("vector")
    def _(e, w=p_ohxT):
        e.wait_ge(mP.h, w)
        e.tensor_copy(ohxT[0:8, 0:512], pf[1][0:8, 0:512])
        e.tensor_copy(ohxT[0:8, 512:1024], pf[2][0:8, 0:512])
        e.drain().then_inc(mV.h, 1)
    v_ohxT = mV.bump()

    @P.on("gpsimd")
    def _(e, w=v_ohxT, tgt=sAG3.n + 16):
        e.wait_ge(mV.h, w)
        e.dma_start(out=ag3_in, in_=ohxT[:]).then_inc(sAG3.h, 16)
        e.wait_ge(sAG3.h, tgt)
        e.collective_compute("AllGather", ALU.bypass, RG, [ag3_in],
                             [ag3_out]).then_inc(cC.h)
    sAG3.bump(16)
    cc_ag3 = cC.bump()

    @P.on("sync")
    def _(e, w=cc_ag3):
        e.wait_ge(cC.h, w)
        e.dma_start(out=ohx_rm[:], in_=ag3_out).then_inc(sRM.h, 16)
    g_ohxrm = sRM.bump(16)

    esA2.close()
    esA3 = ExitStack()
    nc._esA3 = esA3
    h_T = sbx(esA3, "h_T", [128, HT * OWN], BF16, side="right")
    iota_j = sbx(esA3, "iota_j", [128, 1024], F32, side="right")
    cmpb = [sbx(esA3, f"cmpb{i}", [128, 1024], BF16, side="right")
            for i in range(2)]
    esA3a = ExitStack()
    nc._esA3a = esA3a
    wsl = [sbx(esA3a, f"wsl{i}", [128, D], BF16) for i in range(4)]
    hsil = [sbx(esA3a, f"hsil{i}", [128, 512], BF16) for i in range(2)]

    @P.on("gpsimd")
    def _(e):
        e.iota(iota_j[:], [[1, 1024]], base=0, channel_multiplier=0,
               allow_small_or_imprecise_dtypes=True).then_inc(mG.h, 1)
    g_iotaj = mG.bump()

    cmp_done = [0] * 64
    cnt_done = [0] * 64

    def emit_cmp(gt):
        @P.on("vector")
        def _(e, gt=gt, wj=g_iotaj):
            if gt == 0:
                e.wait_ge(mG.h, wj)
            if gt >= 2:
                e.wait_ge(mP.h, cnt_done[gt - 2])
            e.tensor_scalar(cmpb[gt % 2][:], iota_j[:], f_tok[:, gt:gt + 1],
                            None, ALU.is_ge)
            e.drain().then_inc(mV.h, 1)
        cmp_done[gt] = mV.bump()

    def emit_cnt(gt, extra_wait=0):
        @P.on("tensor")
        def _(e, gt=gt, wx=extra_wait):
            e.wait_ge(mV.h, cmp_done[gt])
            if wx:
                e.wait_ge(mV.h, wx)
            e.matmul(pf[0][0:1, 0:512], ones_c[:], cmpb[gt % 2][:, 0:512],
                     start=(gt == 0), stop=(gt == 63))
            e.matmul(pf[1][0:1, 0:512], ones_c[:], cmpb[gt % 2][:, 512:1024],
                     start=(gt == 0), stop=(gt == 63)).then_inc(mP.h, 1)
        cnt_done[gt] = mP.bump()

    p_hn = {}
    v_hev = {}
    sw_ld = {}
    for mH in range(HT):
        @P.on("gpsimd")
        def _(e, mH=mH, wpl=p_logit[NT - 1], wag=ag2d[NT - 1]):
            if mH == 0:
                e.wait_ge(mP.h, wpl)
                e.wait_ge(sAG2.h, wag)
            if mH >= 2:
                e.wait_ge(mP.h, p_hn[(mH - 2, 1)])
            e.dma_start(out=wsl[mH % 2][:],
                        in_=SW1[mH]).then_inc(sW[mH % 2].h, 16)
            e.dma_start(out=wsl[2 + mH % 2][:],
                        in_=SW3[mH]).then_inc(sW[mH % 2].h, 16)
        sw_ld[mH] = sW[mH % 2].bump(32)

        for n in range(2):
            idx = 2 * mH + n

            @P.on("tensor")
            def _(e, mH=mH, n=n, idx=idx, w=sw_ld[mH], wxn=vE_lo[NT - 1]):
                if n == 0:
                    e.wait_ge(sW[mH % 2].h, w)
                if idx == 0:
                    e.wait_ge(mV.h, wxn)
                par = idx % 2
                if idx >= 2:
                    e.wait_ge(mV.h, v_hev[idx - 2])
                b1, b3 = (pf[0], pf[1]) if par == 0 else (pf[4], pf[5])
                for k in range(KT):
                    e.matmul(b1[:], wsl[mH % 2][:, k * 128:(k + 1) * 128],
                             xnT[:, k * OWN + n * 512: k * OWN + n * 512 + 512],
                             start=(k == 0), stop=(k == KT - 1))
                ins = None
                for k in range(KT):
                    ins = e.matmul(b3[:], wsl[2 + mH % 2][:, k * 128:(k + 1) * 128],
                                   xnT[:, k * OWN + n * 512: k * OWN + n * 512 + 512],
                                   start=(k == 0), stop=(k == KT - 1))
                ins.then_inc(mP.h, 1)
            p_hn[(mH, n)] = mP.bump()

            @P.on("scalar")
            def _(e, idx=idx, w=p_hn[(mH, n)]):
                e.wait_ge(mP.h, w)
                e.activation(hsil[idx % 2][:], pf[0 if idx % 2 == 0 else 4][:],
                             AF.Silu, bias=0.0, scale=1.0).then_inc(mA.h, 1)
            a_sil = mA.bump()

            @P.on("vector")
            def _(e, mH=mH, n=n, idx=idx, wa=a_sil):
                e.wait_ge(mA.h, wa)
                e.tensor_tensor(h_T[:, mH * OWN + n * 512: mH * OWN + n * 512 + 512],
                                hsil[idx % 2][:], pf[1 if idx % 2 == 0 else 5][:],
                                ALU.mult)
                e.drain().then_inc(mV.h, 1)
            v_hev[idx] = mV.bump()
    v_hev_last = v_hev[2 * HT - 1]

    @P.on("vector")
    def _(e, w=g_ohxrm):
        e.wait_ge(sRM.h, w)
        e.tensor_tensor_scan(pos_rm[:], ones64[:], ohx_rm[:], 0.0,
                             ALU.mult, ALU.add)
        e.drain()
        e.tensor_copy(tot64[:], pos_rm[:, 1023:1024])
        e.drain().then_inc(mV.h, 1)
    v_scan = mV.bump()

    @P.on("tensor")
    def _(e, w=v_scan):
        e.wait_ge(mV.h, w)
        e.matmul(pf[3][0:64, 0:1], sltb[:], tot64[:], start=True,
                 stop=True).then_inc(mP.h, 1)
    p_off = mP.bump()

    @P.on("vector")
    def _(e, w=p_off):
        e.wait_ge(mP.h, w)
        e.tensor_copy(off64[:], pf[3][0:64, 0:1])
        e.drain()
        e.tensor_scalar(pos_rm[:], pos_rm[:], off64[:], None, ALU.add)
        e.drain().then_inc(mV.h, 1)
    v_posrm = mV.bump()

    @P.on("tensor")
    def _(e, w=v_posrm):
        e.wait_ge(mV.h, w)
        ins = None
        for tl in range(8):
            ins = e.transpose(pf[3][:, tl * 64:(tl + 1) * 64],
                              pos_rm[0:64, tl * 128:(tl + 1) * 128],
                              idn_f[0:64, 0:64])
        ins.then_inc(mP.h, 1)
    p_postok = mP.bump()

    @P.on("vector")
    def _(e, w=p_postok, wg=v_gate):
        e.wait_ge(mP.h, w)
        e.wait_ge(mV.h, wg)
        e.tensor_copy(pos_tok[:], pf[3][:])
        e.drain()
        e.tensor_mul(mscr[:], pos_tok[:], esel[:])
        e.drain()
        e.reduce_sum(f_tok[:],
                     mscr[:].rearrange("p (tl r e2) -> p tl r e2", r=8, e2=8),
                     axis=AX.X)
        e.drain()
        e.tensor_mul(mscr[:], pos_tok[:], rsel[:])
        e.drain()
        e.reduce_sum(pos_own[:],
                     mscr[:].rearrange("p (tl r e2) -> p tl e2 r", r=8, e2=8),
                     axis=AX.X)
        e.drain()
        e.tensor_mul(mscr[:, 0:64], pos_own[:], ohx[:])
        e.drain()
        e.reduce_sum(posv[:], mscr[:, 0:64].rearrange("p (t e2) -> p t e2", e2=8),
                     axis=AX.X)
        e.drain()
        e.tensor_scalar(posv[:], posv[:], -1.0, None, ALU.add)
        e.drain()
        e.tensor_scalar(keep[:], posv[:], float(CAP), None, ALU.is_lt)
        e.drain()
        e.tensor_mul(gk[:], gate[:], keep[:])
        e.scalar_tensor_tensor(comb_f[:], eidx[:], float(CAP), posv[:],
                               ALU.mult, ALU.add)
        e.drain()
        e.tensor_scalar(comb_f[:], comb_f[:], -1.0e6, None, ALU.add)
        e.drain()
        e.tensor_mul(comb_f[:], comb_f[:], keep[:])
        e.drain()
        e.tensor_scalar(comb_f[:], comb_f[:], 1.0e6, None, ALU.add)
        e.drain()
        e.tensor_copy(comb_u[:], comb_f[:])
        e.drain().then_inc(mV.h, 1)
    v_route = mV.bump()

    esA3a.close()
    esA1.close()
    esA3b = ExitStack()
    nc._esA3b = esA3b
    w2sl = [sbx(esA3b, f"w2sl{i}", [128, HT * 512], BF16) for i in range(2)]

    p_w2last = [0] * 4
    v_w2ev = {}
    s2_ld = {}
    gtq = [0]
    for n in range(4):
        @P.on("gpsimd")
        def _(e, n=n):
            if n == 0:
                e.wait_ge(mP.h, p_hn[(HT - 1, 1)])
                e.wait_ge(mV.h, v_hev_last)
            if n >= 2:
                e.wait_ge(mP.h, p_w2last[n - 2])
            e.dma_start(out=w2sl[n % 2][:],
                        in_=SW2[n]).then_inc(sW2[n % 2].h, 16)
        s2_ld[n] = sW2[n % 2].bump(16)

        for j in range(NT):
            gi = n * NT + j

            @P.on("tensor")
            def _(e, n=n, j=j, gi=gi, w=s2_ld[n], whev=v_hev_last):
                if gi == 0:
                    e.wait_ge(mV.h, whev)
                    e.wait_ge(mV.h, v_route)
                if j == 0:
                    e.wait_ge(sW2[n % 2].h, w)
                par = gi % 2
                if gi >= 2:
                    e.wait_ge(mV.h, v_w2ev[gi - 2])
                bank = pf[2 + par]
                ins = None
                for h in range(HT):
                    ins = e.matmul(bank[:],
                                   h_T[:, h * OWN + j * 128: h * OWN + (j + 1) * 128],
                                   w2sl[n % 2][:, h * 512:(h + 1) * 512],
                                   start=(h == 0), stop=(h == HT - 1))
                ins.then_inc(mP.h, 1)
            p_w2 = mP.bump()
            if j == NT - 1:
                p_w2last[n] = p_w2

            @P.on("vector")
            def _(e, n=n, j=j, gi=gi, w=p_w2):
                e.wait_ge(mP.h, w)
                sl = acc[:, j * D + n * 512: j * D + (n + 1) * 512]
                e.scalar_tensor_tensor(sl, pf[2 + gi % 2][:], 1.0, sl,
                                       ALU.mult, ALU.add)
                e.drain().then_inc(mV.h, 1)
            v_w2ev[gi] = mV.bump()

            for _k in range(2):
                if gtq[0] < 64:
                    emit_cmp(gtq[0])
                    emit_cnt(gtq[0], extra_wait=(v_hev_last if gtq[0] == 0 else 0))
                    gtq[0] += 1
    v_w2ev_last = v_w2ev[4 * NT - 1]

    @P.on("vector")
    def _(e, w=cnt_done[63]):
        e.wait_ge(mP.h, w)
        e.tensor_copy(cnt_sb[0:1, 0:512], pf[0][0:1, 0:512])
        e.tensor_copy(cnt_sb[0:1, 512:1024], pf[1][0:1, 0:512])
        e.drain().then_inc(mV.h, 1)
    v_cnt = mV.bump()

    @P.on("tensor")
    def _(e, w=v_cnt):
        e.wait_ge(mV.h, w)
        ins = None
        for j in range(8):
            ins = e.transpose(pf[2][:, j:j + 1],
                              cnt_sb[0:1, j * 128:(j + 1) * 128],
                              idn_f[0:1, 0:1])
        ins.then_inc(mP.h, 1)
    p_tokT = mP.bump()

    @P.on("vector")
    def _(e, w=p_tokT):
        e.wait_ge(mP.h, w)
        e.tensor_copy(tok_f[:], pf[2][:, 0:8])
        e.drain()
        e.tensor_copy(tok_u[:], tok_f[:])
        e.drain().then_inc(mV.h, 1)
    v_tok = mV.bump()

    esA3b.close()
    esA3.close()

    esB1 = ExitStack()
    nc._esB1 = esB1
    qT = sbx(esB1, "qT", [128, T], BF16)
    kT = sbx(esB1, "kT", [128, T], BF16)
    vT = sbx(esB1, "vT", [128, T], BF16)

    esB2 = ExitStack()
    nc._esB2 = esB2
    wqkv_bf = sbx(esB2, "wqkv_bf", [128, KT * 768], BF16)
    rb = [sbx(esB2, f"rb{i}", [128, 8 * 512], BF16) for i in range(3)]
    h1stg = [sbx(esB2, f"h1stg{i}", [128, 3 * 512], BF16) for i in range(2)]

    @P.on("gpsimd")
    def _(e, w=v_tok):
        e.wait_ge(mV.h, w)
        e.dma_start(out=wqkv_bf[:], in_=WQKV).then_inc(sQKV.h, 16)
    g_wqkv = sQKV.bump(16)

    p_qkv = {}
    v_qev = {}
    h1_st = {}
    rb_ld = {}
    for ch in range(16):
        r, hh = ch // 2, ch % 2
        for half in range(2):
            idx = 2 * ch + half

            @P.on("sync")
            def _(e, ch=ch, half=half, r=r, hh=hh, idx=idx, wv=v_tok):
                if idx == 0:
                    e.wait_ge(cC.h, cc_ag1)
                    e.wait_ge(mV.h, wv)
                if idx >= 3:
                    e.wait_ge(mP.h, p_qkv[(idx - 3) // 2])
                src = ag1_out[r * D + half * 1024: r * D + half * 1024 + 1024,
                              hh * 512: hh * 512 + 512]
                e.dma_start(out=rb[idx % 3][:].rearrange("p (k t2) -> p k t2",
                                                         t2=512),
                            in_=src.rearrange("(k p) t2 -> p k t2",
                                              p=128)).then_inc(sRB[idx % 3].h, 16)
            rb_ld[idx] = sRB[idx % 3].bump(16)

        @P.on("tensor")
        def _(e, ch=ch, wb=rb_ld[2 * ch + 1], wq=g_wqkv, wtk=v_tok,
              slot=(2 * ch + 1) % 3):
            if ch == 0:
                e.wait_ge(sQKV.h, wq)
                e.wait_ge(mV.h, wtk)
            e.wait_ge(sRB[(2 * ch) % 3].h, rb_ld[2 * ch])
            e.wait_ge(sRB[slot].h, rb_ld[2 * ch + 1])
            if ch >= 1:
                e.wait_ge(mV.h, v_qev[ch - 1])
            ins = None
            for m in range(6):
                for k in range(KT):
                    ins = e.matmul(
                        pf[m][:],
                        wqkv_bf[:, k * 768 + m * 128: k * 768 + (m + 1) * 128],
                        rb[(2 * ch + k // 8) % 3][:, (k % 8) * 512:(k % 8 + 1) * 512],
                        start=(k == 0), stop=(k == KT - 1))
            ins.then_inc(mP.h, 1)
        p_qkv[ch] = mP.bump()

        @P.on("vector")
        def _(e, ch=ch, w=p_qkv[ch]):
            e.wait_ge(mP.h, w)
            if ch >= 2:
                e.wait_ge(sH1[ch % 2].h, h1_st[ch - 2])
            cs = slice(ch * 512, (ch + 1) * 512)
            e.tensor_scalar(qT[:, cs], pf[0][:], RSQ, None, ALU.mult)
            e.tensor_copy(kT[:, cs], pf[2][:])
            e.tensor_copy(vT[:, cs], pf[4][:])
            e.tensor_scalar(h1stg[ch % 2][:, 0:512], pf[1][:], RSQ, None, ALU.mult)
            e.tensor_copy(h1stg[ch % 2][:, 512:1024], pf[3][:])
            e.tensor_copy(h1stg[ch % 2][:, 1024:1536], pf[5][:])
            e.drain().then_inc(mV.h, 1)
        v_qev[ch] = mV.bump()

        @P.on("gpsimd")
        def _(e, ch=ch, w=v_qev[ch]):
            e.wait_ge(mV.h, w)
            cs = slice(ch * 512, (ch + 1) * 512)
            e.dma_start(out=qs1[:, cs],
                        in_=h1stg[ch % 2][:, 0:512]).then_inc(sH1[ch % 2].h, 16)
            e.dma_start(out=ks1[:, cs],
                        in_=h1stg[ch % 2][:, 512:1024]).then_inc(sH1[ch % 2].h, 16)
            e.dma_start(out=vs1[:, cs],
                        in_=h1stg[ch % 2][:, 1024:1536]).then_inc(sH1[ch % 2].h, 16)
        h1_st[ch] = sH1[ch % 2].bump(48)

    esB2.close()

    esB3 = ExitStack()
    nc._esB3 = esB3
    avb = sbx(esB3, "avb", [128, S], BF16, side="right")
    aebT = [sbx(esB3, f"aebT{i}", [128, 512], BF16, side="right")
            for i in range(3)]
    rr_sb = [sbx(esB3, f"rr_sb{i}", [1, 512], F32) for i in range(2)]
    rrB_sb = sbx(esB3, "rrB_sb", [128, 512], F32, side="right")

    sc_rr = [0]
    sc_free = [(0, 0), (0, 0), (0, 0)]
    pbt_free = [(None, 0), (None, 0)]
    av_free = [0, 0]
    av_stop = [0, 0]
    v_recip = [0, 0]
    bc_done = [0, 0]
    rrb_free = [0]
    slab_free = [0, 0, 0]
    slab_rr = [0]
    g_av = {}
    h1_re = [0, 0, 0]

    def emit_attention(hid, wstart_dS, wstart_mV):
        for g in range(8):
            @P.on("tensor")
            def _(e, g=g, hid=hid, wfree=pbt_free[g % 2],
                  wsd=wstart_dS, wsv=wstart_mV,
                  wh14=h1_st.get(14, 0), wh15=h1_st.get(15, 0)):
                if g == 0:
                    if wsd:
                        e.wait_ge(sRE.h, wsd)
                    if wsv:
                        e.wait_ge(mV.h, wsv)
                    if wh14:
                        e.wait_ge(sH1[0].h, wh14)
                        e.wait_ge(sH1[1].h, wh15)
                if wfree[1]:
                    e.wait_ge(wfree[0], wfree[1])
                ins = None
                for j in range(8):
                    jj = g * 8 + j
                    ins = e.transpose(pb[g % 2][:, j * 128:(j + 1) * 128],
                                      vT[:, jj * 128:(jj + 1) * 128], idn_bf[:])
                ins.then_inc(mP.h, 1)
            p_vt = mP.bump()

            @P.on("vector")
            def _(e, g=g, w=p_vt):
                e.wait_ge(mP.h, w)
                e.tensor_copy(vT[:, g * 1024:(g + 1) * 1024], pb[g % 2][:])
                e.drain().then_inc(mV.h, 1)
            pbt_free[g % 2] = (mV.h, mV.bump())
        v_vtok = mV.n

        for b in range(2):
            tok0 = b * S
            blkid = 2 * hid + b
            pend = [None]

            def emit_scores_exp(I, j):
                bk = sc_rr[0] % 3
                sc_rr[0] += 1
                masks = []
                for ql in range(4):
                    qb = 4 * I + ql
                    if j > qb:
                        masks.append((ql, negf))
                    elif j == qb:
                        masks.append((ql, triu))

                @P.on("tensor")
                def _(e, I=I, j=j, bk=bk, tok0=tok0, masks=masks,
                      wfree=sc_free[bk], wv=v_vtok):
                    if I == 0 and j == 0:
                        e.wait_ge(mV.h, wv)
                    if wfree[0]:
                        e.wait_ge(mA.h, wfree[0])
                    ins = e.matmul(pf[bk][:, 0:512],
                                   kT[:, tok0 + j * 128: tok0 + (j + 1) * 128],
                                   qT[:, tok0 + 512 * I: tok0 + 512 * (I + 1)],
                                   start=True, stop=(not masks))
                    for mi, (ql, mt) in enumerate(masks):
                        ins = e.matmul(pf[bk][:, ql * 128:(ql + 1) * 128],
                                       mt[:], idn_bf[:], start=False,
                                       stop=(mi == len(masks) - 1))
                    ins.then_inc(mP.h, 1)
                p_sc = mP.bump()

                s = slab_rr[0] % 3
                slab_rr[0] += 1

                @P.on("scalar")
                def _(e, s=s, bk=bk, w=p_sc, wsl=slab_free[s]):
                    e.wait_ge(mP.h, w)
                    if wsl:
                        e.wait_ge(mP.h, wsl)
                    e.activation(aebT[s][:], pf[bk][:, 0:512], AF.Exp,
                                 bias=0.0, scale=1.0).then_inc(mA.h, 1)
                a_e = mA.bump()
                sc_free[bk] = (a_e, 0)
                return s, a_e

            def emit_avrs(I, j, s, a_e):
                ntile = 4 * I + 4

                @P.on("tensor")
                def _(e, I=I, j=j, b=b, s=s, w=a_e, ntile=ntile,
                      wav=av_free[I % 2], wrs=rrb_free[0]):
                    e.wait_ge(mA.h, w)
                    if j == 0:
                        if wav:
                            e.wait_ge(mV.h, wav)
                        if wrs:
                            e.wait_ge(mV.h, wrs)
                    e.matmul(pf[3 + I % 2][:, 0:512],
                             vT[:, (b * 32 + j) * 128:(b * 32 + j + 1) * 128],
                             aebT[s][:], start=(j == 0),
                             stop=(j == ntile - 1)).then_inc(mP.h, 1)
                    e.matmul(pf[5][0:1, 0:512], ones_bf[:],
                             aebT[s][:], start=(j == 0),
                             stop=(j == ntile - 1)).then_inc(mP.h, 1)
                v_av = mP.bump()
                v_rs = mP.bump()
                slab_free[s] = v_rs
                if j == ntile - 1:
                    av_stop[I % 2] = v_av
                return v_rs

            def emit_recip(I, w_rs):
                @P.on("vector")
                def _(e, I=I, w=w_rs, wbc=bc_done[I % 2]):
                    e.wait_ge(mP.h, w)
                    if wbc:
                        e.wait_ge(mP.h, wbc)
                    e.tensor_copy(rr_sb[I % 2][0:1, 0:512],
                                  pf[5][0:1, 0:512])
                    e.drain().then_inc(mV.h, 1)
                v_recip[I % 2] = mV.bump()

            def emit_norm_tail(I):
                @P.on("tensor")
                def _(e, I=I, w=v_recip[I % 2]):
                    e.wait_ge(mV.h, w)
                    e.matmul(pf[5][0:128, 0:512], ones_r[:],
                             rr_sb[I % 2][0:1, 0:512],
                             start=True, stop=True).then_inc(mP.h, 1)
                bc = mP.bump()
                bc_done[I % 2] = bc

                @P.on("vector")
                def _(e, I=I, blkid=blkid, w=bc, wav=av_stop[I % 2],
                      wdma=(g_av.get(blkid - 1, 0) if I == 0 else 0)):
                    e.wait_ge(mP.h, w)
                    e.reciprocal(rrB_sb[:], pf[5][0:128, 0:512])
                    e.drain().then_inc(mV.h, 1)
                    e.wait_ge(mP.h, wav)
                    if wdma:
                        e.wait_ge(sAV.h, wdma)
                    ins = e.tensor_tensor(avb[:, 512 * I:512 * (I + 1)],
                                          pf[3 + I % 2][:, 0:512], rrB_sb[:],
                                          ALU.mult)
                    if I == 7:
                        e.drain().then_inc(mV.h, 1)
                    else:
                        ins.then_inc(mV.h, 1)
                rrb_free[0] = mV.bump()
                av_free[I % 2] = mV.bump()

            for I in range(8):
                ntile = 4 * I + 4
                prev = None
                w_rs = 0
                for j in range(ntile):
                    se = emit_scores_exp(I, j)
                    if j == 1 and pend[0] is not None:
                        emit_norm_tail(pend[0])
                        pend[0] = None
                    if prev is not None:
                        w_rs = emit_avrs(I, j - 1, *prev)
                    prev = se
                w_rs = emit_avrs(I, ntile - 1, *prev)
                emit_recip(I, w_rs)
                pend[0] = I
            emit_norm_tail(pend[0])
            v_bh = mV.n

            @P.on("gpsimd")
            def _(e, hid=hid, b=b, w=v_bh):
                e.wait_ge(mV.h, w)
                e.dma_start(out=ag5_in[hid * 128:(hid + 1) * 128,
                                       b * S:(b + 1) * S],
                            in_=avb[:]).then_inc(sAV.h, 16)
            g_av[blkid] = sAV.bump(16)
        return mP.n

    p_head0 = emit_attention(0, 0, v_qev[15])

    for iq, (dst, src) in enumerate(((qT, qs1), (kT, ks1), (vT, vs1))):
        @P.on("sync")
        def _(e, dst=dst, src=src, iq=iq, w=p_head0, wh14=h1_st[14],
              wh15=h1_st[15]):
            if iq == 0:
                e.wait_ge(mP.h, w)
                e.wait_ge(sH1[0].h, wh14)
                e.wait_ge(sH1[1].h, wh15)
            e.dma_start(out=dst[:], in_=src).then_inc(sRE.h, 16)
        h1_re[iq] = sRE.bump(16)

    p_head1 = emit_attention(1, h1_re[2], 0)
    v_att_end = mV.n

    @P.on("gpsimd")
    def _(e, w=g_av[3]):
        e.wait_ge(sAV.h, w)
        e.collective_compute("AllGather", ALU.bypass, RG, [ag5_in],
                             [ag5_out]).then_inc(cC.h)
    cc_ag5 = cC.bump()

    if dbg:
        @P.on("gpsimd")
        def _(e, w=v_qev[15]):
            e.wait_ge(mV.h, w)
            e.dma_start(out=DBG["d_qT"], in_=qT[:]).then_inc(dG.h, 16)
            e.dma_start(out=DBG["d_kT"], in_=kT[:]).then_inc(dG.h, 16)
        dG.bump(32)

        @P.on("gpsimd")
        def _(e, w=g_av[0]):
            e.wait_ge(dG.h, w)
            e.dma_start(out=DBG["d_vtok"], in_=vT[:]).then_inc(dG.h, 16)
            e.dma_start(out=DBG["d_av"], in_=avb[:]).then_inc(dG.h, 16)
        dG.bump(32)

    esB3.close()
    esB1.close()

    esC1 = ExitStack()
    nc._esC1 = esC1
    h_T2 = sbx(esC1, "h_T2", [128, HT * CAP], BF16, side="right")
    esC1x = ExitStack()
    nc._esC1x = esC1x
    xd_T = sbx(esC1x, "xd_T", [128, KT * CAP], BF16)

    esC2 = ExitStack()
    nc._esC2 = esC2
    xdt = [sbx(esC2, f"xdt{i}", [128, D], BF16) for i in range(2)]

    g_xd = {}
    p_xdt = {}
    v_xdev = {}
    for j in range(NT):
        @P.on("gpsimd")
        def _(e, j=j, wtok=v_tok, wag2=cc_ag2, whd=p_head1):
            if j == 0:
                e.wait_ge(mV.h, wtok)
                e.wait_ge(cC.h, wag2)
                e.wait_ge(mP.h, whd)
            if j >= 2:
                e.wait_ge(mP.h, p_xdt[j - 2])
            e.memset(xdt[j % 2][:], 0.0)
            e.indirect_dma_start(
                xdt[j % 2][:], None, ag2_out,
                IndirectOffsetOnAxis(ap=tok_u[:, j:j + 1], axis=0),
                bounds_check=T - 1, oob_is_err=False).then_inc(sXD[j % 2].h, 16)
        g_xd[j] = sXD[j % 2].bump(16)

        if dbg and j == 0:
            @P.on("gpsimd")
            def _(e, w=g_xd[0], tgt=dG.n + 32):
                e.wait_ge(dG.h, w)
                e.dma_start(out=DBG["d_xd0"], in_=xdt[0][:]).then_inc(dG.h, 16)
                e.dma_start(out=DBG["d_ag2"], in_=ag2_out[0:1024, :]).then_inc(dG.h, 16)
                e.wait_ge(dG.h, tgt)
            dG.bump(32)

        for g in range(2):
            @P.on("tensor")
            def _(e, j=j, g=g, w=g_xd[j], wfree=pbt_free[(2 * j + g) % 2]):
                if g == 0:
                    e.wait_ge(sXD[j % 2].h, w)
                if wfree[1]:
                    e.wait_ge(wfree[0], wfree[1])
                ins = None
                for kk in range(8):
                    k = g * 8 + kk
                    ins = e.transpose(
                        pb[(2 * j + g) % 2][:, kk * 128:(kk + 1) * 128],
                        xdt[j % 2][:, k * 128:(k + 1) * 128], idn_bf[:])
                ins.then_inc(mP.h, 1)
            pw = mP.bump()

            @P.on("vector")
            def _(e, j=j, g=g, w=pw):
                e.wait_ge(mP.h, w)
                xv = xd_T[:].rearrange("p (k t2) -> p k t2", t2=CAP)
                e.tensor_copy(xv[:, g * 8:(g + 1) * 8, j * 128:(j + 1) * 128],
                              pb[(2 * j + g) % 2][:]
                              .rearrange("p (kk t2) -> p kk t2", t2=128))
                e.drain().then_inc(mV.h, 1)
            pbt_free[(2 * j + g) % 2] = (mV.h, mV.bump())
        p_xdt[j] = mP.n
    v_xdT = mV.n

    esC2.close()

    esC3 = ExitStack()
    nc._esC3 = esC3
    wslC = [sbx(esC3, f"wslC{i}", [128, D], BF16) for i in range(4)]
    hsilC = [sbx(esC3, f"hsilC{i}", [128, 512], BF16) for i in range(2)]

    p_hn2 = {}
    v_hev2 = {}
    for mH in range(HT):
        @P.on("gpsimd")
        def _(e, mH=mH, whd=p_head1):
            if mH == 0:
                e.wait_ge(mP.h, whd)
            if mH <= 1:
                e.wait_ge(mP.h, p_xdt[NT - 1])
            if mH >= 2:
                e.wait_ge(mP.h, p_hn2[(mH - 2, 1)])
            e.dma_start(out=wslC[mH % 2][:],
                        in_=W1[mH]).then_inc(sW13[mH % 2].h, 16)
            e.dma_start(out=wslC[2 + mH % 2][:],
                        in_=W3[mH]).then_inc(sW13[mH % 2].h, 16)
        w13_ld = sW13[mH % 2].bump(32)

        for n in range(2):
            idx = 2 * mH + n

            @P.on("tensor")
            def _(e, mH=mH, n=n, idx=idx, w=w13_ld, wxd=v_xdT, wat=v_att_end):
                if n == 0:
                    e.wait_ge(sW13[mH % 2].h, w)
                if idx == 0:
                    e.wait_ge(mV.h, wxd)
                if idx <= 1:
                    e.wait_ge(mV.h, wat)
                par = idx % 2
                if idx >= 2:
                    e.wait_ge(mV.h, v_hev2[idx - 2])
                b1, b3 = (pf[0], pf[1]) if par == 0 else (pf[4], pf[5])
                for k in range(KT):
                    e.matmul(b1[:], wslC[mH % 2][:, k * 128:(k + 1) * 128],
                             xd_T[:, k * CAP + n * 512: k * CAP + n * 512 + 512],
                             start=(k == 0), stop=(k == KT - 1))
                ins = None
                for k in range(KT):
                    ins = e.matmul(b3[:], wslC[2 + mH % 2][:, k * 128:(k + 1) * 128],
                                   xd_T[:, k * CAP + n * 512: k * CAP + n * 512 + 512],
                                   start=(k == 0), stop=(k == KT - 1))
                ins.then_inc(mP.h, 1)
            p_hn2[(mH, n)] = mP.bump()

            @P.on("scalar")
            def _(e, idx=idx, w=p_hn2[(mH, n)]):
                e.wait_ge(mP.h, w)
                e.activation(hsilC[idx % 2][:], pf[0 if idx % 2 == 0 else 4][:],
                             AF.Silu, bias=0.0, scale=1.0).then_inc(mA.h, 1)
            a_sil2 = mA.bump()

            @P.on("vector")
            def _(e, mH=mH, n=n, idx=idx, wa=a_sil2):
                e.wait_ge(mA.h, wa)
                e.tensor_tensor(
                    h_T2[:, mH * CAP + n * 512: mH * CAP + n * 512 + 512],
                    hsilC[idx % 2][:], pf[1 if idx % 2 == 0 else 5][:],
                    ALU.mult)
                e.drain().then_inc(mV.h, 1)
            v_hev2[idx] = mV.bump()
    v_hev2_last = v_hev2[2 * HT - 1]
    a_sil2_last = mA.n

    if dbg:
        @P.on("gpsimd")
        def _(e, w=v_xdT, w2=v_hev2_last, tgt=dG.n + 16):
            e.wait_ge(mV.h, w)
            e.dma_start(out=DBG["d_xdT"], in_=xd_T[:]).then_inc(dG.h, 16)
            e.wait_ge(dG.h, tgt)
            e.wait_ge(mV.h, w2)
            e.dma_start(out=DBG["d_hT"], in_=h_T2[:]).then_inc(dG.h, 16)
        dG.bump(32)
    esC3.close()
    esC1x.close()

    esD0 = ExitStack()
    nc._esD0 = esD0
    woutb0 = sbx(esD0, "woutb0", [128, KT * 512], BF16)

    esC4 = ExitStack()
    nc._esC4 = esC4
    w2slC = [sbx(esC4, f"w2slC{i}", [128, HT * 512], BF16) for i in range(2)]
    oe_st = [sbx(esC4, f"oest{i}", [128, 512], BF16) for i in range(2)]

    p_w2last2 = [0] * 4
    v_oeev = {}
    g_oest = {}
    wo_pre0 = 0
    for n in range(4):
        @P.on("gpsimd")
        def _(e, n=n, whe=v_hev2_last):
            if n == 0:
                e.wait_ge(mP.h, p_hn2[(HT - 1, 1)])
                e.wait_ge(mV.h, whe)
            if n >= 2:
                e.wait_ge(mP.h, p_w2last2[n - 2])
            e.dma_start(out=w2slC[n % 2][:],
                        in_=W2[n]).then_inc(sW2b[n % 2].h, 16)
        w2_ld = sW2b[n % 2].bump(16)

        if n == 0:
            @P.on("gpsimd")
            def _(e):
                e.dma_start(out=woutb0[:],
                            in_=WOUT[0]).then_inc(sWO[0].h, 16)
            wo_pre0 = sWO[0].bump(16)

        for j in range(NT):
            gi = n * NT + j

            @P.on("tensor")
            def _(e, n=n, j=j, gi=gi, w=w2_ld, whev=v_hev2_last):
                if gi == 0:
                    e.wait_ge(mV.h, whev)
                if j == 0:
                    e.wait_ge(sW2b[n % 2].h, w)
                par = gi % 2
                if gi >= 2:
                    e.wait_ge(mV.h, v_oeev[gi - 2])
                bank = pf[2 + par]
                ins = None
                for h in range(HT):
                    ins = e.matmul(bank[:],
                                   h_T2[:, h * CAP + j * 128: h * CAP + (j + 1) * 128],
                                   w2slC[n % 2][:, h * 512:(h + 1) * 512],
                                   start=(h == 0), stop=(h == HT - 1))
                ins.then_inc(mP.h, 1)
            p_oe = mP.bump()
            if j == NT - 1:
                p_w2last2[n] = p_oe

            @P.on("vector")
            def _(e, n=n, j=j, gi=gi, w=p_oe):
                e.wait_ge(mP.h, w)
                if gi >= 2:
                    e.wait_ge(sOE[gi % 2].h, g_oest[gi - 2])
                e.tensor_copy(oe_st[gi % 2][:], pf[2 + gi % 2][:])
                e.drain().then_inc(mV.h, 1)
            v_oe = mV.bump()
            v_oeev[gi] = v_oe

            @P.on("sync")
            def _(e, n=n, j=j, gi=gi, w=v_oe):
                e.wait_ge(mV.h, w)
                e.dma_start(out=ag4_in[j * 128:(j + 1) * 128,
                                       n * 512:(n + 1) * 512],
                            in_=oe_st[gi % 2][:]).then_inc(sOE[gi % 2].h, 16)
            g_oest[gi] = sOE[gi % 2].bump(16)

    @P.on("gpsimd")
    def _(e, w14=g_oest[4 * NT - 2], w15=g_oest[4 * NT - 1]):
        e.wait_ge(sOE[0].h, w14)
        e.wait_ge(sOE[1].h, w15)
        e.collective_compute("AllGather", ALU.bypass, RG, [ag4_in],
                             [ag4_out]).then_inc(cC.h)
    cc_ag4 = cC.bump()

    esC4.close()
    esC1.close()

    esD = ExitStack()
    nc._esD = esD
    att_own = sbx(esD, "att_own", [128, KT * OWN], BF16)
    woutb = [woutb0, sbx(esD, "woutb1", [128, KT * 512], BF16)]
    gtl = [sbx(esD, f"gtl{i}", [128, D], BF16) for i in range(2)]

    g_att = {}
    for k in range(KT):
        @P.on("gpsimd")
        def _(e, k=k, w=cc_ag5, wr=v_route, wc=p_w2last2[3]):
            if k == 0:
                e.wait_ge(cC.h, w)
                e.wait_ge(mV.h, wr)
                e.wait_ge(mP.h, wc)
            e.indirect_dma_start(
                att_own[:, k * OWN:(k + 1) * OWN], None, ag5_out,
                IndirectOffsetOnAxis(ap=att_ix[:, k:k + 1], axis=1),
                bounds_check=NCORES * 256 * T - 1,
                oob_is_err=False).then_inc(sATT.h, 16)
    g_att = sATT.bump(16 * KT)

    p_wolast = [0] * 4
    wo_lds = [wo_pre0, 0, 0, 0]
    for n in range(1, 4):
        @P.on("gpsimd")
        def _(e, n=n):
            if n >= 2:
                e.wait_ge(mP.h, p_wolast[n - 2])
            e.dma_start(out=woutb[n % 2][:],
                        in_=WOUT[n]).then_inc(sWO[n % 2].h, 16)
        wo_lds[n] = sWO[n % 2].bump(16)

    v_comb = {}
    g_cb = {}
    for j in range(NT):
        @P.on("gpsimd")
        def _(e, j=j, w=cc_ag4):
            if j == 0:
                e.wait_ge(cC.h, w)
            if j < 2:
                e.memset(gtl[j % 2][:], 0.0)
            if j >= 2:
                e.wait_ge(mV.h, v_comb[j - 2])
            e.indirect_dma_start(
                gtl[j % 2][:], None, ag4_out,
                IndirectOffsetOnAxis(ap=comb_u[:, j:j + 1], axis=0),
                bounds_check=E * CAP - 1, oob_is_err=False).then_inc(sCB[j % 2].h, 16)
        g_cb[j] = sCB[j % 2].bump(16)

    v_woev = {}
    for n in range(4):
        wo_ld = wo_lds[n]
        for j in range(NT):
            gi = n * NT + j

            @P.on("tensor")
            def _(e, n=n, j=j, gi=gi, w=wo_ld, wgat=g_att,
                  wsl2=a_sil2_last):
                if gi == 0:
                    e.wait_ge(sATT.h, wgat)
                    e.wait_ge(mA.h, wsl2)
                if j == 0:
                    e.wait_ge(sWO[n % 2].h, w)
                par = gi % 2
                if gi >= 2:
                    e.wait_ge(mV.h, v_woev[gi - 2])
                bank = pf[par]
                ins = None
                for k in range(KT):
                    ins = e.matmul(bank[:],
                                   att_own[:, k * OWN + j * 128:
                                           k * OWN + (j + 1) * 128],
                                   woutb[n % 2][:, k * 512:(k + 1) * 512],
                                   start=(k == 0), stop=(k == KT - 1))
                ins.then_inc(mP.h, 1)
            p_wo = mP.bump()
            if j == NT - 1:
                p_wolast[n] = p_wo

            @P.on("vector")
            def _(e, n=n, j=j, gi=gi, w=p_wo):
                e.wait_ge(mP.h, w)
                sl = acc[:, j * D + n * 512: j * D + (n + 1) * 512]
                e.scalar_tensor_tensor(sl, pf[gi % 2][:], 1.0, sl,
                                       ALU.mult, ALU.add)
                e.drain().then_inc(mV.h, 1)
            v_woev[gi] = mV.bump()

            if n == 3:
                @P.on("vector")
                def _(e, j=j, w=g_cb[j]):
                    e.wait_ge(sCB[j % 2].h, w)
                    sl = acc[:, j * D:(j + 1) * D]
                    e.scalar_tensor_tensor(sl, gtl[j % 2][:], gk[:, j:j + 1],
                                           sl, ALU.mult, ALU.add)
                    e.drain().then_inc(mV.h, 1)
                v_comb[j] = mV.bump()
    v_wo_last = v_woev[4 * NT - 1]

    st_base = dG.n
    for j in range(NT):
        @P.on("sync" if j % 2 == 0 else "scalar")
        def _(e, j=j, wv=0):
            e.wait_ge(mV.h, v_comb[j])
            e.dma_start(out=Y[j * 128:(j + 1) * 128, :],
                        in_=acc[:, j * D:(j + 1) * D]).then_inc(dG.h, 16)
        dG.bump(16)
    st_done = dG.n

    @P.on("gpsimd")
    def _(e, tgt=st_done):
        e.wait_ge(dG.h, tgt)

    if dbg:
        @P.on("gpsimd")
        def _(e, tgt=dG.n + 16 * 9):
            for nm, src in (("d_lown", l_own), ("d_posrm", pos_rm),
                            ("d_tok", tok_u), ("d_comb", comb_u), ("d_gk", gk),
                            ("d_eidx", eidx), ("d_posv", posv),
                            ("d_eexp", eexp), ("d_ftok", f_tok)):
                e.dma_start(out=DBG[nm], in_=src[:]).then_inc(dG.h, 16)
            e.wait_ge(dG.h, tgt)
        dG.bump(16 * 9)

    esD.close()

    with nc.Block() as block:
        @block.sync
        def _(eng):
            for fn in P.items["sync"]:
                fn(eng)

        @block.tensor
        def _(eng):
            for fn in P.items["tensor"]:
                fn(eng)

        @block.vector
        def _(eng):
            for fn in P.items["vector"]:
                fn(eng)

        @block.scalar
        def _(eng):
            for fn in P.items["scalar"]:
                fn(eng)

        @block.gpsimd
        def _(eng):
            for fn in P.items["gpsimd"]:
                fn(eng)

    return nc


def _host_inputs(L0_attn_x, L0_attn_w_qkv, L0_attn_w_out, L0_ffn_Wg, L0_ffn_W1,
                 L0_ffn_W3, L0_ffn_W2, L0_ffn_sw1, L0_ffn_sw3, L0_ffn_sw2):
    f32 = np.float32
    x = np.ascontiguousarray(np.asarray(L0_attn_x, f32).reshape(T, D))
    wqkv = np.asarray(L0_attn_w_qkv, f32)
    idn = np.eye(128, dtype=f32)
    trim = np.triu(np.full((128, 128), -1e9, dtype=f32), 1)
    sltb = np.zeros((64, 64), dtype=f32)
    for rp in range(8):
        for ep in range(8):
            for r2 in range(rp):
                sltb[r2 * 8 + ep, rp * 8 + ep] = 1.0

    def kslice(w, c0, w_=128):
        return np.ascontiguousarray(
            w[:, c0:c0 + w_].reshape(KT, 128, w_).transpose(1, 0, 2)
            .reshape(128, KT * w_))

    def lhs_tiles(w):
        return np.ascontiguousarray(
            w.reshape(KT, 128, HT, 128).transpose(2, 1, 0, 3)
            .reshape(HT, 128, KT * 128))

    def rhs_chunks(w, kt):
        return np.ascontiguousarray(
            w.reshape(kt, 128, 4, 512).transpose(2, 1, 0, 3)
            .reshape(4, 128, kt * 512))

    wg_pre = np.ascontiguousarray(
        np.asarray(L0_ffn_Wg, f32).reshape(KT, 128, E).transpose(1, 0, 2)
        .reshape(128, KT * E))
    sw1p = lhs_tiles(np.asarray(L0_ffn_sw1, f32))
    sw3p = lhs_tiles(np.asarray(L0_ffn_sw3, f32))
    sw2p = rhs_chunks(np.asarray(L0_ffn_sw2, f32), HT)
    woutp = rhs_chunks(np.asarray(L0_attn_w_out, f32), KT)
    maps = []
    for c in range(NCORES):
        esel = np.zeros((128, 512), dtype=f32)
        rsel = np.zeros((128, 512), dtype=f32)
        for tl in range(8):
            for r in range(8):
                esel[:, tl * 64 + r * 8 + c] = 1.0
            for ee in range(8):
                rsel[:, tl * 64 + c * 8 + ee] = 1.0
        wq = np.concatenate([kslice(wqkv, 256 * c, 256),
                             kslice(wqkv, D + 256 * c, 256),
                             kslice(wqkv, 2 * D + 256 * c, 256)], axis=1)
        qb = wq[:, 0:KT * 256].reshape(128, KT, 256)
        kb = wq[:, KT * 256:2 * KT * 256].reshape(128, KT, 256)
        vb = wq[:, 2 * KT * 256:].reshape(128, KT, 256)
        wqp = np.ascontiguousarray(
            np.concatenate([qb, kb, vb], axis=2).reshape(128, KT * 768))
        maps.append({
            "x_own": np.ascontiguousarray(x[c * OWN:(c + 1) * OWN]),
            "wqkv_sl": wqp,
            "w_out": woutp,
            "wg": wg_pre,
            "w1_e": lhs_tiles(np.asarray(L0_ffn_W1[c], f32)),
            "w3_e": lhs_tiles(np.asarray(L0_ffn_W3[c], f32)),
            "w2_e": rhs_chunks(np.asarray(L0_ffn_W2[c], f32), HT),
            "sw1": sw1p,
            "sw3": sw3p,
            "sw2": sw2p,
            "idn_bf": np.ascontiguousarray(idn.astype(BF)),
            "idn_f32": idn,
            "tri": np.ascontiguousarray(trim.T.astype(BF)),
            "triu": np.ascontiguousarray(trim.astype(BF)),
            "esel_rep": esel,
            "rsel_rep": rsel,
            "tbase": np.full((128, 1), 1024.0 * c, dtype=f32),
            "sltb": sltb,
        })
    return maps


def kernel(**inputs):
    maps = _host_inputs(**inputs)
    nc = build_nc()
    res = run_bass_kernel_spmd(nc, maps, core_ids=list(range(NCORES)))
    out = np.concatenate([np.asarray(r["y"]) for r in res.results], axis=0)
    return out.reshape(2, S, D).astype(np.float32)
